# version 07
# baseline (speedup 1.0000x reference)
import contextlib
import numpy as np
import ml_dtypes
import concourse.bass as bass
import concourse.mybir as mybir
from concourse.bass_utils import run_bass_kernel_spmd

F32 = mybir.dt.float32
BF16 = mybir.dt.bfloat16
I32 = mybir.dt.int32
AF = mybir.ActivationFunctionType
ALU = mybir.AluOpType
AX = mybir.AxisListType

D = 2048
KC = 16
TPC = 1024
NSLAB = 2
SL = 512
H_FFN = 8192
NH = 16
EH = 128
EPS = 1e-5


class Buf:
    __slots__ = ("name", "w", "r")
    def __init__(self, name):
        self.name = name
        self.w = None
        self.r = []

class DmaGroup:
    __slots__ = ("sem",)
    def __init__(self, sem):
        self.sem = sem

class Queue:
    def __init__(self, name, get_eng, csem):
        self.name = name
        self.get_eng = get_eng
        self.csem = csem
        self.waited = {}
        self.ops = []

class Ctx:
    def __init__(self, nc):
        self.nc = nc
        self.semh = {}
        self.semcount = {}
        self.queues = {}
        self.bufs = []
        self._es = None
        self._sw = None

    def _sem(self, name):
        if name not in self.semh:
            self.semh[name] = self._es.enter_context(self.nc.semaphore(name))
            self.semcount[name] = 0
        return name

    def add_queue(self, name, eng_attr, csem=None):
        if csem:
            self._sem(csem)
        self.queues[name] = Queue(name, eng_attr, csem)

    def dma_group(self, name):
        return DmaGroup(self._sem(name))

    def buf(self, name):
        b = Buf(name)
        self.bufs.append(b)
        return b

    def _resolve(self, tok):
        if tok is None:
            return None
        if tok[0] == "D":
            _, sem, _v = tok
            return (sem, self.semcount[sem])
        return (tok[1], tok[2])

    def _collect(self, reads, writes, extra):
        waits = {}
        def addw(tok):
            rv = self._resolve(tok)
            if rv is None:
                return
            s, v = rv
            if waits.get(s, 0) < v:
                waits[s] = v
        for b in reads:
            addw(b.w)
        for b in writes:
            addw(b.w)
            for t in b.r:
                addw(t)
        for t in extra:
            addw(t)
        return waits

    def _emit(self, qname, fn, waits, sem, amt, reads, writes):
        q = self.queues[qname]
        final = []
        for s, v in waits.items():
            if qname == "pe" and s == "pe":
                continue
            if q.waited.get(s, -1) >= v:
                continue
            final.append((s, v))
            q.waited[s] = v
        if sem is not None:
            self.semcount[sem] += amt
            tok_kind = "D" if amt >= 16 else "C"
            tok = (tok_kind, sem, self.semcount[sem])
            q.ops.append(("op", final, fn, (sem, amt)))
        else:
            tok = None
            q.ops.append(("op", final, fn, None))
        for b in reads:
            if tok is not None:
                b.r.append(tok)
        for b in writes:
            b.w = tok
            b.r = []
        return tok

    def do(self, qname, fn, reads=(), writes=(), extra=()):
        q = self.queues[qname]
        waits = self._collect(reads, writes, extra)
        return self._emit(qname, fn, waits, q.csem, 1, reads, writes)

    def dma(self, qname, fn, grp, reads=(), writes=(), extra=()):
        waits = self._collect(reads, writes, extra)
        return self._emit(qname, fn, waits, grp.sem, 16, reads, writes)

    def cc(self, fn, reads=(), writes=(), extra=()):
        self._sem("cc")
        waits = self._collect(reads, writes, extra)
        return self._emit("gp", fn, waits, "cc", 1, reads, writes)

    def raw(self, qname, fn):
        self.queues[qname].ops.append(("raw", fn))

    def fence(self, exclude=()):
        snap = {s: v for s, v in self.semcount.items() if v > 0 and s not in exclude}
        for q in self.queues.values():
            final = []
            for s, v in snap.items():
                if q.waited.get(s, -1) < v:
                    final.append((s, v))
                    q.waited[s] = v
            if final:
                q.ops.append(("op", final, None, None))
        for b in self.bufs:
            b.w = None
            b.r = []

    def begin_switch(self, qnames):
        self._sw = {
            "queues": list(qnames),
            "count0": dict(self.semcount),
            "waited0": {qn: dict(self.queues[qn].waited) for qn in qnames},
            "ops0": {qn: self.queues[qn].ops for qn in qnames},
            "buf0": [(b, b.w, list(b.r)) for b in self.bufs],
            "bodies": {qn: [] for qn in qnames},
            "counts": [],
        }
        for qn in qnames:
            self.queues[qn].ops = []

    def begin_body(self):
        sw = self._sw
        self.semcount = dict(sw["count0"])
        for qn in sw["queues"]:
            q = self.queues[qn]
            q.count = None
            q.waited = dict(sw["waited0"][qn])
            q.ops = []
        for b, w, r in sw["buf0"]:
            b.w = w
            b.r = list(r)

    def end_body(self):
        sw = self._sw
        for qn in sw["queues"]:
            sw["bodies"][qn].append(self.queues[qn].ops)
        sw["counts"].append(dict(self.semcount))

    def end_switch(self, rank_ap):
        sw = self._sw
        maxima = dict(sw["count0"])
        for counts in sw["counts"]:
            for s, v in counts.items():
                maxima[s] = max(maxima.get(s, 0), v)
        for qn in sw["queues"]:
            q = self.queues[qn]
            bodies = []
            for i in range(8):
                ops = sw["bodies"][qn][i]
                incs = {}
                for item in ops:
                    if item[0] == "op" and item[3] is not None:
                        s, a = item[3]
                        incs[s] = incs.get(s, 0) + a
                pads = []
                for s, a in incs.items():
                    target = maxima[s] - sw["count0"].get(s, 0)
                    if target > a:
                        pads.append((s, target - a))
                body = list(ops)
                if pads:
                    body.append(("pad", pads))
                bodies.append((body, incs))
            for i, (body, incs) in enumerate(bodies):
                all_incs = set()
                for _b2, inc2 in bodies:
                    all_incs.update(inc2)
                missing = []
                for s in all_incs:
                    if s not in incs:
                        target = maxima[s] - sw["count0"].get(s, 0)
                        if target > 0:
                            missing.append((s, target))
                if missing:
                    body.append(("pad", missing))
            q.ops = sw["ops0"][qn]
            q.ops.append(("switch", rank_ap, [b for b, _ in bodies]))
            q.waited = {}
        self.semcount = maxima
        self._sw = None
        for b in self.bufs:
            b.w = None
            b.r = []
        self.fence()

    def finalize(self, block):
        semh = self.semh
        def run_ops(eng, ops):
            for item in ops:
                if item[0] == "op":
                    _, waits, fn, inc = item
                    for s, v in waits:
                        eng.wait_ge(semh[s], v)
                    if fn is not None:
                        instr = fn(eng)
                        if inc is not None:
                            instr.then_inc(semh[inc[0]], inc[1])
                elif item[0] == "raw":
                    item[1](eng)
                elif item[0] == "pad":
                    for s, amt in item[1]:
                        eng.sem_inc(semh[s], amt)
                elif item[0] == "switch":
                    _, rank_ap, bodies = item
                    with eng.register("rk") as rk:
                        eng.reg_load(rk, rank_ap)
                        idx = eng.snap(rk)
                        for i in self.nc.Switch(engines=[eng], index=[idx], n=8):
                            run_ops(eng, bodies[i])
        for q in self.queues.values():
            ops = q.ops
            getattr(block, q.get_eng)(lambda eng, ops=ops: run_ops(eng, ops))



def _hadamard(n):
    h = np.array([[1.0]], dtype=np.float64)
    while h.shape[0] < n:
        h = np.block([[h, h], [h, -h]])
    return h

def prep_inputs(I):
    bf = ml_dtypes.bfloat16
    f32 = np.float32
    x = np.asarray(I["L0_attn_x"], f32)

    shared = {}
    C = np.ascontiguousarray
    shared["w_in"] = C(np.asarray(I["L0_attn_W_in"], f32))
    shared["w_out_aug"] = C(np.concatenate([np.asarray(I["L0_attn_W_out"], f32),
                                            np.asarray(I["L0_attn_b_out"], f32)[None, :]], 0))
    shared["w_down0"] = C(np.asarray(I["L0_attn_W_down"], f32))
    shared["w_up_aug"] = C(np.concatenate([np.asarray(I["L0_attn_W_up"], f32),
                                           np.asarray(I["L0_attn_b_up"], f32)[None, :]], 0))
    w1t = np.asarray(I["L0_ffn_W1"], f32).T
    w3t = np.asarray(I["L0_ffn_W3"], f32).T
    w2t = np.asarray(I["L0_ffn_W2"], f32).T
    wqkvt = np.asarray(I["L1_attn_W_qkv"], f32).T
    woutt = np.asarray(I["L1_attn_W_out"], f32).T
    wg = np.asarray(I["L1_ffn_w_gate"], f32)
    wu = np.asarray(I["L1_ffn_w_up"], f32)
    wd = np.asarray(I["L1_ffn_w_down"], f32)
    shared["w1tp"] = C(w1t.reshape(16, 128, 64, 128).transpose(1, 2, 0, 3).reshape(128, 131072))
    shared["w3tp"] = C(w3t.reshape(16, 128, 64, 128).transpose(1, 2, 0, 3).reshape(128, 131072))
    shared["w2tp"] = C(w2t.reshape(64, 128, 4, 512).transpose(1, 2, 0, 3).reshape(128, 131072))
    shared["wqkvtp"] = C(wqkvt.reshape(16, 128, 48, 128).transpose(1, 2, 0, 3).reshape(128, 98304))
    shared["wouttp"] = C(woutt.reshape(16, 128, 8, 256).transpose(1, 2, 0, 3).reshape(128, 32768))
    shared["wgp"] = C(wg.reshape(16, 128, 32, 256).transpose(1, 2, 0, 3).reshape(128, 131072))
    shared["wup"] = C(wu.reshape(16, 128, 32, 256).transpose(1, 2, 0, 3).reshape(128, 131072))
    shared["wdp"] = C(wd.reshape(4, 16, 128, 8, 256).transpose(2, 0, 3, 1, 4).reshape(128, 131072))

    def colmajor(v, nch):
        return C(np.asarray(v, f32).reshape(nch, 128).T)
    shared["gamma_c"] = colmajor(I["L0_attn_ln_gamma"], 16)
    shared["beta_c"] = colmajor(I["L0_attn_ln_beta"], 16)
    shared["dskip_c"] = colmajor(I["L0_attn_D_skip"], 16)
    shared["bout1_c"] = colmajor(I["L1_attn_b_out"], 16)
    shared["bqkv_c"] = colmajor(I["L1_attn_b_qkv"], 48)
    shared["bqkv_row"] = C(np.asarray(I["L1_attn_b_qkv"], f32)[None, :])
    shared["b_in"] = C(np.asarray(I["L0_attn_b_in"], f32)[:, None])
    shared["b_down"] = C(np.asarray(I["L0_attn_b_down"], f32)[:, None])
    shared["decay_logit"] = C(np.asarray(I["L0_attn_decay_logit"], f32)[:, None])

    h128 = (_hadamard(128) * (2.0 ** -7)).astype(bf)
    shared["h128"] = h128
    shared["ident_bf"] = np.eye(128, dtype=bf)
    dm = np.zeros((4, 128, SL), dtype=bf)
    for j in range(4):
        ki = np.arange(128)[:, None]
        qi = np.arange(SL)[None, :]
        dm[j] = ((128 * j + ki) <= qi).astype(bf)
    shared["dmask"] = C(dm.reshape(4 * 128, SL))

    per_core = []
    for c in range(8):
        pc = {}
        j0, j1 = c, 7 - c
        s0 = x[0, SL * j0:SL * (j0 + 1), :]
        s1 = x[1, SL * j1:SL * (j1 + 1), :]
        pc["x_fm"] = C(np.concatenate([s0.T, s1.T], axis=1))
        pc["rank_in"] = np.array([[c]], dtype=np.int32)
        e = np.zeros((2, 8, 16), f32)
        m = np.zeros((2, 8, 16), f32)
        for s, (b, j) in enumerate([(0, j0), (1, j1)]):
            for owner in range(8):
                for sown in range(2):
                    ag = owner * 2 + sown
                    bb = 0 if sown == 0 else 1
                    jj = owner if sown == 0 else 7 - owner
                    if bb == b and jj < j:
                        m[s, :, ag] = 1.0
                        e[s, :, ag] = 512.0 * (j - 1 - jj)
        pc["scan_e"] = C(np.concatenate([e[0], e[1]], axis=1))
        pc["scan_m"] = C(np.concatenate([m[0], m[1]], axis=1))
        per_core.append(pc)
    return shared, per_core



STOP_AFTER = None

def build_program(dbg=()):
    nc = bass.Bass("TRN2", target_bir_lowering=False)
    g = {}

    def din(name, shape, dt=F32):
        g[name] = nc.dram_tensor(name, shape, dt, kind="ExternalInput").ap()
        return g[name]

    din("x_fm", [D, TPC]); din("rank_in", [1, 1], I32)
    din("scan_e", [8, 32]); din("scan_m", [8, 32])
    din("w_in", [D, 8]); din("w_out_aug", [9, D]); din("w_down0", [D, 32]); din("w_up_aug", [33, D])
    din("w1tp", [128, 131072]); din("w3tp", [128, 131072]); din("w2tp", [128, 131072])
    din("wqkvtp", [128, 98304]); din("wouttp", [128, 32768])
    din("wgp", [128, 131072]); din("wup", [128, 131072]); din("wdp", [128, 131072])
    din("gamma_c", [128, 16]); din("beta_c", [128, 16]); din("dskip_c", [128, 16])
    din("bout1_c", [128, 16]); din("bqkv_c", [128, 48]); din("bqkv_row", [1, 3 * D])
    din("b_in", [8, 1]); din("b_down", [32, 1]); din("decay_logit", [8, 1])
    din("h128", [128, 128], BF16); din("ident_bf", [128, 128], BF16); din("dmask", [512, SL], BF16)

    y = nc.dram_tensor("y_fm", [D, TPC], F32, kind="ExternalOutput").ap()

    sum_contrib = nc.dram_tensor("sum_contrib", [2, 8], F32).ap()
    x_spill = nc.dram_tensor("x_spill", [128, KC * TPC], F32).ap()
    ffn0_spill = nc.dram_tensor("ffn0_spill", [128, KC * TPC], F32).ap()
    sum_ag = nc.dram_tensor("sum_ag", [16, 8], F32).ap()
    k_contrib = nc.dram_tensor("k_contrib", [2 * NH * EH, SL], BF16).ap()
    v_contrib = nc.dram_tensor("v_contrib", [2 * NH * 128, SL], BF16).ap()
    k_ag = nc.dram_tensor("k_ag", [8 * 2 * NH * EH, SL], BF16, addr_space="Shared").ap()
    v_ag = nc.dram_tensor("v_ag", [8 * 2 * NH * 128, SL], BF16, addr_space="Shared").ap()

    dbg_outs = {}
    for name, shape, dt in dbg:
        dbg_outs[name] = nc.dram_tensor("dbg_" + name, shape, dt, kind="ExternalOutput").ap()

    ctx = Ctx(nc)
    with contextlib.ExitStack() as es:
        ctx._es = es
        ctx.add_queue("pe", "tensor", "pe")
        ctx.add_queue("dve", "vector", "ve")
        ctx.add_queue("act", "scalar", "ac")
        ctx.add_queue("sp", "sync")
        ctx.add_queue("gp", "gpsimd", "gc")
        grp_init = ctx.dma_group("g_init")
        grp_gpi = ctx.dma_group("g_gpi")
        grp_out = ctx.dma_group("g_out")

        def sb(name, shape, dt):
            h = es.enter_context(nc.sbuf_tensor(name, shape, dt))
            return h

        X = sb("X", [128, KC * TPC], F32)
        XN1 = sb("XN1", [128, KC * TPC], BF16)
        BA = sb("BA", [128, 16384], BF16)
        BB = sb("BB", [128, 16384], BF16)
        SC = sb("SC", [128, 4 * SL], F32)
        gamma = sb("gamma", [128, 16], F32); beta = sb("beta", [128, 16], F32)
        dskip = sb("dskip", [128, 16], F32); bout1 = sb("bout1", [128, 16], F32)
        bqkvc = sb("bqkvc", [128, 48], F32)
        w_in_b = sb("w_in_b", [128, KC * 8], BF16)
        w_out_b = sb("w_out_b", [9, D], BF16)
        w_dn_b = sb("w_dn_b", [128, KC * 32], BF16)
        w_up_b = sb("w_up_b", [33, D], BF16)
        h128_s = sb("h128_s", [128, 128], BF16)
        ident_s = sb("ident_s", [128, 128], BF16)
        dmask_s = sb("dmask_s", [128, 4 * SL], BF16)
        ones_cf = sb("ones_cf", [128, 1], F32)
        ones_rf = sb("ones_rf", [1, 128], F32)
        ones_cb = sb("ones_cb", [128, 1], BF16)
        ones_rb = sb("ones_rb", [1, 128], BF16)
        eps_t = sb("eps_t", [1, 1], F32)
        b_in_s = sb("b_in_s", [8, 1], F32)
        b_dn_s = sb("b_dn_s", [32, 1], F32)
        dlg_s = sb("dlg_s", [8, 1], F32)
        mu_r = sb("mu_r", [1, SL], F32); rstd_r = sb("rstd_r", [1, SL], F32)
        row_t = sb("row_t", [1, SL], F32)
        et_s = sb("et_s", [8, 32], F32); mt_s = sb("mt_s", [8, 32], F32)

        bX = [ctx.buf(f"X{k}") for k in range(KC)]
        bXN1 = [ctx.buf(f"XN1_{k}") for k in range(KC)]
        bBA = ctx.buf("BA"); bBB = ctx.buf("BB")
        bSC = [ctx.buf(f"SC{i}") for i in range(4)]
        bsm = ctx.buf("smalls")
        bmu = ctx.buf("mu"); brstd = ctx.buf("rstd"); brow = ctx.buf("row")
        bdram = ctx.buf("dram_misc")

        def Xs(k, s=None):
            if s is None:
                return X[:, k * TPC:(k + 1) * TPC]
            return X[:, k * TPC + s * SL: k * TPC + (s + 1) * SL]
        def XN1s(k, s=None):
            if s is None:
                return XN1[:, k * TPC:(k + 1) * TPC]
            return XN1[:, k * TPC + s * SL: k * TPC + (s + 1) * SL]
        def SCt(i):
            return SC[:, i * SL:(i + 1) * SL]

        for k in range(KC):
            ctx.dma("sp", lambda e, k=k: e.dma_start(out=Xs(k), in_=g["x_fm"][k * 128:(k + 1) * 128, :]),
                    grp_init, writes=[bX[k]])
        for nm, tile in [("gamma_c", gamma), ("beta_c", beta), ("dskip_c", dskip),
                         ("bout1_c", bout1), ("bqkv_c", bqkvc), ("b_in", b_in_s),
                         ("b_down", b_dn_s), ("decay_logit", dlg_s),
                         ("scan_e", et_s), ("scan_m", mt_s),
                         ("h128", h128_s), ("ident_bf", ident_s)]:
            ctx.dma("sp", lambda e, nm=nm, tile=tile: e.dma_start(out=tile[:, :], in_=g[nm][:, :]),
                    grp_init, writes=[bsm])
        for j in range(4):
            ctx.dma("sp", lambda e, j=j: e.dma_start(
                out=dmask_s[:, j * SL:(j + 1) * SL], in_=g["dmask"][j * 128:(j + 1) * 128, :]),
                grp_init, writes=[bsm])
        ctx.dma("gp", lambda e: e.dma_start(out=w_in_b[:, :].rearrange("p (kc r) -> p kc r", r=8),
                                            in_=g["w_in"].rearrange("(kc p) r -> p kc r", p=128)),
                grp_gpi, writes=[bsm])
        ctx.dma("gp", lambda e: e.dma_start(out=w_out_b[:, :], in_=g["w_out_aug"][:, :]), grp_gpi, writes=[bsm])
        ctx.dma("gp", lambda e: e.dma_start(out=w_dn_b[:, :].rearrange("p (kc r) -> p kc r", r=32),
                                            in_=g["w_down0"].rearrange("(kc p) r -> p kc r", p=128)),
                grp_gpi, writes=[bsm])
        ctx.dma("gp", lambda e: e.dma_start(out=w_up_b[:, :], in_=g["w_up_aug"][:, :]), grp_gpi, writes=[bsm])
        for tile, val in [(ones_cf, 1.0), (ones_rf, 1.0), (ones_cb, 1.0), (ones_rb, 1.0), (eps_t, EPS)]:
            ctx.do("gp", lambda e, tile=tile, val=val: e.memset(tile[:, :], val), writes=[bsm])

        bdbg = ctx.buf("dbg")
        def dbg_snap(name, src_s, bufs, nchunks=KC):
            if name not in dbg_outs:
                return
            for k in range(nchunks):
                ctx.dma("sp", lambda e, k=k, name=name: e.dma_start(
                    out=dbg_outs[name][k * 128:(k + 1) * 128, :], in_=src_s(k)),
                    grp_out, reads=[bufs[k]], writes=[bdbg])

        def ln_stats(src_s, src_dtype, psA, psB, psG, bsrc, bpsA, bpsB, bpsG, s):
            ones = ones_cf if src_dtype == F32 else ones_cb
            for k in range(KC):
                ctx.do("act", lambda e, k=k: e.activation(out=SCt(0)[:, :], in_=src_s(k, s), func=AF.Square),
                       reads=[bsrc[k]], writes=[bSC[0]])
                ctx.do("pe", lambda e, k=k: e.matmul(psG[0:1, 0:SL], ones[:, :], src_s(k, s),
                                                     start=(k == 0), stop=(k == KC - 1)),
                       reads=[bsrc[k], bsm], writes=[bpsG])
                ctx.do("pe", lambda e, k=k: e.matmul(psG[0:1, SL:2 * SL], ones_cf[:, :], SCt(0)[:, :],
                                                     start=(k == 0), stop=(k == KC - 1)),
                       reads=[bSC[0], bsm], writes=[bpsG])
            ctx.do("dve", lambda e: e.tensor_scalar(out=mu_r[:, :], in0=psG[0:1, 0:SL], scalar1=1.0 / D,
                                                    scalar2=None, op0=ALU.mult),
                   reads=[bpsG], writes=[bmu])
            ctx.do("dve", lambda e: e.tensor_scalar(out=row_t[:, :], in0=psG[0:1, SL:2 * SL], scalar1=1.0 / D,
                                                    scalar2=None, op0=ALU.mult),
                   reads=[bpsG], writes=[brow])
            ctx.do("dve", lambda e: e.tensor_tensor(out=rstd_r[:, :], in0=mu_r[:, :], in1=mu_r[:, :], op=ALU.mult),
                   reads=[bmu], writes=[brstd])
            ctx.do("dve", lambda e: e.tensor_tensor(out=rstd_r[:, :], in0=row_t[:, :], in1=rstd_r[:, :], op=ALU.subtract),
                   reads=[brow, brstd], writes=[brstd])
            ctx.do("act", lambda e: e.activation(out=rstd_r[:, :], in_=rstd_r[:, :], func=AF.Ln, bias=eps_t[:, :]),
                   reads=[brstd, bsm], writes=[brstd])
            ctx.do("act", lambda e: e.activation(out=rstd_r[:, :], in_=rstd_r[:, :], func=AF.Exp, scale=-0.5),
                   reads=[brstd], writes=[brstd])
            ctx.do("pe", lambda e: e.matmul(psA[:, :], ones_rf[:, :], mu_r[:, :], start=True, stop=True),
                   reads=[bmu, bsm], writes=[bpsA])
            ctx.do("pe", lambda e: e.matmul(psB[:, :], ones_rf[:, :], rstd_r[:, :], start=True, stop=True),
                   reads=[brstd, bsm], writes=[bpsB])

        _lnc = [0]
        def ln_plain():
            _lnc[0] += 1
            with nc.psum_tensor(f"lnA{_lnc[0]}", [128, SL], F32) as psA, \
                 nc.psum_tensor(f"lnB{_lnc[0]}", [128, SL], F32) as psB, \
                 nc.psum_tensor(f"lnG{_lnc[0]}", [1, 2 * SL], F32) as psG:
                bpsA = ctx.buf("lnA"); bpsB = ctx.buf("lnB"); bpsG = ctx.buf("lnG")
                for s in range(NSLAB):
                    ln_stats(Xs, F32, psA, psB, psG, bX, bpsA, bpsB, bpsG, s)
                    for k in range(KC):
                        ctx.do("dve", lambda e, k=k, s=s: e.tensor_tensor(
                            out=SCt(1)[:, :], in0=Xs(k, s), in1=psA[:, :], op=ALU.subtract),
                            reads=[bX[k], bpsA], writes=[bSC[1]])
                        ctx.do("dve", lambda e, k=k, s=s: e.tensor_tensor(
                            out=XN1s(k, s), in0=SCt(1)[:, :], in1=psB[:, :], op=ALU.mult),
                            reads=[bSC[1], bpsB], writes=[bXN1[k]])
            ctx.fence()

        ln_plain()
        dbg_snap("xn1", XN1s, bXN1)

        def ln_affine(src_s, bsrc, src_dt, dst_s, bdst, psA, psB, psG, bpsA, bpsB, bpsG):
            for s in range(NSLAB):
                ln_stats(src_s, src_dt, psA, psB, psG, bsrc, bpsA, bpsB, bpsG, s)
                for k in range(KC):
                    ctx.do("dve", lambda e, k=k, s=s: e.tensor_tensor(
                        out=SCt(1)[:, :], in0=src_s(k, s), in1=psA[:, :], op=ALU.subtract),
                        reads=[bsrc[k], bpsA], writes=[bSC[1]])
                    ctx.do("dve", lambda e: e.tensor_tensor(
                        out=SCt(1)[:, :], in0=SCt(1)[:, :], in1=psB[:, :], op=ALU.mult),
                        reads=[bSC[1], bpsB], writes=[bSC[1]])
                    ctx.do("dve", lambda e, k=k, s=s: e.tensor_scalar(
                        out=dst_s(k, s), in0=SCt(1)[:, :], scalar1=gamma[:, k:k + 1],
                        scalar2=beta[:, k:k + 1], op0=ALU.mult, op1=ALU.add),
                        reads=[bSC[1], bsm], writes=[bdst[k]])

        def phase2():
            with contextlib.ExitStack() as p2s:
                psA = p2s.enter_context(nc.psum_tensor("p2A", [128, SL], F32))
                psB = p2s.enter_context(nc.psum_tensor("p2B", [128, SL], F32))
                psG = p2s.enter_context(nc.psum_tensor("p2G", [1, 2 * SL], F32))
                psC = p2s.enter_context(nc.psum_tensor("p2C", [128, SL], F32))
                psD = p2s.enter_context(nc.psum_tensor("p2D", [128, SL], F32))
                u_s = p2s.enter_context(nc.sbuf_tensor("u_s", [8, TPC], F32))
                h_s = p2s.enter_context(nc.sbuf_tensor("h_s", [8, TPC], F32))
                dbc = p2s.enter_context(nc.sbuf_tensor("dbc", [8, SL], F32))
                ppow = p2s.enter_context(nc.sbuf_tensor("ppow", [8, SL], F32))
                dec_s = p2s.enter_context(nc.sbuf_tensor("dec_s", [8, 1], F32))
                lnd_s = p2s.enter_context(nc.sbuf_tensor("lnd_s", [8, 1], F32))
                carry = p2s.enter_context(nc.sbuf_tensor("carry", [8, 2], F32))
                pw = p2s.enter_context(nc.sbuf_tensor("pw", [8, 16], F32))
                st_t = p2s.enter_context(nc.sbuf_tensor("st_t", [8, 16], F32))
                h_aug = p2s.enter_context(nc.sbuf_tensor("h_aug", [9, TPC], BF16))
                hf_aug = p2s.enter_context(nc.sbuf_tensor("hf_aug", [33, TPC], BF16))
                bpsA = ctx.buf("p2A"); bpsB = ctx.buf("p2B"); bpsG = ctx.buf("p2G")
                bpsC = ctx.buf("p2C"); bpsD = ctx.buf("p2D")
                bu = ctx.buf("u"); bh = ctx.buf("h"); bdbc = ctx.buf("dbc")
                bppow = ctx.buf("ppow"); bdec = ctx.buf("dec"); bcar = ctx.buf("carry")
                bpw = ctx.buf("pw"); bstt = ctx.buf("st_t"); bhaug = ctx.buf("h_aug")
                bhf = ctx.buf("hf_aug")
                bBAc = [ctx.buf(f"BAc{k}") for k in range(KC)]
                bBBc = [ctx.buf(f"BBc{k}") for k in range(KC)]
                grp_ctr = ctx.dma_group("g_ctr")

                def BAs(k, s):
                    return BA[:, k * TPC + s * SL: k * TPC + (s + 1) * SL]
                def BBs(k, s):
                    return BB[:, k * TPC + s * SL: k * TPC + (s + 1) * SL]

                ln_affine(XN1s, bXN1, BF16, BAs, bBAc, psA, psB, psG, bpsA, bpsB, bpsG)

                for s in range(NSLAB):
                    for k in range(KC):
                        ctx.do("pe", lambda e, k=k, s=s: e.matmul(
                            psC[0:8, :], w_in_b[:, k * 8:(k + 1) * 8], BAs(k, s),
                            start=(k == 0), stop=(k == KC - 1)),
                            reads=[bBAc[k], bsm], writes=[bpsC])
                    ctx.do("dve", lambda e, s=s: e.tensor_scalar(
                        out=u_s[0:8, s * SL:(s + 1) * SL], in0=psC[0:8, :], scalar1=b_in_s[:, :],
                        scalar2=None, op0=ALU.add),
                        reads=[bpsC, bsm], writes=[bu])

                ctx.do("act", lambda e: e.activation(out=dec_s[:, :], in_=dlg_s[:, :], func=AF.Sigmoid),
                       reads=[bsm], writes=[bdec])
                ctx.do("act", lambda e: e.activation(out=lnd_s[:, :], in_=dec_s[:, :], func=AF.Ln),
                       reads=[bdec], writes=[bdec])
                ctx.do("gp", lambda e: e.iota(ppow[0:8, :], pattern=[[1, SL]], base=1,
                                              channel_multiplier=0, allow_small_or_imprecise_dtypes=True),
                       writes=[bppow])
                ctx.do("dve", lambda e: e.tensor_scalar(
                    out=dbc[0:8, :], in0=ppow[0:8, :], scalar1=0.0, scalar2=dec_s[:, :],
                    op0=ALU.mult, op1=ALU.add),
                    reads=[bppow, bdec], writes=[bdbc])
                ctx.do("dve", lambda e: e.tensor_scalar(
                    out=ppow[0:8, :], in0=ppow[0:8, :], scalar1=lnd_s[:, :], scalar2=None, op0=ALU.mult),
                    reads=[bppow, bdec], writes=[bppow])
                ctx.do("act", lambda e: e.activation(out=ppow[0:8, :], in_=ppow[0:8, :], func=AF.Exp),
                       reads=[bppow], writes=[bppow])
                for s in range(NSLAB):
                    ctx.do("dve", lambda e, s=s: e.tensor_tensor_scan(
                        out=h_s[0:8, s * SL:(s + 1) * SL], data0=dbc[0:8, :],
                        data1=u_s[0:8, s * SL:(s + 1) * SL], initial=0.0, op0=ALU.mult, op1=ALU.add),
                        reads=[bdbc, bu], writes=[bh])
                for s in range(NSLAB):
                    ctx.dma("sp", lambda e, s=s: e.dma_start(
                        out=sum_contrib[s:s + 1, :], in_=h_s[0:8, (s + 1) * SL - 1:(s + 1) * SL]),
                        grp_ctr, reads=[bh], writes=[bdram])
                ctx.cc(lambda e: e.collective_compute(
                    "AllGather", ALU.bypass, replica_groups=[list(range(8))],
                    ins=[sum_contrib[:, :]], outs=[sum_ag[:, :]]),
                    reads=[bdram], writes=[bdram])
                ctx.dma("sp", lambda e: e.dma_start(
                    out=st_t[:, 0:16], in_=sum_ag.rearrange("ag r -> r ag"),
                    allow_slow_non_contiguous=True),
                    grp_ctr, reads=[bdram], writes=[bstt])
                for s in range(NSLAB):
                    ctx.do("dve", lambda e, s=s: e.tensor_scalar(
                        out=pw[:, :], in0=et_s[:, s * 16:(s + 1) * 16], scalar1=lnd_s[:, :],
                        scalar2=None, op0=ALU.mult),
                        reads=[bsm, bdec], writes=[bpw])
                    ctx.do("act", lambda e: e.activation(out=pw[:, :], in_=pw[:, :], func=AF.Exp),
                           reads=[bpw], writes=[bpw])
                    ctx.do("dve", lambda e, s=s: e.tensor_tensor(
                        out=pw[:, :], in0=pw[:, :], in1=mt_s[:, s * 16:(s + 1) * 16], op=ALU.mult),
                        reads=[bpw, bsm], writes=[bpw])
                    ctx.do("dve", lambda e: e.tensor_tensor(
                        out=pw[:, :], in0=pw[:, :], in1=st_t[:, :], op=ALU.mult),
                        reads=[bpw, bstt], writes=[bpw])
                    ctx.do("dve", lambda e, s=s: e.tensor_reduce(
                        out=carry[:, s:s + 1], in_=pw[:, :], axis=AX.X, op=ALU.add),
                        reads=[bpw], writes=[bcar])
                    ctx.do("dve", lambda e, s=s: e.scalar_tensor_tensor(
                        out=h_s[0:8, s * SL:(s + 1) * SL], in0=ppow[0:8, :], scalar=carry[:, s:s + 1],
                        in1=h_s[0:8, s * SL:(s + 1) * SL], op0=ALU.mult, op1=ALU.add),
                        reads=[bppow, bcar, bh], writes=[bh])
                ctx.do("gp", lambda e: e.memset(h_aug[0:9, :], 1.0), writes=[bhaug])
                ctx.do("act", lambda e: e.activation(out=h_aug[0:8, :], in_=h_s[0:8, :], func=AF.Copy),
                       reads=[bh], writes=[bhaug])

                for s in range(NSLAB):
                    for k in range(KC):
                        ctx.do("pe", lambda e, k=k, s=s: e.matmul(
                            psD[:, :], w_out_b[:, k * 128:(k + 1) * 128],
                            h_aug[:, s * SL:(s + 1) * SL], start=True, stop=True),
                            reads=[bhaug, bsm], writes=[bpsD])
                        ctx.do("dve", lambda e, k=k, s=s: e.scalar_tensor_tensor(
                            out=SCt(2)[:, :], in0=BAs(k, s), scalar=dskip[:, k:k + 1],
                            in1=psD[:, :], op0=ALU.mult, op1=ALU.add),
                            reads=[bBAc[k], bsm, bpsD], writes=[bSC[2]])
                        ctx.do("dve", lambda e, k=k, s=s: e.tensor_tensor(
                            out=SCt(3)[:, :], in0=SCt(2)[:, :], in1=XN1s(k, s), op=ALU.add),
                            reads=[bSC[2], bXN1[k]], writes=[bSC[3]])
                        ctx.do("dve", lambda e, k=k, s=s: e.tensor_tensor(
                            out=Xs(k, s), in0=Xs(k, s), in1=SCt(3)[:, :], op=ALU.add),
                            reads=[bSC[3]], writes=[bX[k]])
                        ctx.do("act", lambda e, k=k, s=s: e.activation(
                            out=BBs(k, s), in_=SCt(3)[:, :], func=AF.Copy),
                            reads=[bSC[3]], writes=[bBBc[k]])

                ln_affine(BBs, bBBc, BF16, BAs, bBAc, psA, psB, psG, bpsA, bpsB, bpsG)

                ctx.do("gp", lambda e: e.memset(hf_aug[0:33, :], 1.0), writes=[bhf])
                for s in range(NSLAB):
                    for k in range(KC):
                        ctx.do("pe", lambda e, k=k, s=s: e.matmul(
                            psC[0:32, :], w_dn_b[:, k * 32:(k + 1) * 32], BAs(k, s),
                            start=(k == 0), stop=(k == KC - 1)),
                            reads=[bBAc[k], bsm], writes=[bpsC])
                    ctx.do("act", lambda e, s=s: e.activation(
                        out=hf_aug[0:32, s * SL:(s + 1) * SL], in_=psC[0:32, :], func=AF.Relu,
                        bias=b_dn_s[:, :]),
                        reads=[bpsC, bsm], writes=[bhf])

                for s in range(NSLAB):
                    for k in range(KC):
                        ctx.do("pe", lambda e, k=k, s=s: e.matmul(
                            psD[:, :], w_up_b[:, k * 128:(k + 1) * 128],
                            hf_aug[:, s * SL:(s + 1) * SL], start=True, stop=True),
                            reads=[bhf, bsm], writes=[bpsD])
                        ctx.do("dve", lambda e, k=k, s=s: e.tensor_tensor(
                            out=Xs(k, s), in0=Xs(k, s), in1=psD[:, :], op=ALU.add),
                            reads=[bpsD], writes=[bX[k]])
            ctx.fence()

        phase2()
        dbg_snap("X2", Xs, bX)

        WS = es.enter_context(nc.sbuf_tensor("WS", [128, 8192], BF16))
        bWS = [ctx.buf(f"WS{i}") for i in range(4)]
        gws = [ctx.dma_group(f"g_ws{i}") for i in range(4)]
        def WSs(i):
            return WS[:, i * 2048:(i + 1) * 2048]

        SQRT2 = 1.4142135623730951

        SCB = SC.bitcast(BF16)
        XB = X.bitcast(BF16)

        def phase3():
            bhid = [ctx.buf(f"hid{j}") for j in range(132)]
            def slot_ap(i):
                if i < 32:
                    return BA[:, i * SL:(i + 1) * SL]
                if i < 64:
                    return BB[:, (i - 32) * SL:(i - 32 + 1) * SL]
                if i < 128:
                    return XB[:, (i - 64) * SL:(i - 64 + 1) * SL]
                return SCB[:, 2048 + (i - 128) * SL: 2048 + (i - 128 + 1) * SL]
            grp_sp3 = ctx.dma_group("g_sp3")
            with contextlib.ExitStack() as p3s:
                ps = [p3s.enter_context(nc.psum_tensor(f"p3_{i}", [128, SL], F32)) for i in range(8)]
                bps = [ctx.buf(f"p3_{i}") for i in range(8)]
                for k in range(KC):
                    ctx.dma("sp", lambda e, k=k: e.dma_start(
                        out=x_spill[:, k * TPC:(k + 1) * TPC], in_=Xs(k)),
                        grp_sp3, reads=[bX[k]], writes=[bdram])
                spill_tok = ("D", grp_sp3.sem, ctx.semcount[grp_sp3.sem])
                for j in range(64, 128):
                    bhid[j].w = spill_tok
                for i, sp in enumerate((128, 129, 130, 131)):
                    bhid[sp].w = bhid[sp].w or bSC[2 + i // 2].w
                    bhid[sp].r = bhid[sp].r + list(bSC[2 + i // 2].r)

                chunk_slot = [list(range(64)), list(range(64, 128))]
                spares = [[128, 129], [130, 131]]
                stage_done = [[[False] * 64 for _ in range(7)] for _ in range(2)]
                pend = [[], []]
                for slx in range(2):
                    for st in range(6):
                        dd = 1 << st
                        for base in range(0, 64, dd * 2):
                            for a in range(base, base + dd):
                                pend[slx].append((st, a, a + dd))

                def try_emit_stages(slx):
                    prog = True
                    while prog:
                        prog = False
                        for idx in range(len(pend[slx])):
                            st, a, b2 = pend[slx][idx]
                            if stage_done[slx][st][a] and stage_done[slx][st][b2]:
                                sA, sB = chunk_slot[slx][a], chunk_slot[slx][b2]
                                d1, d2 = spares[slx]
                                ctx.do("dve", lambda e, sA=sA, sB=sB, d1=d1: e.tensor_tensor(
                                    out=slot_ap(d1), in0=slot_ap(sA), in1=slot_ap(sB), op=ALU.add),
                                    reads=[bhid[sA], bhid[sB]], writes=[bhid[d1]])
                                ctx.do("dve", lambda e, sA=sA, sB=sB, d2=d2: e.tensor_tensor(
                                    out=slot_ap(d2), in0=slot_ap(sA), in1=slot_ap(sB), op=ALU.subtract),
                                    reads=[bhid[sA], bhid[sB]], writes=[bhid[d2]])
                                chunk_slot[slx][a] = d1
                                chunk_slot[slx][b2] = d2
                                spares[slx] = [sA, sB]
                                stage_done[slx][st + 1][a] = True
                                stage_done[slx][st + 1][b2] = True
                                pend[slx].pop(idx)
                                prog = True
                                break

                for j in range(64):
                    wb = 2 * (j % 2)
                    for nm, wsl in (("w1tp", wb), ("w3tp", wb + 1)):
                        ctx.dma("gp", lambda e, nm=nm, wsl=wsl, j=j: e.dma_start(
                            out=WSs(wsl), in_=g[nm][:, j * 2048:(j + 1) * 2048]),
                            gws[wsl], writes=[bWS[wsl]])
                    for slx in range(2):
                        pg, pu = ps[2 * slx], ps[2 * slx + 1]
                        bpg, bpu = bps[2 * slx], bps[2 * slx + 1]
                        for k in range(KC):
                            ctx.do("pe", lambda e, k=k, slx=slx, wb=wb, pg=pg: e.matmul(
                                pg[:, :], WSs(wb)[:, k * 128:(k + 1) * 128], XN1s(k, slx),
                                start=(k == 0), stop=(k == KC - 1)),
                                reads=[bWS[wb], bXN1[k]], writes=[bpg])
                        for k in range(KC):
                            ctx.do("pe", lambda e, k=k, slx=slx, wb=wb, pu=pu: e.matmul(
                                pu[:, :], WSs(wb + 1)[:, k * 128:(k + 1) * 128], XN1s(k, slx),
                                start=(k == 0), stop=(k == KC - 1)),
                                reads=[bWS[wb + 1], bXN1[k]], writes=[bpu])
                        sct = slx
                        ctx.do("act", lambda e, pg=pg, sct=sct: e.activation(
                            out=SCt(sct)[:, :], in_=pg[:, :], func=AF.Silu),
                            reads=[bpg], writes=[bSC[sct]])
                        dst = j if slx == 0 else 64 + j
                        ctx.do("dve", lambda e, dst=dst, pu=pu, sct=sct: e.scalar_tensor_tensor(
                            out=slot_ap(dst), in0=SCt(sct)[:, :], scalar=SQRT2, in1=pu[:, :],
                            op0=ALU.mult, op1=ALU.mult),
                            reads=[bSC[sct], bpu], writes=[bhid[dst]])
                        pb = (2 * j + slx) % 2
                        ph = ps[4 + pb]
                        ctx.do("pe", lambda e, dst=dst, ph=ph: e.matmul(
                            ph[:, :], h128_s[:, :], slot_ap(dst), start=True, stop=True),
                            reads=[bhid[dst], bsm], writes=[bps[4 + pb]])
                        ctx.do("act", lambda e, dst=dst, ph=ph: e.activation(
                            out=slot_ap(dst), in_=ph[:, :], func=AF.Copy),
                            reads=[bps[4 + pb]], writes=[bhid[dst]])
                        stage_done[slx][0][j] = True
                        try_emit_stages(slx)
                for slx in range(2):
                    try_emit_stages(slx)
                    assert not pend[slx]

                for q in range(4):
                    for k in range(64):
                        wsl = 2 + ((k // 4) % 2)
                        if k % 4 == 0:
                            ctx.dma("gp", lambda e, k=k, q=q, wsl=wsl: e.dma_start(
                                out=WSs(wsl),
                                in_=g["w2tp"][:, q * 32768 + k * 512: q * 32768 + (k + 4) * 512]),
                                gws[wsl], writes=[bWS[wsl]])
                        for slx in range(2):
                            for dcl in range(4):
                                pw = ps[slx * 4 + dcl]
                                si = chunk_slot[slx][k]
                                ctx.do("pe", lambda e, k=k, dcl=dcl, wsl=wsl, pw=pw, si=si: e.matmul(
                                    pw[:, :], WSs(wsl)[:, (k % 4) * 512 + dcl * 128: (k % 4) * 512 + (dcl + 1) * 128],
                                    slot_ap(si), start=(k == 0), stop=(k == 63)),
                                    reads=[bWS[wsl], bhid[si]], writes=[bps[slx * 4 + dcl]])
                    for slx in range(2):
                        for dcl in range(4):
                            pw = ps[slx * 4 + dcl]
                            kd = q * 4 + dcl
                            sct = (slx * 4 + dcl) % 2
                            ctx.do("act", lambda e, pw=pw, sct=sct: e.activation(
                                out=SCt(sct)[:, :], in_=pw[:, :], func=AF.Copy),
                                reads=[bps[slx * 4 + dcl]], writes=[bSC[sct]])
                            ctx.dma("sp", lambda e, kd=kd, slx=slx, sct=sct: e.dma_start(
                                out=ffn0_spill[:, kd * TPC + slx * SL: kd * TPC + (slx + 1) * SL],
                                in_=SCt(sct)[:, :]),
                                grp_sp3, reads=[bSC[sct]], writes=[bdram])
            ctx.fence()
            grp_r = [ctx.dma_group("g_xrest0"), ctx.dma_group("g_xrest1")]
            grp_rg = [ctx.dma_group("g_xrestg0"), ctx.dma_group("g_xrestg1")]
            bxa = [ctx.buf("xr0"), ctx.buf("xr1")]
            prev_add = [None, None]
            for k in range(KC):
                pb = k % 2
                ctx.dma("sp", lambda e, k=k, pb=pb: e.dma_start(
                    out=SC[:, pb * TPC:(pb + 1) * TPC],
                    in_=x_spill[:, k * TPC:(k + 1) * TPC]),
                    grp_r[pb], reads=[bdram], writes=[bxa[pb]])
                extra = (prev_add[pb],) if prev_add[pb] else ()
                ctx.dma("gp", lambda e, k=k: e.dma_start(
                    out=Xs(k), in_=ffn0_spill[:, k * TPC:(k + 1) * TPC]),
                    grp_rg[pb], reads=[bdram], writes=[bX[k]], extra=extra)
                prev_add[pb] = ctx.do("dve", lambda e, k=k, pb=pb: e.tensor_tensor(
                    out=Xs(k), in0=Xs(k), in1=SC[:, pb * TPC:(pb + 1) * TPC], op=ALU.add),
                    reads=[bxa[pb]], writes=[bX[k]])
            ctx.fence()

        phase3()

        dbg_snap("X3", Xs, bX)
        if STOP_AFTER == "p3":
            for k in range(KC):
                ctx.dma("sp", lambda e, k=k: e.dma_start(out=y[k * 128:(k + 1) * 128, :], in_=Xs(k)),
                        grp_out, reads=[bX[k]], writes=[bdbg])
            _fo = ctx.semcount["g_out"]
            ctx.raw("sp", lambda e, v=_fo: e.wait_ge(ctx.semh["g_out"], v))
            block = es.enter_context(nc.Block())
            ctx.finalize(block)
            return nc

        ln_plain()
        dbg_snap("xn1b", XN1s, bXN1)

        ATT_SCALE = 0.08838834764831845

        def phase5(part):
            with contextlib.ExitStack() as p5s:
                psA = p5s.enter_context(nc.psum_tensor(f"p5A{part}", [128, SL], F32))
                psB = p5s.enter_context(nc.psum_tensor(f"p5B{part}", [128, SL], F32))
                psC = p5s.enter_context(nc.psum_tensor(f"p5C{part}", [128, SL], F32))
                psD = p5s.enter_context(nc.psum_tensor(f"p5D{part}", [128, SL], F32))
                psE = p5s.enter_context(nc.psum_tensor(f"p5E{part}", [128, SL], F32))
                psL = p5s.enter_context(nc.psum_tensor(f"p5L{part}", [1, SL], F32))
                psB2 = p5s.enter_context(nc.psum_tensor(f"p5L2{part}", [1, SL], F32))
                expb = p5s.enter_context(nc.sbuf_tensor(f"expb{part}", [128, 1], F32))
                bqv = p5s.enter_context(nc.sbuf_tensor(f"bqv{part}", [1, 128], BF16))
                vt0 = p5s.enter_context(nc.sbuf_tensor(f"vt0{part}", [128, 256], BF16))
                bpsA = ctx.buf("p5A"); bpsB = ctx.buf("p5B"); bpsC = ctx.buf("p5C")
                bpsD = ctx.buf("p5D"); bpsE = ctx.buf("p5E"); bpsL = ctx.buf("p5L")
                bpsB2 = ctx.buf("p5L2")
                bexpb = ctx.buf("expb"); bbqv = ctx.buf("bqv")
                bvt0 = [ctx.buf("bvt0a"), ctx.buf("bvt0b")]
                battn = [ctx.buf(f"attn{h}") for h in range(NH)]
                bQ = [ctx.buf(f"Q{h}") for h in range(NH)]
                grp_c = ctx.dma_group(f"g_kvc{part}")
                grp_cv = ctx.dma_group(f"g_kvcv{part}")
                ctx.do("gp", lambda e: e.memset(expb[:, :], -8.0), writes=[bexpb])
                vb_all = [mu_r.bitcast(BF16), rstd_r.bitcast(BF16)]
                if part == 0:
                    for half in range(2):
                        ctx.dma("gp", lambda e, half=half: e.dma_start(
                            out=vb_all[half][:, :],
                            in_=g["bqkv_row"][:, 2 * D + half * 1024: 2 * D + (half + 1) * 1024]),
                            ctx.dma_group("g_vbias"), writes=[bmu if half == 0 else brstd])

                def qslot(h, s):
                    return BB[:, h * TPC + s * SL: h * TPC + (s + 1) * SL]
                def aslot(h, s):
                    return BA[:, h * TPC + s * SL: h * TPC + (s + 1) * SL]

                for h in (range(NH) if part == 0 else ()):
                    wk, wv = h % 2, 2 + h % 2
                    for wsl, sec in ((wk, 16 + h), (wv, 32 + h)):
                        ctx.dma("gp", lambda e, wsl=wsl, sec=sec: e.dma_start(
                            out=WSs(wsl), in_=g["wqkvtp"][:, sec * 2048:(sec + 1) * 2048]),
                            gws[wsl], writes=[bWS[wsl]])
                    bqv_ap = vb_all[h // 8][:, (h % 8) * 128:(h % 8 + 1) * 128]
                    bbqv_ = bmu if h < 8 else brstd
                    for s in range(NSLAB):
                        pk, bpk = (psA, bpsA) if s == 0 else (psC, bpsC)
                        sck = 0 if s == 0 else 1
                        for k in range(KC):
                            ctx.do("pe", lambda e, k=k, s=s, wk=wk, pk=pk: e.matmul(
                                pk[:, :], WSs(wk)[:, k * 128:(k + 1) * 128], XN1s(k, s),
                                start=(k == 0), stop=(k == KC - 1)),
                                reads=[bWS[wk], bXN1[k]], writes=[bpk])
                        ctx.do("dve", lambda e, h=h, pk=pk, sck=sck: e.tensor_scalar(
                            out=SCt(sck)[:, :], in0=pk[:, :], scalar1=bqkvc[:, 16 + h:17 + h],
                            scalar2=None, op0=ALU.add),
                            reads=[bpk, bsm], writes=[bSC[sck]])
                        ctx.dma("gp", lambda e, s=s, h=h, sck=sck: e.dma_start(
                            out=k_contrib[s * 2048 + h * 128: s * 2048 + (h + 1) * 128, :],
                            in_=SCt(sck)[:, :]),
                            grp_c, reads=[bSC[sck]], writes=[bdram])
                        for ts_ in range(4):
                            vb = ts_ % 2
                            pv, bpv = (psB, bpsB) if vb == 0 else (psD, bpsD)
                            for k in range(KC):
                                ctx.do("pe", lambda e, k=k, s=s, ts_=ts_, wv=wv, pv=pv: e.matmul(
                                    pv[:, :128], XN1s(k, s)[:, ts_ * 128:(ts_ + 1) * 128],
                                    WSs(wv)[:, k * 128:(k + 1) * 128],
                                    start=(k == 0), stop=False),
                                    reads=[bWS[wv], bXN1[k]], writes=[bpv])
                            ctx.do("pe", lambda e, pv=pv, bqv_ap=bqv_ap: e.matmul(
                                pv[:, :128], ones_rb[:, :], bqv_ap, start=False, stop=True),
                                reads=[bbqv_, bsm], writes=[bpv])
                            ctx.do("act", lambda e, vb=vb, pv=pv: e.activation(
                                out=vt0[:, vb * 128:(vb + 1) * 128], in_=pv[:, :128], func=AF.Copy),
                                reads=[bpv], writes=[bvt0[vb]])
                            ctx.dma("sp", lambda e, s=s, h=h, ts_=ts_, vb=vb: e.dma_start(
                                out=v_contrib[s * 2048 + h * 128: s * 2048 + (h + 1) * 128,
                                              ts_ * 128:(ts_ + 1) * 128],
                                in_=vt0[:, vb * 128:(vb + 1) * 128]),
                                grp_cv, reads=[bvt0[vb]], writes=[bdram])

                if part == 0:
                    bkc = ctx.buf("kcontrib"); bvc = ctx.buf("vcontrib")
                    bkag = ctx.buf("kag"); bvag = ctx.buf("vag")
                    bkc.w = bdram.w; bkc.r = list(bdram.r)
                    bvc.w = bdram.w; bvc.r = list(bdram.r)
                    ctx.cc(lambda e: e.collective_compute(
                        "AllGather", ALU.bypass, replica_groups=[list(range(8))],
                        ins=[k_contrib[:, :]], outs=[k_ag[:, :]]),
                        reads=[bkc], writes=[bkag])
                    ctx.cc(lambda e: e.collective_compute(
                        "AllGather", ALU.bypass, replica_groups=[list(range(8))],
                        ins=[v_contrib[:, :]], outs=[v_ag[:, :]]),
                        reads=[bvc], writes=[bvag])
                    return

                for h in range(NH):
                    wq = h % 2
                    ctx.dma("gp", lambda e, h=h, wq=wq: e.dma_start(
                        out=WSs(wq), in_=g["wqkvtp"][:, h * 2048:(h + 1) * 2048]),
                        gws[wq], writes=[bWS[wq]])
                    for s in range(NSLAB):
                        pq, bpq = (psA, bpsA) if (2 * h + s) % 2 == 0 else (psB, bpsB)
                        for k in range(KC):
                            ctx.do("pe", lambda e, k=k, s=s, wq=wq, pq=pq: e.matmul(
                                pq[:, :], WSs(wq)[:, k * 128:(k + 1) * 128], XN1s(k, s),
                                start=(k == 0), stop=(k == KC - 1)),
                                reads=[bWS[wq], bXN1[k]], writes=[bpq])
                        ctx.do("dve", lambda e, h=h, s=s, pq=pq: e.tensor_scalar(
                            out=qslot(h, s), in0=pq[:, :], scalar1=bqkvc[:, h:h + 1],
                            scalar2=None, op0=ALU.add),
                            reads=[bpq, bsm], writes=[bQ[h]])

                def ktile(b):
                    return WS[:, b * 512:(b + 1) * 512]
                def vtile(b):
                    return WS[:, 1024 + b * 512: 1024 + (b + 1) * 512]
                def ptile(b):
                    return WS[:, 2048 + b * 2048: 2048 + (b + 1) * 2048]
                bkt = [ctx.buf("kt0"), ctx.buf("kt1")]
                bvt = [ctx.buf("vt0b"), ctx.buf("vt1b")]
                bpt = [ctx.buf("pt0"), ctx.buf("pt1")]
                gkt = [ctx.dma_group(f"g_kt0{part}"), ctx.dma_group(f"g_kt1{part}")]
                gvt = [ctx.dma_group(f"g_vt0{part}"), ctx.dma_group(f"g_vt1{part}")]
                blrow = ctx.buf("lrow")

                ctx.fence()
                ctx.begin_switch(["pe", "dve", "act", "sp"])
                for c8 in range(8):
                    ctx.begin_body()
                    ns = [c8 + 1, 8 - c8]
                    for s in range(NSLAB):
                        for h in range(NH):
                            par = (s * NH + h) % 2
                            psE_, bpsE_ = (psE, bpsE) if par == 0 else (psA, bpsA)
                            psL_, bpsL_ = (psL, bpsL) if par == 0 else (psB2, bpsB2)
                            for kt in range(ns[s]):
                                kb = kt % 2
                                gch = kt if s == 0 else 7 - kt
                                owner = gch
                                sel = 0 if s == 0 else 1
                                krow = owner * 4096 + sel * 2048 + h * 128
                                ctx.dma("sp", lambda e, krow=krow, kb=kb: e.dma_start(
                                    out=ktile(kb), in_=k_ag[krow:krow + 128, :]),
                                    gkt[kb], writes=[bkt[kb]])
                                ctx.dma("act", lambda e, krow=krow, kb=kb: e.dma_start(
                                    out=vtile(kb), in_=v_ag[krow:krow + 128, :]),
                                    gvt[kb], writes=[bvt[kb]])
                                first = (kt == 0)
                                last = (kt == ns[s] - 1)
                                for ksub in range(4):
                                    pss, bpss = ((psC, bpsC) if ksub % 2 == 0 else (psD, bpsD))
                                    ctx.do("pe", lambda e, kb=kb, ksub=ksub, h=h, s=s, pss=pss: e.matmul(
                                        pss[:, :], ktile(kb)[:, ksub * 128:(ksub + 1) * 128],
                                        qslot(h, s), start=True, stop=True),
                                        reads=[bkt[kb], bQ[h]], writes=[bpss])
                                    ctx.do("act", lambda e, kb=kb, ksub=ksub, pss=pss: e.activation(
                                        out=ptile(kb)[:, ksub * 512:(ksub + 1) * 512], in_=pss[:, :],
                                        func=AF.Exp, scale=ATT_SCALE, bias=expb[:, :]),
                                        reads=[bpss, bexpb], writes=[bpt[kb]])
                                    if last:
                                        ctx.do("dve", lambda e, kb=kb, ksub=ksub: e.tensor_tensor(
                                            out=ptile(kb)[:, ksub * 512:(ksub + 1) * 512],
                                            in0=ptile(kb)[:, ksub * 512:(ksub + 1) * 512],
                                            in1=dmask_s[:, ksub * SL:(ksub + 1) * SL], op=ALU.mult),
                                            reads=[bsm], writes=[bpt[kb]])
                                for ksub in range(4):
                                    ctx.do("pe", lambda e, kb=kb, ksub=ksub, st_=(first and ksub == 0),
                                           sp_=(last and ksub == 3), psE_=psE_: e.matmul(
                                        psE_[:, :], vtile(kb)[:, ksub * 128:(ksub + 1) * 128],
                                        ptile(kb)[:, ksub * 512:(ksub + 1) * 512],
                                        start=st_, stop=sp_),
                                        reads=[bvt[kb], bpt[kb]], writes=[bpsE_])
                                    ctx.do("pe", lambda e, kb=kb, ksub=ksub, st_=(first and ksub == 0),
                                           sp_=(last and ksub == 3), psL_=psL_: e.matmul(
                                        psL_[:, :], ones_cb[:, :],
                                        ptile(kb)[:, ksub * 512:(ksub + 1) * 512],
                                        start=st_, stop=sp_),
                                        reads=[bsm, bpt[kb]], writes=[bpsL_])
                            lr_ = row_t
                            ctx.do("dve", lambda e, psL_=psL_, lr_=lr_: e.reciprocal(
                                lr_[:, :], psL_[:, :]),
                                reads=[bpsL_], writes=[blrow])
                            pbc, bpbc = (psC, bpsC) if par == 0 else (psD, bpsD)
                            ctx.do("pe", lambda e, pbc=pbc, lr_=lr_: e.matmul(
                                pbc[:, :], ones_rf[:, :], lr_[:, :], start=True, stop=True),
                                reads=[blrow, bsm], writes=[bpbc])
                            ctx.do("act", lambda e, psE_=psE_, par=par: e.activation(
                                out=SCt(par)[:, :], in_=psE_[:, :], func=AF.Copy),
                                reads=[bpsE_], writes=[bSC[par]])
                            ctx.do("dve", lambda e, h=h, s=s, par=par, pbc=pbc: e.tensor_tensor(
                                out=aslot(h, s), in0=SCt(par)[:, :], in1=pbc[:, :], op=ALU.mult),
                                reads=[bSC[par], bpbc], writes=[battn[h]])
                    ctx.end_body()
                ctx.end_switch(g["rank_in"][0:1, 0:1])

                for q4 in range(8):
                    wsl2 = q4 % 2
                    ctx.dma("gp", lambda e, q4=q4, wsl2=wsl2: e.dma_start(
                        out=WS[:, wsl2 * 4096:(wsl2 + 1) * 4096],
                        in_=g["wouttp"][:, q4 * 4096:(q4 + 1) * 4096],
                        max_dma_last_dim=8192),
                        gws[wsl2], writes=[bWS[2 * wsl2], bWS[2 * wsl2 + 1]])
                    for k in range(NH):
                        wsl = 2 * wsl2
                        for dcl in range(2):
                            for s in range(NSLAB):
                                ps, bps = [(psA, bpsA), (psB, bpsB),
                                           (psC, bpsC), (psD, bpsD)][dcl * 2 + s]
                                ctx.do("pe", lambda e, k=k, dcl=dcl, s=s, wsl2=wsl2, ps=ps: e.matmul(
                                    ps[:, :], WS[:, wsl2 * 4096 + k * 256 + dcl * 128:
                                                 wsl2 * 4096 + k * 256 + (dcl + 1) * 128],
                                    aslot(k, s), start=(k == 0), stop=(k == NH - 1)),
                                    reads=[bWS[2 * wsl2], battn[k]], writes=[bps])
                    for dcl in range(2):
                        for s in range(NSLAB):
                            ps, bps = [(psA, bpsA), (psB, bpsB),
                                       (psC, bpsC), (psD, bpsD)][dcl * 2 + s]
                            kd = q4 * 2 + dcl
                            ctx.do("dve", lambda e, kd=kd, s=s, ps=ps: e.scalar_tensor_tensor(
                                out=Xs(kd, s), in0=ps[:, :], scalar=bout1[:, kd:kd + 1],
                                in1=Xs(kd, s), op0=ALU.add, op1=ALU.add),
                                reads=[bps, bsm], writes=[bX[kd]])
            ctx.fence()

        phase5(0)
        ctx.fence(exclude=("cc",))

        def phase6():
            with contextlib.ExitStack() as p6s:
                psA = p6s.enter_context(nc.psum_tensor("p6A", [128, SL], F32))
                psB = p6s.enter_context(nc.psum_tensor("p6B", [128, SL], F32))
                acc = [p6s.enter_context(nc.psum_tensor(f"p6c{i}", [128, SL], F32)) for i in range(4)]
                psT = p6s.enter_context(nc.psum_tensor("p6T", [128, 1024], BF16))
                bpsA = ctx.buf("p6A"); bpsB = ctx.buf("p6B")
                bacc = [ctx.buf(f"p6c{i}") for i in range(4)]
                bpsT = ctx.buf("p6T")
                bwset = [ctx.buf("p6ws0"), ctx.buf("p6ws1")]
                bhtm = [ctx.buf(f"p6htm{t}") for t in range(8)]
                bhfm = [ctx.buf(f"p6hfm{i}") for i in range(16)]
                bwd = [ctx.buf("p6wd0"), ctx.buf("p6wd1")]
                bSC6 = [[ctx.buf(f"p6sc{p}_{i}") for i in range(4)] for p in range(2)]
                gw6 = [ctx.dma_group("g_w6a"), ctx.dma_group("g_w6b")]
                gw6d = [ctx.dma_group("g_w6d0"), ctx.dma_group("g_w6d1")]
                IW = 256

                def wgsl(st_, k):
                    return BB[:, st_ * 8192 + k * IW: st_ * 8192 + (k + 1) * IW]
                def wusl(st_, k):
                    return BB[:, st_ * 8192 + 4096 + k * IW: st_ * 8192 + 4096 + (k + 1) * IW]
                def htm(tsub):
                    return WS[:, tsub * IW:(tsub + 1) * IW]
                def hfm(ic):
                    return BA[:, ic * TPC:(ic + 1) * TPC]
                def wd_t(hb):
                    return WS[:, 4096 + hb * 2048: 4096 + (hb + 1) * 2048]

                for ig in range(4):
                    for isl8 in range(8):
                        ii = ig * 8 + isl8
                        st_ = ii % 2
                        ctx.dma("gp", lambda e, ii=ii, st_=st_: e.dma_start(
                            out=BB[:, st_ * 8192: st_ * 8192 + 4096],
                            in_=g["wgp"][:, ii * 4096:(ii + 1) * 4096],
                            max_dma_last_dim=8192),
                            gw6[st_], writes=[bwset[st_]])
                        ctx.dma("gp", lambda e, ii=ii, st_=st_: e.dma_start(
                            out=BB[:, st_ * 8192 + 4096: st_ * 8192 + 8192],
                            in_=g["wup"][:, ii * 4096:(ii + 1) * 4096],
                            max_dma_last_dim=8192),
                            gw6[st_], writes=[bwset[st_]])
                        for tsub in range(8):
                            s, tl = tsub // 4, tsub % 4
                            pp = tsub % 2
                            pg, bpg = (psA, bpsA) if pp == 0 else (acc[0], bacc[0])
                            pu, bpu = (psB, bpsB) if pp == 0 else (acc[1], bacc[1])
                            PO = pp * IW
                            bSCp = [bSC6[pp][i] for i in range(4)]
                            for k in range(KC):
                                ctx.do("pe", lambda e, PO=PO, k=k, s=s, tl=tl, st_=st_, pg=pg: e.matmul(
                                    pg[:, :IW], XN1s(k, s)[:, tl * 128:(tl + 1) * 128], wgsl(st_, k),
                                    start=(k == 0), stop=(k == KC - 1)),
                                    reads=[bXN1[k], bwset[st_]], writes=[bpg])
                            for k in range(KC):
                                ctx.do("pe", lambda e, PO=PO, k=k, s=s, tl=tl, st_=st_, pu=pu: e.matmul(
                                    pu[:, :IW], XN1s(k, s)[:, tl * 128:(tl + 1) * 128], wusl(st_, k),
                                    start=(k == 0), stop=(k == KC - 1)),
                                    reads=[bXN1[k], bwset[st_]], writes=[bpu])
                            ctx.do("act", lambda e, PO=PO, pg=pg: e.activation(
                                out=SCt(0)[:, PO:PO + IW], in_=pg[:, :IW], func=AF.Silu),
                                reads=[bpg], writes=[bSCp[0]])
                            ctx.do("dve", lambda e, PO=PO, pu=pu: e.tensor_tensor(
                                out=SCt(1)[:, PO:PO + IW], in0=SCt(0)[:, PO:PO + IW], in1=pu[:, :IW], op=ALU.mult),
                                reads=[bSCp[0], bpu], writes=[bSCp[1]])
                            ctx.do("act", lambda e, PO=PO: e.activation(
                                out=SCt(2)[:, PO:PO + IW], in_=SCt(1)[:, PO:PO + IW], func=AF.Abs),
                                reads=[bSCp[1]], writes=[bSCp[2]])
                            NG = IW // 4
                            def gview(tile, j, PO=0):
                                return tile[:, PO:PO + IW].rearrange("p (gg f) -> p gg f", f=4)[:, :, j]
                            for (dst, i0, i1, op) in [(0, 0, 1, ALU.max), (1, 0, 1, ALU.min),
                                                      (2, 2, 3, ALU.max), (3, 2, 3, ALU.min)]:
                                ctx.do("dve", lambda e, PO=PO, dst=dst, i0=i0, i1=i1, op=op: e.tensor_tensor(
                                    out=SCt(3)[:, PO + dst * NG:PO + (dst + 1) * NG],
                                    in0=gview(SCt(2), i0, PO), in1=gview(SCt(2), i1, PO), op=op),
                                    reads=[bSCp[2]], writes=[bSCp[3]])
                            ctx.do("dve", lambda e, PO=PO: e.tensor_tensor(
                                out=SCt(0)[:, PO + 0:PO + NG], in0=SCt(3)[:, PO + 0:PO + NG], in1=SCt(3)[:, PO + 2 * NG:PO + 3 * NG], op=ALU.min),
                                reads=[bSCp[3]], writes=[bSCp[0]])
                            ctx.do("dve", lambda e, PO=PO: e.tensor_tensor(
                                out=SCt(0)[:, PO + NG:PO + 2 * NG], in0=SCt(3)[:, PO + NG:PO + 2 * NG], in1=SCt(3)[:, PO + 3 * NG:PO + 4 * NG], op=ALU.max),
                                reads=[bSCp[3]], writes=[bSCp[0]])
                            ctx.do("dve", lambda e, PO=PO: e.tensor_tensor(
                                out=SCt(0)[:, PO + 2 * NG:PO + 3 * NG], in0=SCt(0)[:, PO + 0:PO + NG], in1=SCt(0)[:, PO + NG:PO + 2 * NG], op=ALU.max),
                                reads=[bSCp[0]], writes=[bSCp[0]])
                            for j4 in range(4):
                                ctx.do("dve", lambda e, PO=PO, j4=j4: e.tensor_tensor(
                                    out=SCt(3)[:, PO + j4 * NG:PO + (j4 + 1) * NG],
                                    in0=gview(SCt(2), j4, PO), in1=SCt(0)[:, PO + 2 * NG:PO + 3 * NG], op=ALU.is_ge),
                                    reads=[bSCp[2], bSCp[0]], writes=[bSCp[3]])
                                ctx.do("dve", lambda e, PO=PO, j4=j4, tsub=tsub: e.tensor_tensor(
                                    out=gview(htm(tsub), j4), in0=gview(SCt(1), j4, PO),
                                    in1=SCt(3)[:, PO + j4 * NG:PO + (j4 + 1) * NG], op=ALU.mult),
                                    reads=[bSCp[1], bSCp[3]], writes=[bhtm[tsub]])
                            for ic2 in range(2):
                                ctx.do("pe", lambda e, tsub=tsub, ic2=ic2: e.transpose(
                                    psT[:, 0:128], htm(tsub)[:, ic2 * 128:(ic2 + 1) * 128], ident_s[:, :]),
                                    reads=[bhtm[tsub], bsm], writes=[bpsT])
                                ic = isl8 * 2 + ic2
                                ctx.do("act", lambda e, ic=ic, s=s, tl=tl: e.activation(
                                    out=hfm(ic)[:, s * SL + tl * 128: s * SL + (tl + 1) * 128],
                                    in_=psT[:, 0:128], func=AF.Copy),
                                    reads=[bpsT], writes=[bhfm[ic]])
                    for q4 in range(8):
                        for hb in range(2):
                            ctx.dma("gp", lambda e, ig=ig, q4=q4, hb=hb: e.dma_start(
                                out=wd_t(hb),
                                in_=g["wdp"][:, (ig * 8 + q4) * 4096 + hb * 2048:
                                             (ig * 8 + q4) * 4096 + (hb + 1) * 2048]),
                                gw6d[hb], writes=[bwd[hb]])
                        for ic in range(KC):
                            hb = ic // 8
                            for dcl in range(2):
                                for s in range(NSLAB):
                                    ps, bps = (acc[dcl * 2 + s], bacc[dcl * 2 + s])
                                    ctx.do("pe", lambda e, ic=ic, dcl=dcl, s=s, hb=hb, ps=ps: e.matmul(
                                        ps[:, :], wd_t(hb)[:, (ic % 8) * 256 + dcl * 128:
                                                           (ic % 8) * 256 + (dcl + 1) * 128],
                                        hfm(ic)[:, s * SL:(s + 1) * SL],
                                        start=(ic == 0), stop=(ic == KC - 1)),
                                        reads=[bwd[hb], bhfm[ic]], writes=[bps])
                        for dcl in range(2):
                            for s in range(NSLAB):
                                ps, bps = (acc[dcl * 2 + s], bacc[dcl * 2 + s])
                                kd = q4 * 2 + dcl
                                ctx.do("dve", lambda e, kd=kd, s=s, ps=ps: e.tensor_tensor(
                                    out=Xs(kd, s), in0=Xs(kd, s), in1=ps[:, :], op=ALU.add),
                                    reads=[bps], writes=[bX[kd]])
            ctx.fence()

        phase6()
        phase5(1)
        dbg_snap("X5", Xs, bX)
        if "attnfm" in dbg_outs:
            battn2 = ctx.buf("attn2")
            for k in range(KC):
                ctx.dma("sp", lambda e, k=k: e.dma_start(
                    out=dbg_outs["attnfm"][k * 128:(k + 1) * 128, :], in_=BA[:, k * TPC:(k + 1) * TPC]),
                    grp_out, reads=[battn2], writes=[bdbg])

        for k in range(KC):
            ctx.dma("sp", lambda e, k=k: e.dma_start(out=y[k * 128:(k + 1) * 128, :], in_=Xs(k)),
                    grp_out, reads=[bX[k]], writes=[bdbg])
        final_out = ctx.semcount["g_out"]
        ctx.raw("sp", lambda e, v=final_out: e.wait_ge(ctx.semh["g_out"], v))

        block = es.enter_context(nc.Block())
        ctx.finalize(block)
    return nc



def kernel(**inputs):
    shared, per_core = prep_inputs(inputs)
    nc = build_program()
    in_maps = []
    for c in range(8):
        m = dict(shared)
        m.update(per_core[c])
        in_maps.append(m)
    res = run_bass_kernel_spmd(nc, in_maps, core_ids=list(range(8)))
    out = np.zeros((2, 4096, D), dtype=np.float32)
    for c in range(8):
        yc = np.asarray(res.results[c]["y_fm"])
        out[0, SL * c:SL * (c + 1), :] = yc[:, 0:SL].T
        out[1, SL * (7 - c):SL * (8 - c), :] = yc[:, SL:2 * SL].T
    return out


# version 08 (final)
# speedup vs baseline: 1.1904x; 1.1904x over previous
import contextlib
import numpy as np
import ml_dtypes
import concourse.bass as bass
import concourse.mybir as mybir
from concourse.bass_utils import run_bass_kernel_spmd

F32 = mybir.dt.float32
BF16 = mybir.dt.bfloat16
I32 = mybir.dt.int32
AF = mybir.ActivationFunctionType
ALU = mybir.AluOpType
AX = mybir.AxisListType

D = 2048
KC = 16
TPC = 1024
NSLAB = 2
SL = 512
H_FFN = 8192
NH = 16
EH = 128
EPS = 1e-5


class Buf:
    __slots__ = ("name", "w", "r")
    def __init__(self, name):
        self.name = name
        self.w = None
        self.r = []

class DmaGroup:
    __slots__ = ("sem",)
    def __init__(self, sem):
        self.sem = sem

class Queue:
    def __init__(self, name, get_eng, csem):
        self.name = name
        self.get_eng = get_eng
        self.csem = csem
        self.waited = {}
        self.ops = []

class Ctx:
    def __init__(self, nc):
        self.nc = nc
        self.semh = {}
        self.semcount = {}
        self.queues = {}
        self.bufs = []
        self._es = None
        self._sw = None

    def _sem(self, name):
        if name not in self.semh:
            self.semh[name] = self._es.enter_context(self.nc.semaphore(name))
            self.semcount[name] = 0
        return name

    def add_queue(self, name, eng_attr, csem=None):
        if csem:
            self._sem(csem)
        self.queues[name] = Queue(name, eng_attr, csem)

    def dma_group(self, name):
        return DmaGroup(self._sem(name))

    def buf(self, name):
        b = Buf(name)
        self.bufs.append(b)
        return b

    def _resolve(self, tok):
        if tok is None:
            return None
        if tok[0] == "D":
            _, sem, _v = tok
            return (sem, self.semcount[sem])
        return (tok[1], tok[2])

    def _collect(self, reads, writes, extra):
        waits = {}
        def addw(tok):
            rv = self._resolve(tok)
            if rv is None:
                return
            s, v = rv
            if waits.get(s, 0) < v:
                waits[s] = v
        for b in reads:
            addw(b.w)
        for b in writes:
            addw(b.w)
            for t in b.r:
                addw(t)
        for t in extra:
            addw(t)
        return waits

    def _emit(self, qname, fn, waits, sem, amt, reads, writes):
        q = self.queues[qname]
        final = []
        for s, v in waits.items():
            if qname == "pe" and s == "pe":
                continue
            if q.waited.get(s, -1) >= v:
                continue
            final.append((s, v))
            q.waited[s] = v
        if sem is not None:
            self.semcount[sem] += amt
            tok_kind = "D" if amt >= 16 else "C"
            tok = (tok_kind, sem, self.semcount[sem])
            q.ops.append(("op", final, fn, (sem, amt)))
        else:
            tok = None
            q.ops.append(("op", final, fn, None))
        for b in reads:
            if tok is not None:
                b.r.append(tok)
        for b in writes:
            b.w = tok
            b.r = []
        return tok

    def do(self, qname, fn, reads=(), writes=(), extra=()):
        q = self.queues[qname]
        waits = self._collect(reads, writes, extra)
        return self._emit(qname, fn, waits, q.csem, 1, reads, writes)

    def dma(self, qname, fn, grp, reads=(), writes=(), extra=()):
        waits = self._collect(reads, writes, extra)
        return self._emit(qname, fn, waits, grp.sem, 16, reads, writes)

    def cc(self, fn, reads=(), writes=(), extra=()):
        self._sem("cc")
        waits = self._collect(reads, writes, extra)
        return self._emit("gp", fn, waits, "cc", 1, reads, writes)

    def raw(self, qname, fn):
        self.queues[qname].ops.append(("raw", fn))

    def fence(self, exclude=()):
        snap = {s: v for s, v in self.semcount.items() if v > 0 and s not in exclude}
        for q in self.queues.values():
            final = []
            for s, v in snap.items():
                if q.waited.get(s, -1) < v:
                    final.append((s, v))
                    q.waited[s] = v
            if final:
                q.ops.append(("op", final, None, None))
        for b in self.bufs:
            b.w = None
            b.r = []

    def begin_switch(self, qnames):
        self._sw = {
            "queues": list(qnames),
            "count0": dict(self.semcount),
            "waited0": {qn: dict(self.queues[qn].waited) for qn in qnames},
            "ops0": {qn: self.queues[qn].ops for qn in qnames},
            "buf0": [(b, b.w, list(b.r)) for b in self.bufs],
            "bodies": {qn: [] for qn in qnames},
            "counts": [],
        }
        for qn in qnames:
            self.queues[qn].ops = []

    def begin_body(self):
        sw = self._sw
        self.semcount = dict(sw["count0"])
        for qn in sw["queues"]:
            q = self.queues[qn]
            q.count = None
            q.waited = dict(sw["waited0"][qn])
            q.ops = []
        for b, w, r in sw["buf0"]:
            b.w = w
            b.r = list(r)

    def end_body(self):
        sw = self._sw
        for qn in sw["queues"]:
            sw["bodies"][qn].append(self.queues[qn].ops)
        sw["counts"].append(dict(self.semcount))

    def end_switch(self, rank_ap):
        sw = self._sw
        maxima = dict(sw["count0"])
        for counts in sw["counts"]:
            for s, v in counts.items():
                maxima[s] = max(maxima.get(s, 0), v)
        for qn in sw["queues"]:
            q = self.queues[qn]
            bodies = []
            for i in range(8):
                ops = sw["bodies"][qn][i]
                incs = {}
                for item in ops:
                    if item[0] == "op" and item[3] is not None:
                        s, a = item[3]
                        incs[s] = incs.get(s, 0) + a
                pads = []
                for s, a in incs.items():
                    target = maxima[s] - sw["count0"].get(s, 0)
                    if target > a:
                        pads.append((s, target - a))
                body = list(ops)
                if pads:
                    body.append(("pad", pads))
                bodies.append((body, incs))
            for i, (body, incs) in enumerate(bodies):
                all_incs = set()
                for _b2, inc2 in bodies:
                    all_incs.update(inc2)
                missing = []
                for s in all_incs:
                    if s not in incs:
                        target = maxima[s] - sw["count0"].get(s, 0)
                        if target > 0:
                            missing.append((s, target))
                if missing:
                    body.append(("pad", missing))
            q.ops = sw["ops0"][qn]
            q.ops.append(("switch", rank_ap, [b for b, _ in bodies]))
            q.waited = {}
        self.semcount = maxima
        self._sw = None
        for b in self.bufs:
            b.w = None
            b.r = []
        self.fence()

    def finalize(self, block):
        semh = self.semh
        def run_ops(eng, ops):
            for item in ops:
                if item[0] == "op":
                    _, waits, fn, inc = item
                    for s, v in waits:
                        eng.wait_ge(semh[s], v)
                    if fn is not None:
                        instr = fn(eng)
                        if inc is not None:
                            instr.then_inc(semh[inc[0]], inc[1])
                elif item[0] == "raw":
                    item[1](eng)
                elif item[0] == "pad":
                    for s, amt in item[1]:
                        eng.sem_inc(semh[s], amt)
                elif item[0] == "switch":
                    _, rank_ap, bodies = item
                    with eng.register("rk") as rk:
                        eng.reg_load(rk, rank_ap)
                        idx = eng.snap(rk)
                        for i in self.nc.Switch(engines=[eng], index=[idx], n=8):
                            run_ops(eng, bodies[i])
        for q in self.queues.values():
            ops = q.ops
            getattr(block, q.get_eng)(lambda eng, ops=ops: run_ops(eng, ops))



def _hadamard(n):
    h = np.array([[1.0]], dtype=np.float64)
    while h.shape[0] < n:
        h = np.block([[h, h], [h, -h]])
    return h

def prep_inputs(I):
    bf = ml_dtypes.bfloat16
    f32 = np.float32
    x = np.asarray(I["L0_attn_x"], f32)

    shared = {}
    C = np.ascontiguousarray
    shared["w_in"] = C(np.asarray(I["L0_attn_W_in"], f32))
    shared["w_out_aug"] = C(np.concatenate([np.asarray(I["L0_attn_W_out"], f32),
                                            np.asarray(I["L0_attn_b_out"], f32)[None, :]], 0))
    shared["w_down0"] = C(np.asarray(I["L0_attn_W_down"], f32))
    shared["w_up_aug"] = C(np.concatenate([np.asarray(I["L0_attn_W_up"], f32),
                                           np.asarray(I["L0_attn_b_up"], f32)[None, :]], 0))
    w1t = np.asarray(I["L0_ffn_W1"], f32).T
    w3t = np.asarray(I["L0_ffn_W3"], f32).T
    w2t = np.asarray(I["L0_ffn_W2"], f32).T
    wqkvt = np.asarray(I["L1_attn_W_qkv"], f32).T
    woutt = np.asarray(I["L1_attn_W_out"], f32).T
    wg = np.asarray(I["L1_ffn_w_gate"], f32)
    wu = np.asarray(I["L1_ffn_w_up"], f32)
    wd = np.asarray(I["L1_ffn_w_down"], f32)
    shared["w1tp"] = C(w1t.reshape(16, 128, 64, 128).transpose(1, 2, 0, 3).reshape(128, 131072))
    shared["w3tp"] = C(w3t.reshape(16, 128, 64, 128).transpose(1, 2, 0, 3).reshape(128, 131072))
    shared["w2tp"] = C(w2t.reshape(64, 128, 4, 512).transpose(1, 2, 0, 3).reshape(128, 131072))
    shared["wqkvtp"] = C(wqkvt.reshape(16, 128, 48, 128).transpose(1, 2, 0, 3).reshape(128, 98304))
    shared["wouttp"] = C(woutt.reshape(16, 128, 8, 256).transpose(1, 2, 0, 3).reshape(128, 32768))
    shared["wgp"] = C(wg.reshape(16, 128, 32, 256).transpose(1, 2, 0, 3).reshape(128, 131072))
    shared["wup"] = C(wu.reshape(16, 128, 32, 256).transpose(1, 2, 0, 3).reshape(128, 131072))
    shared["wdp"] = C(wd.reshape(4, 16, 128, 8, 256).transpose(2, 0, 3, 1, 4).reshape(128, 131072))

    def colmajor(v, nch):
        return C(np.asarray(v, f32).reshape(nch, 128).T)
    shared["gamma_c"] = colmajor(I["L0_attn_ln_gamma"], 16)
    shared["beta_c"] = colmajor(I["L0_attn_ln_beta"], 16)
    shared["dskip_c"] = colmajor(I["L0_attn_D_skip"], 16)
    shared["bout1_c"] = colmajor(I["L1_attn_b_out"], 16)
    shared["bqkv_c"] = colmajor(I["L1_attn_b_qkv"], 48)
    shared["bqkv_row"] = C(np.asarray(I["L1_attn_b_qkv"], f32)[None, :])
    shared["b_in"] = C(np.asarray(I["L0_attn_b_in"], f32)[:, None])
    shared["b_down"] = C(np.asarray(I["L0_attn_b_down"], f32)[:, None])
    shared["decay_logit"] = C(np.asarray(I["L0_attn_decay_logit"], f32)[:, None])

    h128 = (_hadamard(128) * (2.0 ** -7)).astype(bf)
    shared["h128"] = h128
    shared["ident_bf"] = np.eye(128, dtype=bf)
    dm = np.zeros((4, 128, SL), dtype=bf)
    for j in range(4):
        ki = np.arange(128)[:, None]
        qi = np.arange(SL)[None, :]
        dm[j] = ((128 * j + ki) <= qi).astype(bf)
    shared["dmask"] = C(dm.reshape(4 * 128, SL))

    per_core = []
    for c in range(8):
        pc = {}
        j0, j1 = c, 7 - c
        s0 = x[0, SL * j0:SL * (j0 + 1), :]
        s1 = x[1, SL * j1:SL * (j1 + 1), :]
        pc["x_fm"] = C(np.concatenate([s0.T, s1.T], axis=1))
        pc["rank_in"] = np.array([[c]], dtype=np.int32)
        e = np.zeros((2, 8, 16), f32)
        m = np.zeros((2, 8, 16), f32)
        for s, (b, j) in enumerate([(0, j0), (1, j1)]):
            for owner in range(8):
                for sown in range(2):
                    ag = owner * 2 + sown
                    bb = 0 if sown == 0 else 1
                    jj = owner if sown == 0 else 7 - owner
                    if bb == b and jj < j:
                        m[s, :, ag] = 1.0
                        e[s, :, ag] = 512.0 * (j - 1 - jj)
        pc["scan_e"] = C(np.concatenate([e[0], e[1]], axis=1))
        pc["scan_m"] = C(np.concatenate([m[0], m[1]], axis=1))
        per_core.append(pc)
    return shared, per_core



STOP_AFTER = None

def build_program(dbg=()):
    nc = bass.Bass("TRN2", target_bir_lowering=False)
    g = {}

    def din(name, shape, dt=F32):
        g[name] = nc.dram_tensor(name, shape, dt, kind="ExternalInput").ap()
        return g[name]

    din("x_fm", [D, TPC]); din("rank_in", [1, 1], I32)
    din("scan_e", [8, 32]); din("scan_m", [8, 32])
    din("w_in", [D, 8]); din("w_out_aug", [9, D]); din("w_down0", [D, 32]); din("w_up_aug", [33, D])
    din("w1tp", [128, 131072]); din("w3tp", [128, 131072]); din("w2tp", [128, 131072])
    din("wqkvtp", [128, 98304]); din("wouttp", [128, 32768])
    din("wgp", [128, 131072]); din("wup", [128, 131072]); din("wdp", [128, 131072])
    din("gamma_c", [128, 16]); din("beta_c", [128, 16]); din("dskip_c", [128, 16])
    din("bout1_c", [128, 16]); din("bqkv_c", [128, 48]); din("bqkv_row", [1, 3 * D])
    din("b_in", [8, 1]); din("b_down", [32, 1]); din("decay_logit", [8, 1])
    din("h128", [128, 128], BF16); din("ident_bf", [128, 128], BF16); din("dmask", [512, SL], BF16)

    y = nc.dram_tensor("y_fm", [D, TPC], F32, kind="ExternalOutput").ap()

    sum_contrib = nc.dram_tensor("sum_contrib", [2, 8], F32).ap()
    x_spill = nc.dram_tensor("x_spill", [128, KC * TPC], F32).ap()
    ffn0_spill = nc.dram_tensor("ffn0_spill", [128, KC * TPC], F32).ap()
    sum_ag = nc.dram_tensor("sum_ag", [16, 8], F32).ap()
    k_contrib = nc.dram_tensor("k_contrib", [2 * NH * EH, SL], BF16).ap()
    v_contrib = nc.dram_tensor("v_contrib", [2 * NH * 128, SL], BF16).ap()
    k_ag = nc.dram_tensor("k_ag", [8 * 2 * NH * EH, SL], BF16, addr_space="Shared").ap()
    v_ag = nc.dram_tensor("v_ag", [8 * 2 * NH * 128, SL], BF16, addr_space="Shared").ap()

    dbg_outs = {}
    for name, shape, dt in dbg:
        dbg_outs[name] = nc.dram_tensor("dbg_" + name, shape, dt, kind="ExternalOutput").ap()

    ctx = Ctx(nc)
    with contextlib.ExitStack() as es:
        ctx._es = es
        ctx.add_queue("pe", "tensor", "pe")
        ctx.add_queue("dve", "vector", "ve")
        ctx.add_queue("act", "scalar", "ac")
        ctx.add_queue("sp", "sync")
        ctx.add_queue("gp", "gpsimd", "gc")
        grp_init = ctx.dma_group("g_init")
        grp_gpi = ctx.dma_group("g_gpi")
        grp_out = ctx.dma_group("g_out")

        def sb(name, shape, dt):
            h = es.enter_context(nc.sbuf_tensor(name, shape, dt))
            return h

        X = sb("X", [128, KC * TPC], F32)
        XN1 = sb("XN1", [128, KC * TPC], BF16)
        BA = sb("BA", [128, 16384], BF16)
        BB = sb("BB", [128, 16384], BF16)
        SC = sb("SC", [128, 4 * SL], F32)
        gamma = sb("gamma", [128, 16], F32); beta = sb("beta", [128, 16], F32)
        dskip = sb("dskip", [128, 16], F32); bout1 = sb("bout1", [128, 16], F32)
        bqkvc = sb("bqkvc", [128, 48], F32)
        w_in_b = sb("w_in_b", [128, KC * 8], BF16)
        w_out_b = sb("w_out_b", [9, D], BF16)
        w_dn_b = sb("w_dn_b", [128, KC * 32], BF16)
        w_up_b = sb("w_up_b", [33, D], BF16)
        h128_s = sb("h128_s", [128, 128], BF16)
        ident_s = sb("ident_s", [128, 128], BF16)
        dmask_s = sb("dmask_s", [128, 4 * SL], BF16)
        ones_cf = sb("ones_cf", [128, 1], F32)
        ones_rf = sb("ones_rf", [1, 128], F32)
        ones_cb = sb("ones_cb", [128, 1], BF16)
        ones_rb = sb("ones_rb", [1, 128], BF16)
        eps_t = sb("eps_t", [1, 1], F32)
        b_in_s = sb("b_in_s", [8, 1], F32)
        b_dn_s = sb("b_dn_s", [32, 1], F32)
        dlg_s = sb("dlg_s", [8, 1], F32)
        mu_r = sb("mu_r", [1, SL], F32); rstd_r = sb("rstd_r", [1, SL], F32)
        row_t = sb("row_t", [1, SL], F32)
        et_s = sb("et_s", [8, 32], F32); mt_s = sb("mt_s", [8, 32], F32)

        bX = [ctx.buf(f"X{k}") for k in range(KC)]
        bXN1 = [ctx.buf(f"XN1_{k}") for k in range(KC)]
        bBA = ctx.buf("BA"); bBB = ctx.buf("BB")
        bSC = [ctx.buf(f"SC{i}") for i in range(4)]
        bsm = ctx.buf("smalls")
        bmu = ctx.buf("mu"); brstd = ctx.buf("rstd"); brow = ctx.buf("row")
        bdram = ctx.buf("dram_misc")

        def Xs(k, s=None):
            if s is None:
                return X[:, k * TPC:(k + 1) * TPC]
            return X[:, k * TPC + s * SL: k * TPC + (s + 1) * SL]
        def XN1s(k, s=None):
            if s is None:
                return XN1[:, k * TPC:(k + 1) * TPC]
            return XN1[:, k * TPC + s * SL: k * TPC + (s + 1) * SL]
        def SCt(i):
            return SC[:, i * SL:(i + 1) * SL]

        for k in range(KC):
            ctx.dma("sp", lambda e, k=k: e.dma_start(out=Xs(k), in_=g["x_fm"][k * 128:(k + 1) * 128, :]),
                    grp_init, writes=[bX[k]])
        for nm, tile in [("gamma_c", gamma), ("beta_c", beta), ("dskip_c", dskip),
                         ("bout1_c", bout1), ("bqkv_c", bqkvc), ("b_in", b_in_s),
                         ("b_down", b_dn_s), ("decay_logit", dlg_s),
                         ("scan_e", et_s), ("scan_m", mt_s),
                         ("h128", h128_s), ("ident_bf", ident_s)]:
            ctx.dma("sp", lambda e, nm=nm, tile=tile: e.dma_start(out=tile[:, :], in_=g[nm][:, :]),
                    grp_init, writes=[bsm])
        for j in range(4):
            ctx.dma("sp", lambda e, j=j: e.dma_start(
                out=dmask_s[:, j * SL:(j + 1) * SL], in_=g["dmask"][j * 128:(j + 1) * 128, :]),
                grp_init, writes=[bsm])
        ctx.dma("gp", lambda e: e.dma_start(out=w_in_b[:, :].rearrange("p (kc r) -> p kc r", r=8),
                                            in_=g["w_in"].rearrange("(kc p) r -> p kc r", p=128)),
                grp_gpi, writes=[bsm])
        ctx.dma("gp", lambda e: e.dma_start(out=w_out_b[:, :], in_=g["w_out_aug"][:, :]), grp_gpi, writes=[bsm])
        ctx.dma("gp", lambda e: e.dma_start(out=w_dn_b[:, :].rearrange("p (kc r) -> p kc r", r=32),
                                            in_=g["w_down0"].rearrange("(kc p) r -> p kc r", p=128)),
                grp_gpi, writes=[bsm])
        ctx.dma("gp", lambda e: e.dma_start(out=w_up_b[:, :], in_=g["w_up_aug"][:, :]), grp_gpi, writes=[bsm])
        for tile, val in [(ones_cf, 1.0), (ones_rf, 1.0), (ones_cb, 1.0), (ones_rb, 1.0), (eps_t, EPS)]:
            ctx.do("gp", lambda e, tile=tile, val=val: e.memset(tile[:, :], val), writes=[bsm])

        bdbg = ctx.buf("dbg")
        def dbg_snap(name, src_s, bufs, nchunks=KC):
            if name not in dbg_outs:
                return
            for k in range(nchunks):
                ctx.dma("sp", lambda e, k=k, name=name: e.dma_start(
                    out=dbg_outs[name][k * 128:(k + 1) * 128, :], in_=src_s(k)),
                    grp_out, reads=[bufs[k]], writes=[bdbg])

        def ln_stats(src_s, src_dtype, psA, psB, psG, bsrc, bpsA, bpsB, bpsG, s):
            ones = ones_cf if src_dtype == F32 else ones_cb
            for k in range(KC):
                ctx.do("act", lambda e, k=k: e.activation(out=SCt(0)[:, :], in_=src_s(k, s), func=AF.Square),
                       reads=[bsrc[k]], writes=[bSC[0]])
                ctx.do("pe", lambda e, k=k: e.matmul(psG[0:1, 0:SL], ones[:, :], src_s(k, s),
                                                     start=(k == 0), stop=(k == KC - 1)),
                       reads=[bsrc[k], bsm], writes=[bpsG])
                ctx.do("pe", lambda e, k=k: e.matmul(psG[0:1, SL:2 * SL], ones_cf[:, :], SCt(0)[:, :],
                                                     start=(k == 0), stop=(k == KC - 1)),
                       reads=[bSC[0], bsm], writes=[bpsG])
            ctx.do("dve", lambda e: e.tensor_scalar(out=mu_r[:, :], in0=psG[0:1, 0:SL], scalar1=1.0 / D,
                                                    scalar2=None, op0=ALU.mult),
                   reads=[bpsG], writes=[bmu])
            ctx.do("dve", lambda e: e.tensor_scalar(out=row_t[:, :], in0=psG[0:1, SL:2 * SL], scalar1=1.0 / D,
                                                    scalar2=None, op0=ALU.mult),
                   reads=[bpsG], writes=[brow])
            ctx.do("dve", lambda e: e.tensor_tensor(out=rstd_r[:, :], in0=mu_r[:, :], in1=mu_r[:, :], op=ALU.mult),
                   reads=[bmu], writes=[brstd])
            ctx.do("dve", lambda e: e.tensor_tensor(out=rstd_r[:, :], in0=row_t[:, :], in1=rstd_r[:, :], op=ALU.subtract),
                   reads=[brow, brstd], writes=[brstd])
            ctx.do("act", lambda e: e.activation(out=rstd_r[:, :], in_=rstd_r[:, :], func=AF.Ln, bias=eps_t[:, :]),
                   reads=[brstd, bsm], writes=[brstd])
            ctx.do("act", lambda e: e.activation(out=rstd_r[:, :], in_=rstd_r[:, :], func=AF.Exp, scale=-0.5),
                   reads=[brstd], writes=[brstd])
            ctx.do("pe", lambda e: e.matmul(psA[:, :], ones_rf[:, :], mu_r[:, :], start=True, stop=True),
                   reads=[bmu, bsm], writes=[bpsA])
            ctx.do("pe", lambda e: e.matmul(psB[:, :], ones_rf[:, :], rstd_r[:, :], start=True, stop=True),
                   reads=[brstd, bsm], writes=[bpsB])

        _lnc = [0]
        def ln_plain():
            _lnc[0] += 1
            with nc.psum_tensor(f"lnA{_lnc[0]}", [128, SL], F32) as psA, \
                 nc.psum_tensor(f"lnB{_lnc[0]}", [128, SL], F32) as psB, \
                 nc.psum_tensor(f"lnG{_lnc[0]}", [1, 2 * SL], F32) as psG:
                bpsA = ctx.buf("lnA"); bpsB = ctx.buf("lnB"); bpsG = ctx.buf("lnG")
                for s in range(NSLAB):
                    ln_stats(Xs, F32, psA, psB, psG, bX, bpsA, bpsB, bpsG, s)
                    for k in range(KC):
                        ctx.do("dve", lambda e, k=k, s=s: e.tensor_tensor(
                            out=SCt(1)[:, :], in0=Xs(k, s), in1=psA[:, :], op=ALU.subtract),
                            reads=[bX[k], bpsA], writes=[bSC[1]])
                        ctx.do("dve", lambda e, k=k, s=s: e.tensor_tensor(
                            out=XN1s(k, s), in0=SCt(1)[:, :], in1=psB[:, :], op=ALU.mult),
                            reads=[bSC[1], bpsB], writes=[bXN1[k]])
            ctx.fence()

        ln_plain()
        dbg_snap("xn1", XN1s, bXN1)

        def ln_affine(src_s, bsrc, src_dt, dst_s, bdst, psA, psB, psG, bpsA, bpsB, bpsG):
            for s in range(NSLAB):
                ln_stats(src_s, src_dt, psA, psB, psG, bsrc, bpsA, bpsB, bpsG, s)
                for k in range(KC):
                    ctx.do("dve", lambda e, k=k, s=s: e.tensor_tensor(
                        out=SCt(1)[:, :], in0=src_s(k, s), in1=psA[:, :], op=ALU.subtract),
                        reads=[bsrc[k], bpsA], writes=[bSC[1]])
                    ctx.do("dve", lambda e: e.tensor_tensor(
                        out=SCt(1)[:, :], in0=SCt(1)[:, :], in1=psB[:, :], op=ALU.mult),
                        reads=[bSC[1], bpsB], writes=[bSC[1]])
                    ctx.do("dve", lambda e, k=k, s=s: e.tensor_scalar(
                        out=dst_s(k, s), in0=SCt(1)[:, :], scalar1=gamma[:, k:k + 1],
                        scalar2=beta[:, k:k + 1], op0=ALU.mult, op1=ALU.add),
                        reads=[bSC[1], bsm], writes=[bdst[k]])

        def phase2():
            with contextlib.ExitStack() as p2s:
                psA = p2s.enter_context(nc.psum_tensor("p2A", [128, SL], F32))
                psB = p2s.enter_context(nc.psum_tensor("p2B", [128, SL], F32))
                psG = p2s.enter_context(nc.psum_tensor("p2G", [1, 2 * SL], F32))
                psC = p2s.enter_context(nc.psum_tensor("p2C", [128, SL], F32))
                psD = p2s.enter_context(nc.psum_tensor("p2D", [128, SL], F32))
                u_s = p2s.enter_context(nc.sbuf_tensor("u_s", [8, TPC], F32))
                h_s = p2s.enter_context(nc.sbuf_tensor("h_s", [8, TPC], F32))
                dbc = p2s.enter_context(nc.sbuf_tensor("dbc", [8, SL], F32))
                ppow = p2s.enter_context(nc.sbuf_tensor("ppow", [8, SL], F32))
                dec_s = p2s.enter_context(nc.sbuf_tensor("dec_s", [8, 1], F32))
                lnd_s = p2s.enter_context(nc.sbuf_tensor("lnd_s", [8, 1], F32))
                carry = p2s.enter_context(nc.sbuf_tensor("carry", [8, 2], F32))
                pw = p2s.enter_context(nc.sbuf_tensor("pw", [8, 16], F32))
                st_t = p2s.enter_context(nc.sbuf_tensor("st_t", [8, 16], F32))
                h_aug = p2s.enter_context(nc.sbuf_tensor("h_aug", [9, TPC], BF16))
                hf_aug = p2s.enter_context(nc.sbuf_tensor("hf_aug", [33, TPC], BF16))
                bpsA = ctx.buf("p2A"); bpsB = ctx.buf("p2B"); bpsG = ctx.buf("p2G")
                bpsC = ctx.buf("p2C"); bpsD = ctx.buf("p2D")
                bu = ctx.buf("u"); bh = ctx.buf("h"); bdbc = ctx.buf("dbc")
                bppow = ctx.buf("ppow"); bdec = ctx.buf("dec"); bcar = ctx.buf("carry")
                bpw = ctx.buf("pw"); bstt = ctx.buf("st_t"); bhaug = ctx.buf("h_aug")
                bhf = ctx.buf("hf_aug")
                bBAc = [ctx.buf(f"BAc{k}") for k in range(KC)]
                bBBc = [ctx.buf(f"BBc{k}") for k in range(KC)]
                grp_ctr = ctx.dma_group("g_ctr")

                def BAs(k, s):
                    return BA[:, k * TPC + s * SL: k * TPC + (s + 1) * SL]
                def BBs(k, s):
                    return BB[:, k * TPC + s * SL: k * TPC + (s + 1) * SL]

                ln_affine(XN1s, bXN1, BF16, BAs, bBAc, psA, psB, psG, bpsA, bpsB, bpsG)

                for s in range(NSLAB):
                    for k in range(KC):
                        ctx.do("pe", lambda e, k=k, s=s: e.matmul(
                            psC[0:8, :], w_in_b[:, k * 8:(k + 1) * 8], BAs(k, s),
                            start=(k == 0), stop=(k == KC - 1)),
                            reads=[bBAc[k], bsm], writes=[bpsC])
                    ctx.do("dve", lambda e, s=s: e.tensor_scalar(
                        out=u_s[0:8, s * SL:(s + 1) * SL], in0=psC[0:8, :], scalar1=b_in_s[:, :],
                        scalar2=None, op0=ALU.add),
                        reads=[bpsC, bsm], writes=[bu])

                ctx.do("act", lambda e: e.activation(out=dec_s[:, :], in_=dlg_s[:, :], func=AF.Sigmoid),
                       reads=[bsm], writes=[bdec])
                ctx.do("act", lambda e: e.activation(out=lnd_s[:, :], in_=dec_s[:, :], func=AF.Ln),
                       reads=[bdec], writes=[bdec])
                ctx.do("gp", lambda e: e.iota(ppow[0:8, :], pattern=[[1, SL]], base=1,
                                              channel_multiplier=0, allow_small_or_imprecise_dtypes=True),
                       writes=[bppow])
                ctx.do("dve", lambda e: e.tensor_scalar(
                    out=dbc[0:8, :], in0=ppow[0:8, :], scalar1=0.0, scalar2=dec_s[:, :],
                    op0=ALU.mult, op1=ALU.add),
                    reads=[bppow, bdec], writes=[bdbc])
                ctx.do("dve", lambda e: e.tensor_scalar(
                    out=ppow[0:8, :], in0=ppow[0:8, :], scalar1=lnd_s[:, :], scalar2=None, op0=ALU.mult),
                    reads=[bppow, bdec], writes=[bppow])
                ctx.do("act", lambda e: e.activation(out=ppow[0:8, :], in_=ppow[0:8, :], func=AF.Exp),
                       reads=[bppow], writes=[bppow])
                for s in range(NSLAB):
                    ctx.do("dve", lambda e, s=s: e.tensor_tensor_scan(
                        out=h_s[0:8, s * SL:(s + 1) * SL], data0=dbc[0:8, :],
                        data1=u_s[0:8, s * SL:(s + 1) * SL], initial=0.0, op0=ALU.mult, op1=ALU.add),
                        reads=[bdbc, bu], writes=[bh])
                for s in range(NSLAB):
                    ctx.dma("sp", lambda e, s=s: e.dma_start(
                        out=sum_contrib[s:s + 1, :], in_=h_s[0:8, (s + 1) * SL - 1:(s + 1) * SL]),
                        grp_ctr, reads=[bh], writes=[bdram])
                ctx.cc(lambda e: e.collective_compute(
                    "AllGather", ALU.bypass, replica_groups=[list(range(8))],
                    ins=[sum_contrib[:, :]], outs=[sum_ag[:, :]]),
                    reads=[bdram], writes=[bdram])
                ctx.dma("sp", lambda e: e.dma_start(
                    out=st_t[:, 0:16], in_=sum_ag.rearrange("ag r -> r ag"),
                    allow_slow_non_contiguous=True),
                    grp_ctr, reads=[bdram], writes=[bstt])
                for s in range(NSLAB):
                    ctx.do("dve", lambda e, s=s: e.tensor_scalar(
                        out=pw[:, :], in0=et_s[:, s * 16:(s + 1) * 16], scalar1=lnd_s[:, :],
                        scalar2=None, op0=ALU.mult),
                        reads=[bsm, bdec], writes=[bpw])
                    ctx.do("act", lambda e: e.activation(out=pw[:, :], in_=pw[:, :], func=AF.Exp),
                           reads=[bpw], writes=[bpw])
                    ctx.do("dve", lambda e, s=s: e.tensor_tensor(
                        out=pw[:, :], in0=pw[:, :], in1=mt_s[:, s * 16:(s + 1) * 16], op=ALU.mult),
                        reads=[bpw, bsm], writes=[bpw])
                    ctx.do("dve", lambda e: e.tensor_tensor(
                        out=pw[:, :], in0=pw[:, :], in1=st_t[:, :], op=ALU.mult),
                        reads=[bpw, bstt], writes=[bpw])
                    ctx.do("dve", lambda e, s=s: e.tensor_reduce(
                        out=carry[:, s:s + 1], in_=pw[:, :], axis=AX.X, op=ALU.add),
                        reads=[bpw], writes=[bcar])
                    ctx.do("dve", lambda e, s=s: e.scalar_tensor_tensor(
                        out=h_s[0:8, s * SL:(s + 1) * SL], in0=ppow[0:8, :], scalar=carry[:, s:s + 1],
                        in1=h_s[0:8, s * SL:(s + 1) * SL], op0=ALU.mult, op1=ALU.add),
                        reads=[bppow, bcar, bh], writes=[bh])
                ctx.do("gp", lambda e: e.memset(h_aug[0:9, :], 1.0), writes=[bhaug])
                ctx.do("act", lambda e: e.activation(out=h_aug[0:8, :], in_=h_s[0:8, :], func=AF.Copy),
                       reads=[bh], writes=[bhaug])

                for s in range(NSLAB):
                    for k in range(KC):
                        ctx.do("pe", lambda e, k=k, s=s: e.matmul(
                            psD[:, :], w_out_b[:, k * 128:(k + 1) * 128],
                            h_aug[:, s * SL:(s + 1) * SL], start=True, stop=True),
                            reads=[bhaug, bsm], writes=[bpsD])
                        ctx.do("dve", lambda e, k=k, s=s: e.scalar_tensor_tensor(
                            out=SCt(2)[:, :], in0=BAs(k, s), scalar=dskip[:, k:k + 1],
                            in1=psD[:, :], op0=ALU.mult, op1=ALU.add),
                            reads=[bBAc[k], bsm, bpsD], writes=[bSC[2]])
                        ctx.do("dve", lambda e, k=k, s=s: e.tensor_tensor(
                            out=SCt(3)[:, :], in0=SCt(2)[:, :], in1=XN1s(k, s), op=ALU.add),
                            reads=[bSC[2], bXN1[k]], writes=[bSC[3]])
                        ctx.do("dve", lambda e, k=k, s=s: e.tensor_tensor(
                            out=Xs(k, s), in0=Xs(k, s), in1=SCt(3)[:, :], op=ALU.add),
                            reads=[bSC[3]], writes=[bX[k]])
                        ctx.do("act", lambda e, k=k, s=s: e.activation(
                            out=BBs(k, s), in_=SCt(3)[:, :], func=AF.Copy),
                            reads=[bSC[3]], writes=[bBBc[k]])

                ln_affine(BBs, bBBc, BF16, BAs, bBAc, psA, psB, psG, bpsA, bpsB, bpsG)

                ctx.do("gp", lambda e: e.memset(hf_aug[0:33, :], 1.0), writes=[bhf])
                for s in range(NSLAB):
                    for k in range(KC):
                        ctx.do("pe", lambda e, k=k, s=s: e.matmul(
                            psC[0:32, :], w_dn_b[:, k * 32:(k + 1) * 32], BAs(k, s),
                            start=(k == 0), stop=(k == KC - 1)),
                            reads=[bBAc[k], bsm], writes=[bpsC])
                    ctx.do("act", lambda e, s=s: e.activation(
                        out=hf_aug[0:32, s * SL:(s + 1) * SL], in_=psC[0:32, :], func=AF.Relu,
                        bias=b_dn_s[:, :]),
                        reads=[bpsC, bsm], writes=[bhf])

                for s in range(NSLAB):
                    for k in range(KC):
                        ctx.do("pe", lambda e, k=k, s=s: e.matmul(
                            psD[:, :], w_up_b[:, k * 128:(k + 1) * 128],
                            hf_aug[:, s * SL:(s + 1) * SL], start=True, stop=True),
                            reads=[bhf, bsm], writes=[bpsD])
                        ctx.do("dve", lambda e, k=k, s=s: e.tensor_tensor(
                            out=Xs(k, s), in0=Xs(k, s), in1=psD[:, :], op=ALU.add),
                            reads=[bpsD], writes=[bX[k]])
            ctx.fence()

        phase2()
        dbg_snap("X2", Xs, bX)

        WS = es.enter_context(nc.sbuf_tensor("WS", [128, 8192], BF16))
        bWS = [ctx.buf(f"WS{i}") for i in range(4)]
        gws = [ctx.dma_group(f"g_ws{i}") for i in range(4)]
        def WSs(i):
            return WS[:, i * 2048:(i + 1) * 2048]

        SQRT2 = 1.4142135623730951

        SCB = SC.bitcast(BF16)
        XB = X.bitcast(BF16)

        def phase3():
            bhid = [ctx.buf(f"hid{j}") for j in range(132)]
            def slot_ap(i):
                if i < 32:
                    return BA[:, i * SL:(i + 1) * SL]
                if i < 64:
                    return BB[:, (i - 32) * SL:(i - 32 + 1) * SL]
                if i < 128:
                    return XB[:, (i - 64) * SL:(i - 64 + 1) * SL]
                return SCB[:, 2048 + (i - 128) * SL: 2048 + (i - 128 + 1) * SL]
            grp_sp3 = ctx.dma_group("g_sp3")
            with contextlib.ExitStack() as p3s:
                ps = [p3s.enter_context(nc.psum_tensor(f"p3_{i}", [128, SL], F32)) for i in range(8)]
                bps = [ctx.buf(f"p3_{i}") for i in range(8)]
                for k in range(KC):
                    ctx.dma("sp", lambda e, k=k: e.dma_start(
                        out=x_spill[:, k * TPC:(k + 1) * TPC], in_=Xs(k)),
                        grp_sp3, reads=[bX[k]], writes=[bdram])
                spill_tok = ("D", grp_sp3.sem, ctx.semcount[grp_sp3.sem])
                for j in range(64, 128):
                    bhid[j].w = spill_tok
                for i, sp in enumerate((128, 129, 130, 131)):
                    bhid[sp].w = bhid[sp].w or bSC[2 + i // 2].w
                    bhid[sp].r = bhid[sp].r + list(bSC[2 + i // 2].r)

                chunk_slot = [list(range(64)), list(range(64, 128))]
                spares = [[128, 129], [130, 131]]
                stage_done = [[[False] * 64 for _ in range(7)] for _ in range(2)]
                pend = [[], []]
                for slx in range(2):
                    for st in range(6):
                        dd = 1 << st
                        for base in range(0, 64, dd * 2):
                            for a in range(base, base + dd):
                                pend[slx].append((st, a, a + dd))

                def try_emit_stages(slx):
                    prog = True
                    while prog:
                        prog = False
                        for idx in range(len(pend[slx])):
                            st, a, b2 = pend[slx][idx]
                            if stage_done[slx][st][a] and stage_done[slx][st][b2]:
                                sA, sB = chunk_slot[slx][a], chunk_slot[slx][b2]
                                d1, d2 = spares[slx]
                                ctx.do("dve", lambda e, sA=sA, sB=sB, d1=d1: e.tensor_tensor(
                                    out=slot_ap(d1), in0=slot_ap(sA), in1=slot_ap(sB), op=ALU.add),
                                    reads=[bhid[sA], bhid[sB]], writes=[bhid[d1]])
                                ctx.do("dve", lambda e, sA=sA, sB=sB, d2=d2: e.tensor_tensor(
                                    out=slot_ap(d2), in0=slot_ap(sA), in1=slot_ap(sB), op=ALU.subtract),
                                    reads=[bhid[sA], bhid[sB]], writes=[bhid[d2]])
                                chunk_slot[slx][a] = d1
                                chunk_slot[slx][b2] = d2
                                spares[slx] = [sA, sB]
                                stage_done[slx][st + 1][a] = True
                                stage_done[slx][st + 1][b2] = True
                                pend[slx].pop(idx)
                                prog = True
                                break

                for j in range(64):
                    wb = 2 * (j % 2)
                    for nm, wsl in (("w1tp", wb), ("w3tp", wb + 1)):
                        ctx.dma("gp", lambda e, nm=nm, wsl=wsl, j=j: e.dma_start(
                            out=WSs(wsl), in_=g[nm][:, j * 2048:(j + 1) * 2048]),
                            gws[wsl], writes=[bWS[wsl]])
                    for slx in range(2):
                        pg, pu = ps[2 * slx], ps[2 * slx + 1]
                        bpg, bpu = bps[2 * slx], bps[2 * slx + 1]
                        for k in range(KC):
                            ctx.do("pe", lambda e, k=k, slx=slx, wb=wb, pg=pg: e.matmul(
                                pg[:, :], WSs(wb)[:, k * 128:(k + 1) * 128], XN1s(k, slx),
                                start=(k == 0), stop=(k == KC - 1)),
                                reads=[bWS[wb], bXN1[k]], writes=[bpg])
                        for k in range(KC):
                            ctx.do("pe", lambda e, k=k, slx=slx, wb=wb, pu=pu: e.matmul(
                                pu[:, :], WSs(wb + 1)[:, k * 128:(k + 1) * 128], XN1s(k, slx),
                                start=(k == 0), stop=(k == KC - 1)),
                                reads=[bWS[wb + 1], bXN1[k]], writes=[bpu])
                        sct = slx
                        ctx.do("act", lambda e, pg=pg, sct=sct: e.activation(
                            out=SCt(sct)[:, :], in_=pg[:, :], func=AF.Silu),
                            reads=[bpg], writes=[bSC[sct]])
                        dst = j if slx == 0 else 64 + j
                        ctx.do("dve", lambda e, dst=dst, pu=pu, sct=sct: e.scalar_tensor_tensor(
                            out=slot_ap(dst), in0=SCt(sct)[:, :], scalar=SQRT2, in1=pu[:, :],
                            op0=ALU.mult, op1=ALU.mult),
                            reads=[bSC[sct], bpu], writes=[bhid[dst]])
                        pb = (2 * j + slx) % 2
                        ph = ps[4 + pb]
                        ctx.do("pe", lambda e, dst=dst, ph=ph: e.matmul(
                            ph[:, :], h128_s[:, :], slot_ap(dst), start=True, stop=True),
                            reads=[bhid[dst], bsm], writes=[bps[4 + pb]])
                        ctx.do("act", lambda e, dst=dst, ph=ph: e.activation(
                            out=slot_ap(dst), in_=ph[:, :], func=AF.Copy),
                            reads=[bps[4 + pb]], writes=[bhid[dst]])
                        stage_done[slx][0][j] = True
                        try_emit_stages(slx)
                for slx in range(2):
                    try_emit_stages(slx)
                    assert not pend[slx]

                for q in range(4):
                    for k in range(64):
                        wsl = 2 + ((k // 4) % 2)
                        if k % 4 == 0:
                            ctx.dma("gp", lambda e, k=k, q=q, wsl=wsl: e.dma_start(
                                out=WSs(wsl),
                                in_=g["w2tp"][:, q * 32768 + k * 512: q * 32768 + (k + 4) * 512]),
                                gws[wsl], writes=[bWS[wsl]])
                        for slx in range(2):
                            for dcl in range(4):
                                pw = ps[slx * 4 + dcl]
                                si = chunk_slot[slx][k]
                                ctx.do("pe", lambda e, k=k, dcl=dcl, wsl=wsl, pw=pw, si=si: e.matmul(
                                    pw[:, :], WSs(wsl)[:, (k % 4) * 512 + dcl * 128: (k % 4) * 512 + (dcl + 1) * 128],
                                    slot_ap(si), start=(k == 0), stop=(k == 63)),
                                    reads=[bWS[wsl], bhid[si]], writes=[bps[slx * 4 + dcl]])
                    for slx in range(2):
                        for dcl in range(4):
                            pw = ps[slx * 4 + dcl]
                            kd = q * 4 + dcl
                            sct = (slx * 4 + dcl) % 2
                            ctx.do("act", lambda e, pw=pw, sct=sct: e.activation(
                                out=SCt(sct)[:, :], in_=pw[:, :], func=AF.Copy),
                                reads=[bps[slx * 4 + dcl]], writes=[bSC[sct]])
                            ctx.dma("sp", lambda e, kd=kd, slx=slx, sct=sct: e.dma_start(
                                out=ffn0_spill[:, kd * TPC + slx * SL: kd * TPC + (slx + 1) * SL],
                                in_=SCt(sct)[:, :]),
                                grp_sp3, reads=[bSC[sct]], writes=[bdram])
            ctx.fence()
            grp_r = [ctx.dma_group("g_xrest0"), ctx.dma_group("g_xrest1")]
            grp_rg = [ctx.dma_group("g_xrestg0"), ctx.dma_group("g_xrestg1")]
            bxa = [ctx.buf("xr0"), ctx.buf("xr1")]
            prev_add = [None, None]
            for k in range(KC):
                pb = k % 2
                ctx.dma("sp", lambda e, k=k, pb=pb: e.dma_start(
                    out=SC[:, pb * TPC:(pb + 1) * TPC],
                    in_=x_spill[:, k * TPC:(k + 1) * TPC]),
                    grp_r[pb], reads=[bdram], writes=[bxa[pb]])
                extra = (prev_add[pb],) if prev_add[pb] else ()
                ctx.dma("gp", lambda e, k=k: e.dma_start(
                    out=Xs(k), in_=ffn0_spill[:, k * TPC:(k + 1) * TPC]),
                    grp_rg[pb], reads=[bdram], writes=[bX[k]], extra=extra)
                prev_add[pb] = ctx.do("dve", lambda e, k=k, pb=pb: e.tensor_tensor(
                    out=Xs(k), in0=Xs(k), in1=SC[:, pb * TPC:(pb + 1) * TPC], op=ALU.add),
                    reads=[bxa[pb]], writes=[bX[k]])
            ctx.fence()

        phase3()

        dbg_snap("X3", Xs, bX)
        if STOP_AFTER == "p3":
            for k in range(KC):
                ctx.dma("sp", lambda e, k=k: e.dma_start(out=y[k * 128:(k + 1) * 128, :], in_=Xs(k)),
                        grp_out, reads=[bX[k]], writes=[bdbg])
            _fo = ctx.semcount["g_out"]
            ctx.raw("sp", lambda e, v=_fo: e.wait_ge(ctx.semh["g_out"], v))
            block = es.enter_context(nc.Block())
            ctx.finalize(block)
            return nc

        ln_plain()
        dbg_snap("xn1b", XN1s, bXN1)

        ATT_SCALE = 0.08838834764831845

        def phase5(part):
            with contextlib.ExitStack() as p5s:
                psA = p5s.enter_context(nc.psum_tensor(f"p5A{part}", [128, SL], F32))
                psB = p5s.enter_context(nc.psum_tensor(f"p5B{part}", [128, SL], F32))
                psC = p5s.enter_context(nc.psum_tensor(f"p5C{part}", [128, SL], F32))
                psD = p5s.enter_context(nc.psum_tensor(f"p5D{part}", [128, SL], F32))
                psE = p5s.enter_context(nc.psum_tensor(f"p5E{part}", [128, SL], F32))
                psL = p5s.enter_context(nc.psum_tensor(f"p5L{part}", [1, SL], F32))
                psB2 = p5s.enter_context(nc.psum_tensor(f"p5L2{part}", [1, SL], F32))
                expb = p5s.enter_context(nc.sbuf_tensor(f"expb{part}", [128, 1], F32))
                bqv = p5s.enter_context(nc.sbuf_tensor(f"bqv{part}", [1, 128], BF16))
                vt0 = p5s.enter_context(nc.sbuf_tensor(f"vt0{part}", [128, 256], BF16))
                bpsA = ctx.buf("p5A"); bpsB = ctx.buf("p5B"); bpsC = ctx.buf("p5C")
                bpsD = ctx.buf("p5D"); bpsE = ctx.buf("p5E"); bpsL = ctx.buf("p5L")
                bpsB2 = ctx.buf("p5L2")
                bexpb = ctx.buf("expb"); bbqv = ctx.buf("bqv")
                bvt0 = [ctx.buf("bvt0a"), ctx.buf("bvt0b")]
                battn = [ctx.buf(f"attn{h}") for h in range(NH)]
                bQ = [ctx.buf(f"Q{h}") for h in range(NH)]
                grp_c = ctx.dma_group(f"g_kvc{part}")
                grp_cv = ctx.dma_group(f"g_kvcv{part}")
                ctx.do("gp", lambda e: e.memset(expb[:, :], -8.0), writes=[bexpb])
                vb_all = [mu_r.bitcast(BF16), rstd_r.bitcast(BF16)]
                if part == 0:
                    for half in range(2):
                        ctx.dma("gp", lambda e, half=half: e.dma_start(
                            out=vb_all[half][:, :],
                            in_=g["bqkv_row"][:, 2 * D + half * 1024: 2 * D + (half + 1) * 1024]),
                            ctx.dma_group("g_vbias"), writes=[bmu if half == 0 else brstd])

                def qslot(h, s):
                    return BB[:, h * TPC + s * SL: h * TPC + (s + 1) * SL]
                def aslot(h, s):
                    return BA[:, h * TPC + s * SL: h * TPC + (s + 1) * SL]

                for h in (range(NH) if part == 0 else ()):
                    wk, wv = h % 2, 2 + h % 2
                    for wsl, sec in ((wk, 16 + h), (wv, 32 + h)):
                        ctx.dma("gp", lambda e, wsl=wsl, sec=sec: e.dma_start(
                            out=WSs(wsl), in_=g["wqkvtp"][:, sec * 2048:(sec + 1) * 2048]),
                            gws[wsl], writes=[bWS[wsl]])
                    bqv_ap = vb_all[h // 8][:, (h % 8) * 128:(h % 8 + 1) * 128]
                    bbqv_ = bmu if h < 8 else brstd
                    for s in range(NSLAB):
                        pk, bpk = (psA, bpsA) if s == 0 else (psC, bpsC)
                        sck = 0 if s == 0 else 1
                        for k in range(KC):
                            ctx.do("pe", lambda e, k=k, s=s, wk=wk, pk=pk: e.matmul(
                                pk[:, :], WSs(wk)[:, k * 128:(k + 1) * 128], XN1s(k, s),
                                start=(k == 0), stop=(k == KC - 1)),
                                reads=[bWS[wk], bXN1[k]], writes=[bpk])
                        ctx.do("dve", lambda e, h=h, pk=pk, sck=sck: e.tensor_scalar(
                            out=SCt(sck)[:, :], in0=pk[:, :], scalar1=bqkvc[:, 16 + h:17 + h],
                            scalar2=None, op0=ALU.add),
                            reads=[bpk, bsm], writes=[bSC[sck]])
                        ctx.dma("gp", lambda e, s=s, h=h, sck=sck: e.dma_start(
                            out=k_contrib[s * 2048 + h * 128: s * 2048 + (h + 1) * 128, :],
                            in_=SCt(sck)[:, :]),
                            grp_c, reads=[bSC[sck]], writes=[bdram])
                        for ts_ in range(4):
                            vb = ts_ % 2
                            pv, bpv = (psB, bpsB) if vb == 0 else (psD, bpsD)
                            for k in range(KC):
                                ctx.do("pe", lambda e, k=k, s=s, ts_=ts_, wv=wv, pv=pv: e.matmul(
                                    pv[:, :128], XN1s(k, s)[:, ts_ * 128:(ts_ + 1) * 128],
                                    WSs(wv)[:, k * 128:(k + 1) * 128],
                                    start=(k == 0), stop=False),
                                    reads=[bWS[wv], bXN1[k]], writes=[bpv])
                            ctx.do("pe", lambda e, pv=pv, bqv_ap=bqv_ap: e.matmul(
                                pv[:, :128], ones_rb[:, :], bqv_ap, start=False, stop=True),
                                reads=[bbqv_, bsm], writes=[bpv])
                            ctx.do("act", lambda e, vb=vb, pv=pv: e.activation(
                                out=vt0[:, vb * 128:(vb + 1) * 128], in_=pv[:, :128], func=AF.Copy),
                                reads=[bpv], writes=[bvt0[vb]])
                            ctx.dma("sp", lambda e, s=s, h=h, ts_=ts_, vb=vb: e.dma_start(
                                out=v_contrib[s * 2048 + h * 128: s * 2048 + (h + 1) * 128,
                                              ts_ * 128:(ts_ + 1) * 128],
                                in_=vt0[:, vb * 128:(vb + 1) * 128]),
                                grp_cv, reads=[bvt0[vb]], writes=[bdram])

                if part == 0:
                    bkc = ctx.buf("kcontrib"); bvc = ctx.buf("vcontrib")
                    bkag = ctx.buf("kag"); bvag = ctx.buf("vag")
                    bkc.w = bdram.w; bkc.r = list(bdram.r)
                    bvc.w = bdram.w; bvc.r = list(bdram.r)
                    ctx.cc(lambda e: e.collective_compute(
                        "AllGather", ALU.bypass, replica_groups=[list(range(8))],
                        ins=[k_contrib[:, :]], outs=[k_ag[:, :]]),
                        reads=[bkc], writes=[bkag])
                    ctx.cc(lambda e: e.collective_compute(
                        "AllGather", ALU.bypass, replica_groups=[list(range(8))],
                        ins=[v_contrib[:, :]], outs=[v_ag[:, :]]),
                        reads=[bvc], writes=[bvag])
                    return

                for h in range(NH):
                    wq = h % 2
                    ctx.dma("gp", lambda e, h=h, wq=wq: e.dma_start(
                        out=WSs(wq), in_=g["wqkvtp"][:, h * 2048:(h + 1) * 2048]),
                        gws[wq], writes=[bWS[wq]])
                    for s in range(NSLAB):
                        pq, bpq = (psA, bpsA) if (2 * h + s) % 2 == 0 else (psB, bpsB)
                        for k in range(KC):
                            ctx.do("pe", lambda e, k=k, s=s, wq=wq, pq=pq: e.matmul(
                                pq[:, :], WSs(wq)[:, k * 128:(k + 1) * 128], XN1s(k, s),
                                start=(k == 0), stop=(k == KC - 1)),
                                reads=[bWS[wq], bXN1[k]], writes=[bpq])
                        ctx.do("dve", lambda e, h=h, s=s, pq=pq: e.tensor_scalar(
                            out=qslot(h, s), in0=pq[:, :], scalar1=bqkvc[:, h:h + 1],
                            scalar2=None, op0=ALU.add),
                            reads=[bpq, bsm], writes=[bQ[h]])

                def ktile(b):
                    return WS[:, b * 512:(b + 1) * 512]
                def vtile(b):
                    return WS[:, 1024 + b * 512: 1024 + (b + 1) * 512]
                def ptile(b):
                    return WS[:, 2048 + b * 2048: 2048 + (b + 1) * 2048]
                bkt = [ctx.buf("kt0"), ctx.buf("kt1")]
                bvt = [ctx.buf("vt0b"), ctx.buf("vt1b")]
                bpt = [ctx.buf("pt0"), ctx.buf("pt1")]
                gkt = [ctx.dma_group(f"g_kt0{part}"), ctx.dma_group(f"g_kt1{part}")]
                gvt = [ctx.dma_group(f"g_vt0{part}"), ctx.dma_group(f"g_vt1{part}")]
                blrow = ctx.buf("lrow")

                ctx.fence()
                ctx.begin_switch(["pe", "dve", "act", "sp"])
                for c8 in range(8):
                    ctx.begin_body()
                    ns = [c8 + 1, 8 - c8]
                    for s in range(NSLAB):
                        for h in range(NH):
                            par = (s * NH + h) % 2
                            psE_, bpsE_ = (psE, bpsE) if par == 0 else (psA, bpsA)
                            psL_, bpsL_ = (psL, bpsL) if par == 0 else (psB2, bpsB2)
                            for kt in range(ns[s]):
                                kb = kt % 2
                                gch = kt if s == 0 else 7 - kt
                                owner = gch
                                sel = 0 if s == 0 else 1
                                krow = owner * 4096 + sel * 2048 + h * 128
                                ctx.dma("sp", lambda e, krow=krow, kb=kb: e.dma_start(
                                    out=ktile(kb), in_=k_ag[krow:krow + 128, :]),
                                    gkt[kb], writes=[bkt[kb]])
                                ctx.dma("act", lambda e, krow=krow, kb=kb: e.dma_start(
                                    out=vtile(kb), in_=v_ag[krow:krow + 128, :]),
                                    gvt[kb], writes=[bvt[kb]])
                                first = (kt == 0)
                                last = (kt == ns[s] - 1)
                                for ksub in range(4):
                                    pss, bpss = ((psC, bpsC) if ksub % 2 == 0 else (psD, bpsD))
                                    ctx.do("pe", lambda e, kb=kb, ksub=ksub, h=h, s=s, pss=pss: e.matmul(
                                        pss[:, :], ktile(kb)[:, ksub * 128:(ksub + 1) * 128],
                                        qslot(h, s), start=True, stop=True),
                                        reads=[bkt[kb], bQ[h]], writes=[bpss])
                                    ctx.do("act", lambda e, kb=kb, ksub=ksub, pss=pss: e.activation(
                                        out=ptile(kb)[:, ksub * 512:(ksub + 1) * 512], in_=pss[:, :],
                                        func=AF.Exp, scale=ATT_SCALE, bias=expb[:, :]),
                                        reads=[bpss, bexpb], writes=[bpt[kb]])
                                    if last:
                                        ctx.do("dve", lambda e, kb=kb, ksub=ksub: e.tensor_tensor(
                                            out=ptile(kb)[:, ksub * 512:(ksub + 1) * 512],
                                            in0=ptile(kb)[:, ksub * 512:(ksub + 1) * 512],
                                            in1=dmask_s[:, ksub * SL:(ksub + 1) * SL], op=ALU.mult),
                                            reads=[bsm], writes=[bpt[kb]])
                                for ksub in range(4):
                                    ctx.do("pe", lambda e, kb=kb, ksub=ksub, st_=(first and ksub == 0),
                                           sp_=(last and ksub == 3), psE_=psE_: e.matmul(
                                        psE_[:, :], vtile(kb)[:, ksub * 128:(ksub + 1) * 128],
                                        ptile(kb)[:, ksub * 512:(ksub + 1) * 512],
                                        start=st_, stop=sp_),
                                        reads=[bvt[kb], bpt[kb]], writes=[bpsE_])
                                    ctx.do("pe", lambda e, kb=kb, ksub=ksub, st_=(first and ksub == 0),
                                           sp_=(last and ksub == 3), psL_=psL_: e.matmul(
                                        psL_[:, :], ones_cb[:, :],
                                        ptile(kb)[:, ksub * 512:(ksub + 1) * 512],
                                        start=st_, stop=sp_),
                                        reads=[bsm, bpt[kb]], writes=[bpsL_])
                            lr_ = row_t
                            ctx.do("dve", lambda e, psL_=psL_, lr_=lr_: e.reciprocal(
                                lr_[:, :], psL_[:, :]),
                                reads=[bpsL_], writes=[blrow])
                            pbc, bpbc = (psC, bpsC) if par == 0 else (psD, bpsD)
                            ctx.do("pe", lambda e, pbc=pbc, lr_=lr_: e.matmul(
                                pbc[:, :], ones_rf[:, :], lr_[:, :], start=True, stop=True),
                                reads=[blrow, bsm], writes=[bpbc])
                            ctx.do("act", lambda e, psE_=psE_, par=par: e.activation(
                                out=SCt(par)[:, :], in_=psE_[:, :], func=AF.Copy),
                                reads=[bpsE_], writes=[bSC[par]])
                            ctx.do("dve", lambda e, h=h, s=s, par=par, pbc=pbc: e.tensor_tensor(
                                out=aslot(h, s), in0=SCt(par)[:, :], in1=pbc[:, :], op=ALU.mult),
                                reads=[bSC[par], bpbc], writes=[battn[h]])
                    ctx.end_body()
                ctx.end_switch(g["rank_in"][0:1, 0:1])

                for q4 in range(8):
                    wsl2 = q4 % 2
                    ctx.dma("gp", lambda e, q4=q4, wsl2=wsl2: e.dma_start(
                        out=WS[:, wsl2 * 4096:(wsl2 + 1) * 4096],
                        in_=g["wouttp"][:, q4 * 4096:(q4 + 1) * 4096],
                        max_dma_last_dim=8192),
                        gws[wsl2], writes=[bWS[2 * wsl2], bWS[2 * wsl2 + 1]])
                    for k in range(NH):
                        wsl = 2 * wsl2
                        for dcl in range(2):
                            for s in range(NSLAB):
                                ps, bps = [(psA, bpsA), (psB, bpsB),
                                           (psC, bpsC), (psD, bpsD)][dcl * 2 + s]
                                ctx.do("pe", lambda e, k=k, dcl=dcl, s=s, wsl2=wsl2, ps=ps: e.matmul(
                                    ps[:, :], WS[:, wsl2 * 4096 + k * 256 + dcl * 128:
                                                 wsl2 * 4096 + k * 256 + (dcl + 1) * 128],
                                    aslot(k, s), start=(k == 0), stop=(k == NH - 1)),
                                    reads=[bWS[2 * wsl2], battn[k]], writes=[bps])
                    for dcl in range(2):
                        for s in range(NSLAB):
                            ps, bps = [(psA, bpsA), (psB, bpsB),
                                       (psC, bpsC), (psD, bpsD)][dcl * 2 + s]
                            kd = q4 * 2 + dcl
                            ctx.do("dve", lambda e, kd=kd, s=s, ps=ps: e.scalar_tensor_tensor(
                                out=Xs(kd, s), in0=ps[:, :], scalar=bout1[:, kd:kd + 1],
                                in1=Xs(kd, s), op0=ALU.add, op1=ALU.add),
                                reads=[bps, bsm], writes=[bX[kd]])
            ctx.fence()

        phase5(0)
        ctx.fence(exclude=("cc",))

        def phase6():
            with contextlib.ExitStack() as p6s:
                psA = p6s.enter_context(nc.psum_tensor("p6A", [128, SL], F32))
                psB = p6s.enter_context(nc.psum_tensor("p6B", [128, SL], F32))
                acc = [p6s.enter_context(nc.psum_tensor(f"p6c{i}", [128, SL], F32)) for i in range(4)]
                psT = p6s.enter_context(nc.psum_tensor("p6T", [128, 1024], BF16))
                psT2 = p6s.enter_context(nc.psum_tensor("p6T2", [128, 1024], BF16))
                bpsA = ctx.buf("p6A"); bpsB = ctx.buf("p6B")
                bacc = [ctx.buf(f"p6c{i}") for i in range(4)]
                bpsT = ctx.buf("p6T"); bpsT2 = ctx.buf("p6T2")
                bwset = [ctx.buf("p6ws0"), ctx.buf("p6ws1")]
                bhtm = [ctx.buf(f"p6htm{t}") for t in range(8)]
                bhfm = [ctx.buf(f"p6hfm{i}") for i in range(16)]
                bwd = [ctx.buf("p6wd0"), ctx.buf("p6wd1")]
                bSC6 = [[ctx.buf(f"p6sc{p}_{i}") for i in range(4)] for p in range(2)]
                gw6 = [ctx.dma_group("g_w6a"), ctx.dma_group("g_w6b")]
                gw6d = [ctx.dma_group("g_w6d0"), ctx.dma_group("g_w6d1")]
                IW = 256

                def wgsl(st_, k):
                    return BB[:, st_ * 8192 + k * IW: st_ * 8192 + (k + 1) * IW]
                def wusl(st_, k):
                    return BB[:, st_ * 8192 + 4096 + k * IW: st_ * 8192 + 4096 + (k + 1) * IW]
                def htm(tsub):
                    return WS[:, tsub * IW:(tsub + 1) * IW]
                def hfm(ic):
                    return BA[:, ic * TPC:(ic + 1) * TPC]
                def wd_t(hb):
                    return WS[:, 4096 + hb * 2048: 4096 + (hb + 1) * 2048]

                for ig in range(4):
                    for isl8 in range(8):
                        ii = ig * 8 + isl8
                        st_ = ii % 2
                        ctx.dma("gp", lambda e, ii=ii, st_=st_: e.dma_start(
                            out=BB[:, st_ * 8192: st_ * 8192 + 4096],
                            in_=g["wgp"][:, ii * 4096:(ii + 1) * 4096],
                            max_dma_last_dim=8192),
                            gw6[st_], writes=[bwset[st_]])
                        ctx.dma("gp", lambda e, ii=ii, st_=st_: e.dma_start(
                            out=BB[:, st_ * 8192 + 4096: st_ * 8192 + 8192],
                            in_=g["wup"][:, ii * 4096:(ii + 1) * 4096],
                            max_dma_last_dim=8192),
                            gw6[st_], writes=[bwset[st_]])
                        for tsub in range(8):
                            s, tl = tsub // 4, tsub % 4
                            pp = tsub % 2
                            pg, bpg = (psA, bpsA) if pp == 0 else (acc[0], bacc[0])
                            pu, bpu = (psB, bpsB) if pp == 0 else (acc[1], bacc[1])
                            PO = pp * IW
                            bSCp = [bSC6[pp][i] for i in range(4)]
                            for k in range(KC):
                                ctx.do("pe", lambda e, PO=PO, k=k, s=s, tl=tl, st_=st_, pg=pg: e.matmul(
                                    pg[:, :IW], XN1s(k, s)[:, tl * 128:(tl + 1) * 128], wgsl(st_, k),
                                    start=(k == 0), stop=(k == KC - 1)),
                                    reads=[bXN1[k], bwset[st_]], writes=[bpg])
                            for k in range(KC):
                                ctx.do("pe", lambda e, PO=PO, k=k, s=s, tl=tl, st_=st_, pu=pu: e.matmul(
                                    pu[:, :IW], XN1s(k, s)[:, tl * 128:(tl + 1) * 128], wusl(st_, k),
                                    start=(k == 0), stop=(k == KC - 1)),
                                    reads=[bXN1[k], bwset[st_]], writes=[bpu])
                            ctx.do("act", lambda e, PO=PO, pg=pg: e.activation(
                                out=SCt(0)[:, PO:PO + IW], in_=pg[:, :IW], func=AF.Silu),
                                reads=[bpg], writes=[bSCp[0]])
                            ctx.do("dve", lambda e, PO=PO, pu=pu: e.tensor_tensor(
                                out=SCt(1)[:, PO:PO + IW], in0=SCt(0)[:, PO:PO + IW], in1=pu[:, :IW], op=ALU.mult),
                                reads=[bSCp[0], bpu], writes=[bSCp[1]])
                            ctx.do("act", lambda e, PO=PO: e.activation(
                                out=SCt(2)[:, PO:PO + IW], in_=SCt(1)[:, PO:PO + IW], func=AF.Abs),
                                reads=[bSCp[1]], writes=[bSCp[2]])
                            NG = IW // 4
                            def gview(tile, j, PO=0):
                                return tile[:, PO:PO + IW].rearrange("p (gg f) -> p gg f", f=4)[:, :, j]
                            for (dst, i0, i1, op) in [(0, 0, 1, ALU.max), (1, 0, 1, ALU.min),
                                                      (2, 2, 3, ALU.max), (3, 2, 3, ALU.min)]:
                                ctx.do("dve", lambda e, PO=PO, dst=dst, i0=i0, i1=i1, op=op: e.tensor_tensor(
                                    out=SCt(3)[:, PO + dst * NG:PO + (dst + 1) * NG],
                                    in0=gview(SCt(2), i0, PO), in1=gview(SCt(2), i1, PO), op=op),
                                    reads=[bSCp[2]], writes=[bSCp[3]])
                            ctx.do("dve", lambda e, PO=PO: e.tensor_tensor(
                                out=SCt(0)[:, PO + 0:PO + NG], in0=SCt(3)[:, PO + 0:PO + NG], in1=SCt(3)[:, PO + 2 * NG:PO + 3 * NG], op=ALU.min),
                                reads=[bSCp[3]], writes=[bSCp[0]])
                            ctx.do("dve", lambda e, PO=PO: e.tensor_tensor(
                                out=SCt(0)[:, PO + NG:PO + 2 * NG], in0=SCt(3)[:, PO + NG:PO + 2 * NG], in1=SCt(3)[:, PO + 3 * NG:PO + 4 * NG], op=ALU.max),
                                reads=[bSCp[3]], writes=[bSCp[0]])
                            ctx.do("dve", lambda e, PO=PO: e.tensor_tensor(
                                out=SCt(0)[:, PO + 2 * NG:PO + 3 * NG], in0=SCt(0)[:, PO + 0:PO + NG], in1=SCt(0)[:, PO + NG:PO + 2 * NG], op=ALU.max),
                                reads=[bSCp[0]], writes=[bSCp[0]])
                            for j4 in range(4):
                                ctx.do("dve", lambda e, PO=PO, j4=j4: e.tensor_tensor(
                                    out=SCt(3)[:, PO + j4 * NG:PO + (j4 + 1) * NG],
                                    in0=gview(SCt(2), j4, PO), in1=SCt(0)[:, PO + 2 * NG:PO + 3 * NG], op=ALU.is_ge),
                                    reads=[bSCp[2], bSCp[0]], writes=[bSCp[3]])
                                ctx.do("dve", lambda e, PO=PO, j4=j4, tsub=tsub: e.tensor_tensor(
                                    out=gview(htm(tsub), j4), in0=gview(SCt(1), j4, PO),
                                    in1=SCt(3)[:, PO + j4 * NG:PO + (j4 + 1) * NG], op=ALU.mult),
                                    reads=[bSCp[1], bSCp[3]], writes=[bhtm[tsub]])
                        for tsub in range(8):
                            s, tl = tsub // 4, tsub % 4
                            for ic2 in range(2):
                                tp = (2 * tsub + ic2) % 2
                                pT, bpT = (psT, bpsT) if tp == 0 else (psT2, bpsT2)
                                ctx.do("pe", lambda e, tsub=tsub, ic2=ic2, pT=pT: e.transpose(
                                    pT[:, 0:128], htm(tsub)[:, ic2 * 128:(ic2 + 1) * 128], ident_s[:, :]),
                                    reads=[bhtm[tsub], bsm], writes=[bpT])
                                ic = isl8 * 2 + ic2
                                ctx.do("act", lambda e, ic=ic, s=s, tl=tl, pT=pT: e.activation(
                                    out=hfm(ic)[:, s * SL + tl * 128: s * SL + (tl + 1) * 128],
                                    in_=pT[:, 0:128], func=AF.Copy),
                                    reads=[bpT], writes=[bhfm[ic]])
                    for q4 in range(8):
                        for hb in range(2):
                            ctx.dma("gp", lambda e, ig=ig, q4=q4, hb=hb: e.dma_start(
                                out=wd_t(hb),
                                in_=g["wdp"][:, (ig * 8 + q4) * 4096 + hb * 2048:
                                             (ig * 8 + q4) * 4096 + (hb + 1) * 2048]),
                                gw6d[hb], writes=[bwd[hb]])
                        for ic in range(KC):
                            hb = ic // 8
                            for dcl in range(2):
                                for s in range(NSLAB):
                                    ps, bps = (acc[dcl * 2 + s], bacc[dcl * 2 + s])
                                    ctx.do("pe", lambda e, ic=ic, dcl=dcl, s=s, hb=hb, ps=ps: e.matmul(
                                        ps[:, :], wd_t(hb)[:, (ic % 8) * 256 + dcl * 128:
                                                           (ic % 8) * 256 + (dcl + 1) * 128],
                                        hfm(ic)[:, s * SL:(s + 1) * SL],
                                        start=(ic == 0), stop=(ic == KC - 1)),
                                        reads=[bwd[hb], bhfm[ic]], writes=[bps])
                        for dcl in range(2):
                            for s in range(NSLAB):
                                ps, bps = (acc[dcl * 2 + s], bacc[dcl * 2 + s])
                                kd = q4 * 2 + dcl
                                ctx.do("dve", lambda e, kd=kd, s=s, ps=ps: e.tensor_tensor(
                                    out=Xs(kd, s), in0=Xs(kd, s), in1=ps[:, :], op=ALU.add),
                                    reads=[bps], writes=[bX[kd]])
            ctx.fence()

        phase6()
        phase5(1)
        dbg_snap("X5", Xs, bX)
        if "attnfm" in dbg_outs:
            battn2 = ctx.buf("attn2")
            for k in range(KC):
                ctx.dma("sp", lambda e, k=k: e.dma_start(
                    out=dbg_outs["attnfm"][k * 128:(k + 1) * 128, :], in_=BA[:, k * TPC:(k + 1) * TPC]),
                    grp_out, reads=[battn2], writes=[bdbg])

        for k in range(KC):
            ctx.dma("sp", lambda e, k=k: e.dma_start(out=y[k * 128:(k + 1) * 128, :], in_=Xs(k)),
                    grp_out, reads=[bX[k]], writes=[bdbg])
        final_out = ctx.semcount["g_out"]
        ctx.raw("sp", lambda e, v=final_out: e.wait_ge(ctx.semh["g_out"], v))

        block = es.enter_context(nc.Block())
        ctx.finalize(block)
    return nc



def kernel(**inputs):
    shared, per_core = prep_inputs(inputs)
    nc = build_program()
    in_maps = []
    for c in range(8):
        m = dict(shared)
        m.update(per_core[c])
        in_maps.append(m)
    res = run_bass_kernel_spmd(nc, in_maps, core_ids=list(range(8)))
    out = np.zeros((2, 4096, D), dtype=np.float32)
    for c in range(8):
        yc = np.asarray(res.results[c]["y_fm"])
        out[0, SL * c:SL * (c + 1), :] = yc[:, 0:SL].T
        out[1, SL * (7 - c):SL * (8 - c), :] = yc[:, SL:2 * SL].T
    return out


# version 09
# speedup vs baseline: 1.1968x; 1.0053x over previous
import contextlib
import numpy as np
import ml_dtypes
import concourse.bass as bass
import concourse.mybir as mybir
from concourse.bass_utils import run_bass_kernel_spmd

F32 = mybir.dt.float32
BF16 = mybir.dt.bfloat16
I32 = mybir.dt.int32
AF = mybir.ActivationFunctionType
ALU = mybir.AluOpType
AX = mybir.AxisListType

D = 2048
KC = 16
TPC = 1024
NSLAB = 2
SL = 512
H_FFN = 8192
NH = 16
EH = 128
EPS = 1e-5


class Buf:
    __slots__ = ("name", "w", "r")
    def __init__(self, name):
        self.name = name
        self.w = None
        self.r = []

class DmaGroup:
    __slots__ = ("sem",)
    def __init__(self, sem):
        self.sem = sem

class Queue:
    def __init__(self, name, get_eng, csem):
        self.name = name
        self.get_eng = get_eng
        self.csem = csem
        self.waited = {}
        self.ops = []

class Ctx:
    def __init__(self, nc):
        self.nc = nc
        self.semh = {}
        self.semcount = {}
        self.queues = {}
        self.bufs = []
        self._es = None
        self._sw = None

    def _sem(self, name):
        if name not in self.semh:
            self.semh[name] = self._es.enter_context(self.nc.semaphore(name))
            self.semcount[name] = 0
        return name

    def add_queue(self, name, eng_attr, csem=None):
        if csem:
            self._sem(csem)
        self.queues[name] = Queue(name, eng_attr, csem)

    def dma_group(self, name):
        return DmaGroup(self._sem(name))

    def buf(self, name):
        b = Buf(name)
        self.bufs.append(b)
        return b

    def _resolve(self, tok):
        if tok is None:
            return None
        if tok[0] == "D":
            _, sem, _v = tok
            return (sem, self.semcount[sem])
        return (tok[1], tok[2])

    def _collect(self, reads, writes, extra):
        waits = {}
        def addw(tok):
            rv = self._resolve(tok)
            if rv is None:
                return
            s, v = rv
            if waits.get(s, 0) < v:
                waits[s] = v
        for b in reads:
            addw(b.w)
        for b in writes:
            addw(b.w)
            for t in b.r:
                addw(t)
        for t in extra:
            addw(t)
        return waits

    def _emit(self, qname, fn, waits, sem, amt, reads, writes):
        q = self.queues[qname]
        final = []
        for s, v in waits.items():
            if qname == "pe" and s == "pe":
                continue
            if q.waited.get(s, -1) >= v:
                continue
            final.append((s, v))
            q.waited[s] = v
        if sem is not None:
            self.semcount[sem] += amt
            tok_kind = "D" if amt >= 16 else "C"
            tok = (tok_kind, sem, self.semcount[sem])
            q.ops.append(("op", final, fn, (sem, amt)))
        else:
            tok = None
            q.ops.append(("op", final, fn, None))
        for b in reads:
            if tok is not None:
                b.r.append(tok)
        for b in writes:
            b.w = tok
            b.r = []
        return tok

    def do(self, qname, fn, reads=(), writes=(), extra=()):
        q = self.queues[qname]
        waits = self._collect(reads, writes, extra)
        return self._emit(qname, fn, waits, q.csem, 1, reads, writes)

    def dma(self, qname, fn, grp, reads=(), writes=(), extra=()):
        waits = self._collect(reads, writes, extra)
        return self._emit(qname, fn, waits, grp.sem, 16, reads, writes)

    def cc(self, fn, reads=(), writes=(), extra=()):
        self._sem("cc")
        waits = self._collect(reads, writes, extra)
        return self._emit("gp", fn, waits, "cc", 1, reads, writes)

    def raw(self, qname, fn):
        self.queues[qname].ops.append(("raw", fn))

    def fence(self, exclude=()):
        snap = {s: v for s, v in self.semcount.items() if v > 0 and s not in exclude}
        for q in self.queues.values():
            final = []
            for s, v in snap.items():
                if q.waited.get(s, -1) < v:
                    final.append((s, v))
                    q.waited[s] = v
            if final:
                q.ops.append(("op", final, None, None))
        for b in self.bufs:
            b.w = None
            b.r = []

    def begin_switch(self, qnames):
        self._sw = {
            "queues": list(qnames),
            "count0": dict(self.semcount),
            "waited0": {qn: dict(self.queues[qn].waited) for qn in qnames},
            "ops0": {qn: self.queues[qn].ops for qn in qnames},
            "buf0": [(b, b.w, list(b.r)) for b in self.bufs],
            "bodies": {qn: [] for qn in qnames},
            "counts": [],
        }
        for qn in qnames:
            self.queues[qn].ops = []

    def begin_body(self):
        sw = self._sw
        self.semcount = dict(sw["count0"])
        for qn in sw["queues"]:
            q = self.queues[qn]
            q.count = None
            q.waited = dict(sw["waited0"][qn])
            q.ops = []
        for b, w, r in sw["buf0"]:
            b.w = w
            b.r = list(r)

    def end_body(self):
        sw = self._sw
        for qn in sw["queues"]:
            sw["bodies"][qn].append(self.queues[qn].ops)
        sw["counts"].append(dict(self.semcount))

    def end_switch(self, rank_ap):
        sw = self._sw
        maxima = dict(sw["count0"])
        for counts in sw["counts"]:
            for s, v in counts.items():
                maxima[s] = max(maxima.get(s, 0), v)
        for qn in sw["queues"]:
            q = self.queues[qn]
            bodies = []
            for i in range(8):
                ops = sw["bodies"][qn][i]
                incs = {}
                for item in ops:
                    if item[0] == "op" and item[3] is not None:
                        s, a = item[3]
                        incs[s] = incs.get(s, 0) + a
                pads = []
                for s, a in incs.items():
                    target = maxima[s] - sw["count0"].get(s, 0)
                    if target > a:
                        pads.append((s, target - a))
                body = list(ops)
                if pads:
                    body.append(("pad", pads))
                bodies.append((body, incs))
            for i, (body, incs) in enumerate(bodies):
                all_incs = set()
                for _b2, inc2 in bodies:
                    all_incs.update(inc2)
                missing = []
                for s in all_incs:
                    if s not in incs:
                        target = maxima[s] - sw["count0"].get(s, 0)
                        if target > 0:
                            missing.append((s, target))
                if missing:
                    body.append(("pad", missing))
            q.ops = sw["ops0"][qn]
            q.ops.append(("switch", rank_ap, [b for b, _ in bodies]))
            q.waited = {}
        self.semcount = maxima
        self._sw = None
        for b in self.bufs:
            b.w = None
            b.r = []
        self.fence()

    def finalize(self, block):
        semh = self.semh
        def run_ops(eng, ops):
            for item in ops:
                if item[0] == "op":
                    _, waits, fn, inc = item
                    for s, v in waits:
                        eng.wait_ge(semh[s], v)
                    if fn is not None:
                        instr = fn(eng)
                        if inc is not None:
                            instr.then_inc(semh[inc[0]], inc[1])
                elif item[0] == "raw":
                    item[1](eng)
                elif item[0] == "pad":
                    for s, amt in item[1]:
                        eng.sem_inc(semh[s], amt)
                elif item[0] == "switch":
                    _, rank_ap, bodies = item
                    with eng.register("rk") as rk:
                        eng.reg_load(rk, rank_ap)
                        idx = eng.snap(rk)
                        for i in self.nc.Switch(engines=[eng], index=[idx], n=8):
                            run_ops(eng, bodies[i])
        for q in self.queues.values():
            ops = q.ops
            getattr(block, q.get_eng)(lambda eng, ops=ops: run_ops(eng, ops))



def _hadamard(n):
    h = np.array([[1.0]], dtype=np.float64)
    while h.shape[0] < n:
        h = np.block([[h, h], [h, -h]])
    return h

def prep_inputs(I):
    bf = ml_dtypes.bfloat16
    f32 = np.float32
    x = np.asarray(I["L0_attn_x"], f32)

    shared = {}
    C = np.ascontiguousarray
    shared["w_in"] = C(np.asarray(I["L0_attn_W_in"], f32))
    shared["w_out_aug"] = C(np.concatenate([np.asarray(I["L0_attn_W_out"], f32),
                                            np.asarray(I["L0_attn_b_out"], f32)[None, :]], 0))
    shared["w_down0"] = C(np.asarray(I["L0_attn_W_down"], f32))
    shared["w_up_aug"] = C(np.concatenate([np.asarray(I["L0_attn_W_up"], f32),
                                           np.asarray(I["L0_attn_b_up"], f32)[None, :]], 0))
    w1t = np.asarray(I["L0_ffn_W1"], f32).T
    w3t = np.asarray(I["L0_ffn_W3"], f32).T
    w2t = np.asarray(I["L0_ffn_W2"], f32).T
    wqkvt = np.asarray(I["L1_attn_W_qkv"], f32).T
    woutt = np.asarray(I["L1_attn_W_out"], f32).T
    wg = np.asarray(I["L1_ffn_w_gate"], f32)
    wu = np.asarray(I["L1_ffn_w_up"], f32)
    wd = np.asarray(I["L1_ffn_w_down"], f32)
    shared["w1tp"] = C(w1t.reshape(16, 128, 64, 128).transpose(1, 2, 0, 3).reshape(128, 131072))
    shared["w3tp"] = C(w3t.reshape(16, 128, 64, 128).transpose(1, 2, 0, 3).reshape(128, 131072))
    shared["w2tp"] = C(w2t.reshape(64, 128, 4, 512).transpose(1, 2, 0, 3).reshape(128, 131072))
    shared["wqkvtp"] = C(wqkvt.reshape(16, 128, 48, 128).transpose(1, 2, 0, 3).reshape(128, 98304))
    shared["wouttp"] = C(woutt.reshape(16, 128, 8, 256).transpose(1, 2, 0, 3).reshape(128, 32768))
    shared["wgp"] = C(wg.reshape(16, 128, 32, 256).transpose(1, 2, 0, 3).reshape(128, 131072))
    shared["wup"] = C(wu.reshape(16, 128, 32, 256).transpose(1, 2, 0, 3).reshape(128, 131072))
    shared["wdp"] = C(wd.reshape(4, 16, 128, 8, 256).transpose(2, 0, 3, 1, 4).reshape(128, 131072))

    def colmajor(v, nch):
        return C(np.asarray(v, f32).reshape(nch, 128).T)
    shared["gamma_c"] = colmajor(I["L0_attn_ln_gamma"], 16)
    shared["beta_c"] = colmajor(I["L0_attn_ln_beta"], 16)
    shared["dskip_c"] = colmajor(I["L0_attn_D_skip"], 16)
    shared["bout1_c"] = colmajor(I["L1_attn_b_out"], 16)
    shared["bqkv_c"] = colmajor(I["L1_attn_b_qkv"], 48)
    shared["bqkv_row"] = C(np.asarray(I["L1_attn_b_qkv"], f32)[None, :])
    shared["b_in"] = C(np.asarray(I["L0_attn_b_in"], f32)[:, None])
    shared["b_down"] = C(np.asarray(I["L0_attn_b_down"], f32)[:, None])
    shared["decay_logit"] = C(np.asarray(I["L0_attn_decay_logit"], f32)[:, None])

    h128 = (_hadamard(128) * (2.0 ** -7)).astype(bf)
    shared["h128"] = h128
    shared["ident_bf"] = np.eye(128, dtype=bf)
    dm = np.zeros((4, 128, SL), dtype=bf)
    for j in range(4):
        ki = np.arange(128)[:, None]
        qi = np.arange(SL)[None, :]
        dm[j] = ((128 * j + ki) <= qi).astype(bf)
    shared["dmask"] = C(dm.reshape(4 * 128, SL))

    per_core = []
    for c in range(8):
        pc = {}
        j0, j1 = c, 7 - c
        s0 = x[0, SL * j0:SL * (j0 + 1), :]
        s1 = x[1, SL * j1:SL * (j1 + 1), :]
        pc["x_fm"] = C(np.concatenate([s0.T, s1.T], axis=1))
        pc["rank_in"] = np.array([[c]], dtype=np.int32)
        e = np.zeros((2, 8, 16), f32)
        m = np.zeros((2, 8, 16), f32)
        for s, (b, j) in enumerate([(0, j0), (1, j1)]):
            for owner in range(8):
                for sown in range(2):
                    ag = owner * 2 + sown
                    bb = 0 if sown == 0 else 1
                    jj = owner if sown == 0 else 7 - owner
                    if bb == b and jj < j:
                        m[s, :, ag] = 1.0
                        e[s, :, ag] = 512.0 * (j - 1 - jj)
        pc["scan_e"] = C(np.concatenate([e[0], e[1]], axis=1))
        pc["scan_m"] = C(np.concatenate([m[0], m[1]], axis=1))
        per_core.append(pc)
    return shared, per_core



STOP_AFTER = None

def build_program(dbg=()):
    nc = bass.Bass("TRN2", target_bir_lowering=False)
    g = {}

    def din(name, shape, dt=F32):
        g[name] = nc.dram_tensor(name, shape, dt, kind="ExternalInput").ap()
        return g[name]

    din("x_fm", [D, TPC]); din("rank_in", [1, 1], I32)
    din("scan_e", [8, 32]); din("scan_m", [8, 32])
    din("w_in", [D, 8]); din("w_out_aug", [9, D]); din("w_down0", [D, 32]); din("w_up_aug", [33, D])
    din("w1tp", [128, 131072]); din("w3tp", [128, 131072]); din("w2tp", [128, 131072])
    din("wqkvtp", [128, 98304]); din("wouttp", [128, 32768])
    din("wgp", [128, 131072]); din("wup", [128, 131072]); din("wdp", [128, 131072])
    din("gamma_c", [128, 16]); din("beta_c", [128, 16]); din("dskip_c", [128, 16])
    din("bout1_c", [128, 16]); din("bqkv_c", [128, 48]); din("bqkv_row", [1, 3 * D])
    din("b_in", [8, 1]); din("b_down", [32, 1]); din("decay_logit", [8, 1])
    din("h128", [128, 128], BF16); din("ident_bf", [128, 128], BF16); din("dmask", [512, SL], BF16)

    y = nc.dram_tensor("y_fm", [D, TPC], F32, kind="ExternalOutput").ap()

    sum_contrib = nc.dram_tensor("sum_contrib", [2, 8], F32).ap()
    x_spill = nc.dram_tensor("x_spill", [128, KC * TPC], F32).ap()
    ffn0_spill = nc.dram_tensor("ffn0_spill", [128, KC * TPC], F32).ap()
    sum_ag = nc.dram_tensor("sum_ag", [16, 8], F32).ap()
    k_contrib = nc.dram_tensor("k_contrib", [2 * NH * EH, SL], BF16).ap()
    v_contrib = nc.dram_tensor("v_contrib", [2 * NH * 128, SL], BF16).ap()
    k_ag = nc.dram_tensor("k_ag", [8 * 2 * NH * EH, SL], BF16, addr_space="Shared").ap()
    v_ag = nc.dram_tensor("v_ag", [8 * 2 * NH * 128, SL], BF16, addr_space="Shared").ap()

    dbg_outs = {}
    for name, shape, dt in dbg:
        dbg_outs[name] = nc.dram_tensor("dbg_" + name, shape, dt, kind="ExternalOutput").ap()

    ctx = Ctx(nc)
    with contextlib.ExitStack() as es:
        ctx._es = es
        ctx.add_queue("pe", "tensor", "pe")
        ctx.add_queue("dve", "vector", "ve")
        ctx.add_queue("act", "scalar", "ac")
        ctx.add_queue("sp", "sync")
        ctx.add_queue("gp", "gpsimd", "gc")
        grp_init = ctx.dma_group("g_init")
        grp_gpi = ctx.dma_group("g_gpi")
        grp_out = ctx.dma_group("g_out")

        def sb(name, shape, dt):
            h = es.enter_context(nc.sbuf_tensor(name, shape, dt))
            return h

        X = sb("X", [128, KC * TPC], F32)
        XN1 = sb("XN1", [128, KC * TPC], BF16)
        BA = sb("BA", [128, 16384], BF16)
        BB = sb("BB", [128, 16384], BF16)
        SC = sb("SC", [128, 4 * SL], F32)
        gamma = sb("gamma", [128, 16], F32); beta = sb("beta", [128, 16], F32)
        dskip = sb("dskip", [128, 16], F32); bout1 = sb("bout1", [128, 16], F32)
        bqkvc = sb("bqkvc", [128, 48], F32)
        w_in_b = sb("w_in_b", [128, KC * 8], BF16)
        w_out_b = sb("w_out_b", [9, D], BF16)
        w_dn_b = sb("w_dn_b", [128, KC * 32], BF16)
        w_up_b = sb("w_up_b", [33, D], BF16)
        h128_s = sb("h128_s", [128, 128], BF16)
        ident_s = sb("ident_s", [128, 128], BF16)
        dmask_s = sb("dmask_s", [128, 4 * SL], BF16)
        ones_cf = sb("ones_cf", [128, 1], F32)
        ones_rf = sb("ones_rf", [1, 128], F32)
        ones_cb = sb("ones_cb", [128, 1], BF16)
        ones_rb = sb("ones_rb", [1, 128], BF16)
        eps_t = sb("eps_t", [1, 1], F32)
        b_in_s = sb("b_in_s", [8, 1], F32)
        b_dn_s = sb("b_dn_s", [32, 1], F32)
        dlg_s = sb("dlg_s", [8, 1], F32)
        mu_r = sb("mu_r", [1, SL], F32); rstd_r = sb("rstd_r", [1, SL], F32)
        row_t = sb("row_t", [1, SL], F32)
        et_s = sb("et_s", [8, 32], F32); mt_s = sb("mt_s", [8, 32], F32)

        bX = [ctx.buf(f"X{k}") for k in range(KC)]
        bXN1 = [ctx.buf(f"XN1_{k}") for k in range(KC)]
        bBA = ctx.buf("BA"); bBB = ctx.buf("BB")
        bSC = [ctx.buf(f"SC{i}") for i in range(4)]
        bsm = ctx.buf("smalls")
        bmu = ctx.buf("mu"); brstd = ctx.buf("rstd"); brow = ctx.buf("row")
        bdram = ctx.buf("dram_misc")

        def Xs(k, s=None):
            if s is None:
                return X[:, k * TPC:(k + 1) * TPC]
            return X[:, k * TPC + s * SL: k * TPC + (s + 1) * SL]
        def XN1s(k, s=None):
            if s is None:
                return XN1[:, k * TPC:(k + 1) * TPC]
            return XN1[:, k * TPC + s * SL: k * TPC + (s + 1) * SL]
        def SCt(i):
            return SC[:, i * SL:(i + 1) * SL]

        for k in range(KC):
            ctx.dma("sp", lambda e, k=k: e.dma_start(out=Xs(k), in_=g["x_fm"][k * 128:(k + 1) * 128, :]),
                    grp_init, writes=[bX[k]])
        for nm, tile in [("gamma_c", gamma), ("beta_c", beta), ("dskip_c", dskip),
                         ("bout1_c", bout1), ("bqkv_c", bqkvc), ("b_in", b_in_s),
                         ("b_down", b_dn_s), ("decay_logit", dlg_s),
                         ("scan_e", et_s), ("scan_m", mt_s),
                         ("h128", h128_s), ("ident_bf", ident_s)]:
            ctx.dma("sp", lambda e, nm=nm, tile=tile: e.dma_start(out=tile[:, :], in_=g[nm][:, :]),
                    grp_init, writes=[bsm])
        for j in range(4):
            ctx.dma("sp", lambda e, j=j: e.dma_start(
                out=dmask_s[:, j * SL:(j + 1) * SL], in_=g["dmask"][j * 128:(j + 1) * 128, :]),
                grp_init, writes=[bsm])
        ctx.dma("gp", lambda e: e.dma_start(out=w_in_b[:, :].rearrange("p (kc r) -> p kc r", r=8),
                                            in_=g["w_in"].rearrange("(kc p) r -> p kc r", p=128)),
                grp_gpi, writes=[bsm])
        ctx.dma("gp", lambda e: e.dma_start(out=w_out_b[:, :], in_=g["w_out_aug"][:, :]), grp_gpi, writes=[bsm])
        ctx.dma("gp", lambda e: e.dma_start(out=w_dn_b[:, :].rearrange("p (kc r) -> p kc r", r=32),
                                            in_=g["w_down0"].rearrange("(kc p) r -> p kc r", p=128)),
                grp_gpi, writes=[bsm])
        ctx.dma("gp", lambda e: e.dma_start(out=w_up_b[:, :], in_=g["w_up_aug"][:, :]), grp_gpi, writes=[bsm])
        for tile, val in [(ones_cf, 1.0), (ones_rf, 1.0), (ones_cb, 1.0), (ones_rb, 1.0), (eps_t, EPS)]:
            ctx.do("gp", lambda e, tile=tile, val=val: e.memset(tile[:, :], val), writes=[bsm])

        bdbg = ctx.buf("dbg")
        def dbg_snap(name, src_s, bufs, nchunks=KC):
            if name not in dbg_outs:
                return
            for k in range(nchunks):
                ctx.dma("sp", lambda e, k=k, name=name: e.dma_start(
                    out=dbg_outs[name][k * 128:(k + 1) * 128, :], in_=src_s(k)),
                    grp_out, reads=[bufs[k]], writes=[bdbg])

        def ln_stats(src_s, src_dtype, psA, psB, psG, bsrc, bpsA, bpsB, bpsG, s):
            ones = ones_cf if src_dtype == F32 else ones_cb
            for k in range(KC):
                ctx.do("act", lambda e, k=k: e.activation(out=SCt(0)[:, :], in_=src_s(k, s), func=AF.Square),
                       reads=[bsrc[k]], writes=[bSC[0]])
                ctx.do("pe", lambda e, k=k: e.matmul(psG[0:1, 0:SL], ones[:, :], src_s(k, s),
                                                     start=(k == 0), stop=(k == KC - 1)),
                       reads=[bsrc[k], bsm], writes=[bpsG])
                ctx.do("pe", lambda e, k=k: e.matmul(psG[0:1, SL:2 * SL], ones_cf[:, :], SCt(0)[:, :],
                                                     start=(k == 0), stop=(k == KC - 1)),
                       reads=[bSC[0], bsm], writes=[bpsG])
            ctx.do("dve", lambda e: e.tensor_scalar(out=mu_r[:, :], in0=psG[0:1, 0:SL], scalar1=1.0 / D,
                                                    scalar2=None, op0=ALU.mult),
                   reads=[bpsG], writes=[bmu])
            ctx.do("dve", lambda e: e.tensor_scalar(out=row_t[:, :], in0=psG[0:1, SL:2 * SL], scalar1=1.0 / D,
                                                    scalar2=None, op0=ALU.mult),
                   reads=[bpsG], writes=[brow])
            ctx.do("dve", lambda e: e.tensor_tensor(out=rstd_r[:, :], in0=mu_r[:, :], in1=mu_r[:, :], op=ALU.mult),
                   reads=[bmu], writes=[brstd])
            ctx.do("dve", lambda e: e.tensor_tensor(out=rstd_r[:, :], in0=row_t[:, :], in1=rstd_r[:, :], op=ALU.subtract),
                   reads=[brow, brstd], writes=[brstd])
            ctx.do("act", lambda e: e.activation(out=rstd_r[:, :], in_=rstd_r[:, :], func=AF.Ln, bias=eps_t[:, :]),
                   reads=[brstd, bsm], writes=[brstd])
            ctx.do("act", lambda e: e.activation(out=rstd_r[:, :], in_=rstd_r[:, :], func=AF.Exp, scale=-0.5),
                   reads=[brstd], writes=[brstd])
            ctx.do("pe", lambda e: e.matmul(psA[:, :], ones_rf[:, :], mu_r[:, :], start=True, stop=True),
                   reads=[bmu, bsm], writes=[bpsA])
            ctx.do("pe", lambda e: e.matmul(psB[:, :], ones_rf[:, :], rstd_r[:, :], start=True, stop=True),
                   reads=[brstd, bsm], writes=[bpsB])

        _lnc = [0]
        def ln_plain():
            _lnc[0] += 1
            with nc.psum_tensor(f"lnA{_lnc[0]}", [128, SL], F32) as psA, \
                 nc.psum_tensor(f"lnB{_lnc[0]}", [128, SL], F32) as psB, \
                 nc.psum_tensor(f"lnG{_lnc[0]}", [1, 2 * SL], F32) as psG:
                bpsA = ctx.buf("lnA"); bpsB = ctx.buf("lnB"); bpsG = ctx.buf("lnG")
                for s in range(NSLAB):
                    ln_stats(Xs, F32, psA, psB, psG, bX, bpsA, bpsB, bpsG, s)
                    for k in range(KC):
                        ctx.do("dve", lambda e, k=k, s=s: e.tensor_tensor(
                            out=SCt(1)[:, :], in0=Xs(k, s), in1=psA[:, :], op=ALU.subtract),
                            reads=[bX[k], bpsA], writes=[bSC[1]])
                        ctx.do("dve", lambda e, k=k, s=s: e.tensor_tensor(
                            out=XN1s(k, s), in0=SCt(1)[:, :], in1=psB[:, :], op=ALU.mult),
                            reads=[bSC[1], bpsB], writes=[bXN1[k]])
            ctx.fence()

        ln_plain()
        dbg_snap("xn1", XN1s, bXN1)

        def ln_affine(src_s, bsrc, src_dt, dst_s, bdst, psA, psB, psG, bpsA, bpsB, bpsG):
            for s in range(NSLAB):
                ln_stats(src_s, src_dt, psA, psB, psG, bsrc, bpsA, bpsB, bpsG, s)
                for k in range(KC):
                    ctx.do("dve", lambda e, k=k, s=s: e.tensor_tensor(
                        out=SCt(1)[:, :], in0=src_s(k, s), in1=psA[:, :], op=ALU.subtract),
                        reads=[bsrc[k], bpsA], writes=[bSC[1]])
                    ctx.do("dve", lambda e: e.tensor_tensor(
                        out=SCt(1)[:, :], in0=SCt(1)[:, :], in1=psB[:, :], op=ALU.mult),
                        reads=[bSC[1], bpsB], writes=[bSC[1]])
                    ctx.do("dve", lambda e, k=k, s=s: e.tensor_scalar(
                        out=dst_s(k, s), in0=SCt(1)[:, :], scalar1=gamma[:, k:k + 1],
                        scalar2=beta[:, k:k + 1], op0=ALU.mult, op1=ALU.add),
                        reads=[bSC[1], bsm], writes=[bdst[k]])

        def phase2():
            with contextlib.ExitStack() as p2s:
                psA = p2s.enter_context(nc.psum_tensor("p2A", [128, SL], F32))
                psB = p2s.enter_context(nc.psum_tensor("p2B", [128, SL], F32))
                psG = p2s.enter_context(nc.psum_tensor("p2G", [1, 2 * SL], F32))
                psC = p2s.enter_context(nc.psum_tensor("p2C", [128, SL], F32))
                psD = p2s.enter_context(nc.psum_tensor("p2D", [128, SL], F32))
                u_s = p2s.enter_context(nc.sbuf_tensor("u_s", [8, TPC], F32))
                h_s = p2s.enter_context(nc.sbuf_tensor("h_s", [8, TPC], F32))
                dbc = p2s.enter_context(nc.sbuf_tensor("dbc", [8, SL], F32))
                ppow = p2s.enter_context(nc.sbuf_tensor("ppow", [8, SL], F32))
                dec_s = p2s.enter_context(nc.sbuf_tensor("dec_s", [8, 1], F32))
                lnd_s = p2s.enter_context(nc.sbuf_tensor("lnd_s", [8, 1], F32))
                carry = p2s.enter_context(nc.sbuf_tensor("carry", [8, 2], F32))
                pw = p2s.enter_context(nc.sbuf_tensor("pw", [8, 16], F32))
                st_t = p2s.enter_context(nc.sbuf_tensor("st_t", [8, 16], F32))
                h_aug = p2s.enter_context(nc.sbuf_tensor("h_aug", [9, TPC], BF16))
                hf_aug = p2s.enter_context(nc.sbuf_tensor("hf_aug", [33, TPC], BF16))
                bpsA = ctx.buf("p2A"); bpsB = ctx.buf("p2B"); bpsG = ctx.buf("p2G")
                bpsC = ctx.buf("p2C"); bpsD = ctx.buf("p2D")
                bu = ctx.buf("u"); bh = ctx.buf("h"); bdbc = ctx.buf("dbc")
                bppow = ctx.buf("ppow"); bdec = ctx.buf("dec"); bcar = ctx.buf("carry")
                bpw = ctx.buf("pw"); bstt = ctx.buf("st_t"); bhaug = ctx.buf("h_aug")
                bhf = ctx.buf("hf_aug")
                bBAc = [ctx.buf(f"BAc{k}") for k in range(KC)]
                bBBc = [ctx.buf(f"BBc{k}") for k in range(KC)]
                grp_ctr = ctx.dma_group("g_ctr")

                def BAs(k, s):
                    return BA[:, k * TPC + s * SL: k * TPC + (s + 1) * SL]
                def BBs(k, s):
                    return BB[:, k * TPC + s * SL: k * TPC + (s + 1) * SL]

                ln_affine(XN1s, bXN1, BF16, BAs, bBAc, psA, psB, psG, bpsA, bpsB, bpsG)

                for s in range(NSLAB):
                    for k in range(KC):
                        ctx.do("pe", lambda e, k=k, s=s: e.matmul(
                            psC[0:8, :], w_in_b[:, k * 8:(k + 1) * 8], BAs(k, s),
                            start=(k == 0), stop=(k == KC - 1)),
                            reads=[bBAc[k], bsm], writes=[bpsC])
                    ctx.do("dve", lambda e, s=s: e.tensor_scalar(
                        out=u_s[0:8, s * SL:(s + 1) * SL], in0=psC[0:8, :], scalar1=b_in_s[:, :],
                        scalar2=None, op0=ALU.add),
                        reads=[bpsC, bsm], writes=[bu])

                ctx.do("act", lambda e: e.activation(out=dec_s[:, :], in_=dlg_s[:, :], func=AF.Sigmoid),
                       reads=[bsm], writes=[bdec])
                ctx.do("act", lambda e: e.activation(out=lnd_s[:, :], in_=dec_s[:, :], func=AF.Ln),
                       reads=[bdec], writes=[bdec])
                ctx.do("gp", lambda e: e.iota(ppow[0:8, :], pattern=[[1, SL]], base=1,
                                              channel_multiplier=0, allow_small_or_imprecise_dtypes=True),
                       writes=[bppow])
                ctx.do("dve", lambda e: e.tensor_scalar(
                    out=dbc[0:8, :], in0=ppow[0:8, :], scalar1=0.0, scalar2=dec_s[:, :],
                    op0=ALU.mult, op1=ALU.add),
                    reads=[bppow, bdec], writes=[bdbc])
                ctx.do("dve", lambda e: e.tensor_scalar(
                    out=ppow[0:8, :], in0=ppow[0:8, :], scalar1=lnd_s[:, :], scalar2=None, op0=ALU.mult),
                    reads=[bppow, bdec], writes=[bppow])
                ctx.do("act", lambda e: e.activation(out=ppow[0:8, :], in_=ppow[0:8, :], func=AF.Exp),
                       reads=[bppow], writes=[bppow])
                for s in range(NSLAB):
                    ctx.do("dve", lambda e, s=s: e.tensor_tensor_scan(
                        out=h_s[0:8, s * SL:(s + 1) * SL], data0=dbc[0:8, :],
                        data1=u_s[0:8, s * SL:(s + 1) * SL], initial=0.0, op0=ALU.mult, op1=ALU.add),
                        reads=[bdbc, bu], writes=[bh])
                for s in range(NSLAB):
                    ctx.dma("sp", lambda e, s=s: e.dma_start(
                        out=sum_contrib[s:s + 1, :], in_=h_s[0:8, (s + 1) * SL - 1:(s + 1) * SL]),
                        grp_ctr, reads=[bh], writes=[bdram])
                ctx.cc(lambda e: e.collective_compute(
                    "AllGather", ALU.bypass, replica_groups=[list(range(8))],
                    ins=[sum_contrib[:, :]], outs=[sum_ag[:, :]]),
                    reads=[bdram], writes=[bdram])
                ctx.dma("sp", lambda e: e.dma_start(
                    out=st_t[:, 0:16], in_=sum_ag.rearrange("ag r -> r ag"),
                    allow_slow_non_contiguous=True),
                    grp_ctr, reads=[bdram], writes=[bstt])
                for s in range(NSLAB):
                    ctx.do("dve", lambda e, s=s: e.tensor_scalar(
                        out=pw[:, :], in0=et_s[:, s * 16:(s + 1) * 16], scalar1=lnd_s[:, :],
                        scalar2=None, op0=ALU.mult),
                        reads=[bsm, bdec], writes=[bpw])
                    ctx.do("act", lambda e: e.activation(out=pw[:, :], in_=pw[:, :], func=AF.Exp),
                           reads=[bpw], writes=[bpw])
                    ctx.do("dve", lambda e, s=s: e.tensor_tensor(
                        out=pw[:, :], in0=pw[:, :], in1=mt_s[:, s * 16:(s + 1) * 16], op=ALU.mult),
                        reads=[bpw, bsm], writes=[bpw])
                    ctx.do("dve", lambda e: e.tensor_tensor(
                        out=pw[:, :], in0=pw[:, :], in1=st_t[:, :], op=ALU.mult),
                        reads=[bpw, bstt], writes=[bpw])
                    ctx.do("dve", lambda e, s=s: e.tensor_reduce(
                        out=carry[:, s:s + 1], in_=pw[:, :], axis=AX.X, op=ALU.add),
                        reads=[bpw], writes=[bcar])
                    ctx.do("dve", lambda e, s=s: e.scalar_tensor_tensor(
                        out=h_s[0:8, s * SL:(s + 1) * SL], in0=ppow[0:8, :], scalar=carry[:, s:s + 1],
                        in1=h_s[0:8, s * SL:(s + 1) * SL], op0=ALU.mult, op1=ALU.add),
                        reads=[bppow, bcar, bh], writes=[bh])
                ctx.do("gp", lambda e: e.memset(h_aug[0:9, :], 1.0), writes=[bhaug])
                ctx.do("act", lambda e: e.activation(out=h_aug[0:8, :], in_=h_s[0:8, :], func=AF.Copy),
                       reads=[bh], writes=[bhaug])

                for s in range(NSLAB):
                    for k in range(KC):
                        ctx.do("pe", lambda e, k=k, s=s: e.matmul(
                            psD[:, :], w_out_b[:, k * 128:(k + 1) * 128],
                            h_aug[:, s * SL:(s + 1) * SL], start=True, stop=True),
                            reads=[bhaug, bsm], writes=[bpsD])
                        ctx.do("dve", lambda e, k=k, s=s: e.scalar_tensor_tensor(
                            out=SCt(2)[:, :], in0=BAs(k, s), scalar=dskip[:, k:k + 1],
                            in1=psD[:, :], op0=ALU.mult, op1=ALU.add),
                            reads=[bBAc[k], bsm, bpsD], writes=[bSC[2]])
                        ctx.do("dve", lambda e, k=k, s=s: e.tensor_tensor(
                            out=SCt(3)[:, :], in0=SCt(2)[:, :], in1=XN1s(k, s), op=ALU.add),
                            reads=[bSC[2], bXN1[k]], writes=[bSC[3]])
                        ctx.do("dve", lambda e, k=k, s=s: e.tensor_tensor(
                            out=Xs(k, s), in0=Xs(k, s), in1=SCt(3)[:, :], op=ALU.add),
                            reads=[bSC[3]], writes=[bX[k]])
                        ctx.do("act", lambda e, k=k, s=s: e.activation(
                            out=BBs(k, s), in_=SCt(3)[:, :], func=AF.Copy),
                            reads=[bSC[3]], writes=[bBBc[k]])

                ln_affine(BBs, bBBc, BF16, BAs, bBAc, psA, psB, psG, bpsA, bpsB, bpsG)

                ctx.do("gp", lambda e: e.memset(hf_aug[0:33, :], 1.0), writes=[bhf])
                for s in range(NSLAB):
                    for k in range(KC):
                        ctx.do("pe", lambda e, k=k, s=s: e.matmul(
                            psC[0:32, :], w_dn_b[:, k * 32:(k + 1) * 32], BAs(k, s),
                            start=(k == 0), stop=(k == KC - 1)),
                            reads=[bBAc[k], bsm], writes=[bpsC])
                    ctx.do("act", lambda e, s=s: e.activation(
                        out=hf_aug[0:32, s * SL:(s + 1) * SL], in_=psC[0:32, :], func=AF.Relu,
                        bias=b_dn_s[:, :]),
                        reads=[bpsC, bsm], writes=[bhf])

                for s in range(NSLAB):
                    for k in range(KC):
                        ctx.do("pe", lambda e, k=k, s=s: e.matmul(
                            psD[:, :], w_up_b[:, k * 128:(k + 1) * 128],
                            hf_aug[:, s * SL:(s + 1) * SL], start=True, stop=True),
                            reads=[bhf, bsm], writes=[bpsD])
                        ctx.do("dve", lambda e, k=k, s=s: e.tensor_tensor(
                            out=Xs(k, s), in0=Xs(k, s), in1=psD[:, :], op=ALU.add),
                            reads=[bpsD], writes=[bX[k]])
            ctx.fence()

        phase2()
        dbg_snap("X2", Xs, bX)

        WS = es.enter_context(nc.sbuf_tensor("WS", [128, 8192], BF16))
        bWS = [ctx.buf(f"WS{i}") for i in range(4)]
        gws = [ctx.dma_group(f"g_ws{i}") for i in range(4)]
        def WSs(i):
            return WS[:, i * 2048:(i + 1) * 2048]

        SQRT2 = 1.4142135623730951

        SCB = SC.bitcast(BF16)
        XB = X.bitcast(BF16)

        def phase3():
            bhid = [ctx.buf(f"hid{j}") for j in range(132)]
            def slot_ap(i):
                if i < 32:
                    return BA[:, i * SL:(i + 1) * SL]
                if i < 64:
                    return BB[:, (i - 32) * SL:(i - 32 + 1) * SL]
                if i < 128:
                    return XB[:, (i - 64) * SL:(i - 64 + 1) * SL]
                return SCB[:, 2048 + (i - 128) * SL: 2048 + (i - 128 + 1) * SL]
            grp_sp3 = ctx.dma_group("g_sp3")
            with contextlib.ExitStack() as p3s:
                ps = [p3s.enter_context(nc.psum_tensor(f"p3_{i}", [128, SL], F32)) for i in range(8)]
                bps = [ctx.buf(f"p3_{i}") for i in range(8)]
                for k in range(KC):
                    ctx.dma("sp", lambda e, k=k: e.dma_start(
                        out=x_spill[:, k * TPC:(k + 1) * TPC], in_=Xs(k)),
                        grp_sp3, reads=[bX[k]], writes=[bdram])
                spill_tok = ("D", grp_sp3.sem, ctx.semcount[grp_sp3.sem])
                for j in range(64, 128):
                    bhid[j].w = spill_tok
                for i, sp in enumerate((128, 129, 130, 131)):
                    bhid[sp].w = bhid[sp].w or bSC[2 + i // 2].w
                    bhid[sp].r = bhid[sp].r + list(bSC[2 + i // 2].r)

                chunk_slot = [list(range(64)), list(range(64, 128))]
                spares = [[128, 129], [130, 131]]
                stage_done = [[[False] * 64 for _ in range(7)] for _ in range(2)]
                pend = [[], []]
                for slx in range(2):
                    for st in range(6):
                        dd = 1 << st
                        for base in range(0, 64, dd * 2):
                            for a in range(base, base + dd):
                                pend[slx].append((st, a, a + dd))

                def try_emit_stages(slx):
                    prog = True
                    while prog:
                        prog = False
                        for idx in range(len(pend[slx])):
                            st, a, b2 = pend[slx][idx]
                            if stage_done[slx][st][a] and stage_done[slx][st][b2]:
                                sA, sB = chunk_slot[slx][a], chunk_slot[slx][b2]
                                d1, d2 = spares[slx]
                                ctx.do("dve", lambda e, sA=sA, sB=sB, d1=d1: e.tensor_tensor(
                                    out=slot_ap(d1), in0=slot_ap(sA), in1=slot_ap(sB), op=ALU.add),
                                    reads=[bhid[sA], bhid[sB]], writes=[bhid[d1]])
                                ctx.do("dve", lambda e, sA=sA, sB=sB, d2=d2: e.tensor_tensor(
                                    out=slot_ap(d2), in0=slot_ap(sA), in1=slot_ap(sB), op=ALU.subtract),
                                    reads=[bhid[sA], bhid[sB]], writes=[bhid[d2]])
                                chunk_slot[slx][a] = d1
                                chunk_slot[slx][b2] = d2
                                spares[slx] = [sA, sB]
                                stage_done[slx][st + 1][a] = True
                                stage_done[slx][st + 1][b2] = True
                                pend[slx].pop(idx)
                                prog = True
                                break

                for j in range(64):
                    wb = 2 * (j % 2)
                    for nm, wsl in (("w1tp", wb), ("w3tp", wb + 1)):
                        ctx.dma("gp", lambda e, nm=nm, wsl=wsl, j=j: e.dma_start(
                            out=WSs(wsl), in_=g[nm][:, j * 2048:(j + 1) * 2048]),
                            gws[wsl], writes=[bWS[wsl]])
                    for slx in range(2):
                        pg, pu = ps[2 * slx], ps[2 * slx + 1]
                        bpg, bpu = bps[2 * slx], bps[2 * slx + 1]
                        for k in range(KC):
                            ctx.do("pe", lambda e, k=k, slx=slx, wb=wb, pg=pg: e.matmul(
                                pg[:, :], WSs(wb)[:, k * 128:(k + 1) * 128], XN1s(k, slx),
                                start=(k == 0), stop=(k == KC - 1)),
                                reads=[bWS[wb], bXN1[k]], writes=[bpg])
                        for k in range(KC):
                            ctx.do("pe", lambda e, k=k, slx=slx, wb=wb, pu=pu: e.matmul(
                                pu[:, :], WSs(wb + 1)[:, k * 128:(k + 1) * 128], XN1s(k, slx),
                                start=(k == 0), stop=(k == KC - 1)),
                                reads=[bWS[wb + 1], bXN1[k]], writes=[bpu])
                        sct = slx
                        ctx.do("act", lambda e, pg=pg, sct=sct: e.activation(
                            out=SCt(sct)[:, :], in_=pg[:, :], func=AF.Silu),
                            reads=[bpg], writes=[bSC[sct]])
                        dst = j if slx == 0 else 64 + j
                        ctx.do("dve", lambda e, dst=dst, pu=pu, sct=sct: e.scalar_tensor_tensor(
                            out=slot_ap(dst), in0=SCt(sct)[:, :], scalar=SQRT2, in1=pu[:, :],
                            op0=ALU.mult, op1=ALU.mult),
                            reads=[bSC[sct], bpu], writes=[bhid[dst]])
                        pb = (2 * j + slx) % 2
                        ph = ps[4 + pb]
                        ctx.do("pe", lambda e, dst=dst, ph=ph: e.matmul(
                            ph[:, :], h128_s[:, :], slot_ap(dst), start=True, stop=True),
                            reads=[bhid[dst], bsm], writes=[bps[4 + pb]])
                        ctx.do("act", lambda e, dst=dst, ph=ph: e.activation(
                            out=slot_ap(dst), in_=ph[:, :], func=AF.Copy),
                            reads=[bps[4 + pb]], writes=[bhid[dst]])
                        stage_done[slx][0][j] = True
                        try_emit_stages(slx)
                for slx in range(2):
                    try_emit_stages(slx)
                    assert not pend[slx]

                for q in range(4):
                    for k in range(64):
                        wsl = 2 + ((k // 4) % 2)
                        if k % 4 == 0:
                            ctx.dma("gp", lambda e, k=k, q=q, wsl=wsl: e.dma_start(
                                out=WSs(wsl),
                                in_=g["w2tp"][:, q * 32768 + k * 512: q * 32768 + (k + 4) * 512]),
                                gws[wsl], writes=[bWS[wsl]])
                        for slx in range(2):
                            for dcl in range(4):
                                pw = ps[slx * 4 + dcl]
                                si = chunk_slot[slx][k]
                                ctx.do("pe", lambda e, k=k, dcl=dcl, wsl=wsl, pw=pw, si=si: e.matmul(
                                    pw[:, :], WSs(wsl)[:, (k % 4) * 512 + dcl * 128: (k % 4) * 512 + (dcl + 1) * 128],
                                    slot_ap(si), start=(k == 0), stop=(k == 63)),
                                    reads=[bWS[wsl], bhid[si]], writes=[bps[slx * 4 + dcl]])
                    for slx in range(2):
                        for dcl in range(4):
                            pw = ps[slx * 4 + dcl]
                            kd = q * 4 + dcl
                            sct = (slx * 4 + dcl) % 2
                            ctx.do("act", lambda e, pw=pw, sct=sct: e.activation(
                                out=SCt(sct)[:, :], in_=pw[:, :], func=AF.Copy),
                                reads=[bps[slx * 4 + dcl]], writes=[bSC[sct]])
                            ctx.dma("sp", lambda e, kd=kd, slx=slx, sct=sct: e.dma_start(
                                out=ffn0_spill[:, kd * TPC + slx * SL: kd * TPC + (slx + 1) * SL],
                                in_=SCt(sct)[:, :]),
                                grp_sp3, reads=[bSC[sct]], writes=[bdram])
            ctx.fence()
            grp_r = [ctx.dma_group("g_xrest0"), ctx.dma_group("g_xrest1")]
            grp_rg = [ctx.dma_group("g_xrestg0"), ctx.dma_group("g_xrestg1")]
            bxa = [ctx.buf("xr0"), ctx.buf("xr1")]
            prev_add = [None, None]
            for k in range(KC):
                pb = k % 2
                ctx.dma("sp", lambda e, k=k, pb=pb: e.dma_start(
                    out=SC[:, pb * TPC:(pb + 1) * TPC],
                    in_=x_spill[:, k * TPC:(k + 1) * TPC]),
                    grp_r[pb], reads=[bdram], writes=[bxa[pb]])
                extra = (prev_add[pb],) if prev_add[pb] else ()
                ctx.dma("gp", lambda e, k=k: e.dma_start(
                    out=Xs(k), in_=ffn0_spill[:, k * TPC:(k + 1) * TPC]),
                    grp_rg[pb], reads=[bdram], writes=[bX[k]], extra=extra)
                prev_add[pb] = ctx.do("dve", lambda e, k=k, pb=pb: e.tensor_tensor(
                    out=Xs(k), in0=Xs(k), in1=SC[:, pb * TPC:(pb + 1) * TPC], op=ALU.add),
                    reads=[bxa[pb]], writes=[bX[k]])
            ctx.fence()

        phase3()

        dbg_snap("X3", Xs, bX)
        if STOP_AFTER == "p3":
            for k in range(KC):
                ctx.dma("sp", lambda e, k=k: e.dma_start(out=y[k * 128:(k + 1) * 128, :], in_=Xs(k)),
                        grp_out, reads=[bX[k]], writes=[bdbg])
            _fo = ctx.semcount["g_out"]
            ctx.raw("sp", lambda e, v=_fo: e.wait_ge(ctx.semh["g_out"], v))
            block = es.enter_context(nc.Block())
            ctx.finalize(block)
            return nc

        ln_plain()
        dbg_snap("xn1b", XN1s, bXN1)

        ATT_SCALE = 0.08838834764831845

        def phase5(part):
            with contextlib.ExitStack() as p5s:
                psA = p5s.enter_context(nc.psum_tensor(f"p5A{part}", [128, SL], F32))
                psB = p5s.enter_context(nc.psum_tensor(f"p5B{part}", [128, SL], F32))
                psC = p5s.enter_context(nc.psum_tensor(f"p5C{part}", [128, SL], F32))
                psD = p5s.enter_context(nc.psum_tensor(f"p5D{part}", [128, SL], F32))
                psE = p5s.enter_context(nc.psum_tensor(f"p5E{part}", [128, SL], F32))
                psL = p5s.enter_context(nc.psum_tensor(f"p5L{part}", [1, SL], F32))
                psB2 = p5s.enter_context(nc.psum_tensor(f"p5L2{part}", [1, SL], F32))
                expb = p5s.enter_context(nc.sbuf_tensor(f"expb{part}", [128, 1], F32))
                bqv = p5s.enter_context(nc.sbuf_tensor(f"bqv{part}", [1, 128], BF16))
                vt0 = p5s.enter_context(nc.sbuf_tensor(f"vt0{part}", [128, 256], BF16))
                bpsA = ctx.buf("p5A"); bpsB = ctx.buf("p5B"); bpsC = ctx.buf("p5C")
                bpsD = ctx.buf("p5D"); bpsE = ctx.buf("p5E"); bpsL = ctx.buf("p5L")
                bpsB2 = ctx.buf("p5L2")
                bexpb = ctx.buf("expb"); bbqv = ctx.buf("bqv")
                bvt0 = [ctx.buf("bvt0a"), ctx.buf("bvt0b")]
                battn = [ctx.buf(f"attn{h}") for h in range(NH)]
                bQ = [ctx.buf(f"Q{h}") for h in range(NH)]
                grp_c = ctx.dma_group(f"g_kvc{part}")
                grp_cv = ctx.dma_group(f"g_kvcv{part}")
                ctx.do("gp", lambda e: e.memset(expb[:, :], -8.0), writes=[bexpb])
                vb_all = [mu_r.bitcast(BF16), rstd_r.bitcast(BF16)]
                if part == 0:
                    for half in range(2):
                        ctx.dma("gp", lambda e, half=half: e.dma_start(
                            out=vb_all[half][:, :],
                            in_=g["bqkv_row"][:, 2 * D + half * 1024: 2 * D + (half + 1) * 1024]),
                            ctx.dma_group("g_vbias"), writes=[bmu if half == 0 else brstd])

                def qslot(h, s):
                    return BB[:, h * TPC + s * SL: h * TPC + (s + 1) * SL]
                def aslot(h, s):
                    return BA[:, h * TPC + s * SL: h * TPC + (s + 1) * SL]

                for h in (range(NH) if part == 0 else ()):
                    wk, wv = h % 2, 2 + h % 2
                    for wsl, sec in ((wk, 16 + h), (wv, 32 + h)):
                        ctx.dma("gp", lambda e, wsl=wsl, sec=sec: e.dma_start(
                            out=WSs(wsl), in_=g["wqkvtp"][:, sec * 2048:(sec + 1) * 2048]),
                            gws[wsl], writes=[bWS[wsl]])
                    bqv_ap = vb_all[h // 8][:, (h % 8) * 128:(h % 8 + 1) * 128]
                    bbqv_ = bmu if h < 8 else brstd
                    for s in range(NSLAB):
                        pk, bpk = (psA, bpsA) if s == 0 else (psC, bpsC)
                        sck = 0 if s == 0 else 1
                        for k in range(KC):
                            ctx.do("pe", lambda e, k=k, s=s, wk=wk, pk=pk: e.matmul(
                                pk[:, :], WSs(wk)[:, k * 128:(k + 1) * 128], XN1s(k, s),
                                start=(k == 0), stop=(k == KC - 1)),
                                reads=[bWS[wk], bXN1[k]], writes=[bpk])
                        ctx.do("dve", lambda e, h=h, pk=pk, sck=sck: e.tensor_scalar(
                            out=BB[:, sck * SL:(sck + 1) * SL], in0=pk[:, :],
                            scalar1=bqkvc[:, 16 + h:17 + h], scalar2=None, op0=ALU.add),
                            reads=[bpk, bsm], writes=[bSC[sck]])
                        ctx.dma("sp", lambda e, s=s, h=h, sck=sck: e.dma_start(
                            out=k_contrib[s * 2048 + h * 128: s * 2048 + (h + 1) * 128, :],
                            in_=BB[:, sck * SL:(sck + 1) * SL]),
                            grp_cv, reads=[bSC[sck]], writes=[bdram])
                        for ts_ in range(4):
                            vb = ts_ % 2
                            pv, bpv = (psB, bpsB) if vb == 0 else (psD, bpsD)
                            for k in range(KC):
                                ctx.do("pe", lambda e, k=k, s=s, ts_=ts_, wv=wv, pv=pv: e.matmul(
                                    pv[:, :128], XN1s(k, s)[:, ts_ * 128:(ts_ + 1) * 128],
                                    WSs(wv)[:, k * 128:(k + 1) * 128],
                                    start=(k == 0), stop=False),
                                    reads=[bWS[wv], bXN1[k]], writes=[bpv])
                            ctx.do("pe", lambda e, pv=pv, bqv_ap=bqv_ap: e.matmul(
                                pv[:, :128], ones_rb[:, :], bqv_ap, start=False, stop=True),
                                reads=[bbqv_, bsm], writes=[bpv])
                            ctx.do("act", lambda e, vb=vb, pv=pv: e.activation(
                                out=vt0[:, vb * 128:(vb + 1) * 128], in_=pv[:, :128], func=AF.Copy),
                                reads=[bpv], writes=[bvt0[vb]])
                            ctx.dma("sp", lambda e, s=s, h=h, ts_=ts_, vb=vb: e.dma_start(
                                out=v_contrib[s * 2048 + h * 128: s * 2048 + (h + 1) * 128,
                                              ts_ * 128:(ts_ + 1) * 128],
                                in_=vt0[:, vb * 128:(vb + 1) * 128]),
                                grp_cv, reads=[bvt0[vb]], writes=[bdram])

                if part == 0:
                    bkc = ctx.buf("kcontrib"); bvc = ctx.buf("vcontrib")
                    bkag = ctx.buf("kag"); bvag = ctx.buf("vag")
                    bkc.w = bdram.w; bkc.r = list(bdram.r)
                    bvc.w = bdram.w; bvc.r = list(bdram.r)
                    ctx.cc(lambda e: e.collective_compute(
                        "AllGather", ALU.bypass, replica_groups=[list(range(8))],
                        ins=[k_contrib[:, :]], outs=[k_ag[:, :]]),
                        reads=[bkc], writes=[bkag])
                    ctx.cc(lambda e: e.collective_compute(
                        "AllGather", ALU.bypass, replica_groups=[list(range(8))],
                        ins=[v_contrib[:, :]], outs=[v_ag[:, :]]),
                        reads=[bvc], writes=[bvag])
                    return

                for h in range(NH):
                    wq = h % 2
                    ctx.dma("gp", lambda e, h=h, wq=wq: e.dma_start(
                        out=WSs(wq), in_=g["wqkvtp"][:, h * 2048:(h + 1) * 2048]),
                        gws[wq], writes=[bWS[wq]])
                    for s in range(NSLAB):
                        pq, bpq = (psA, bpsA) if (2 * h + s) % 2 == 0 else (psB, bpsB)
                        for k in range(KC):
                            ctx.do("pe", lambda e, k=k, s=s, wq=wq, pq=pq: e.matmul(
                                pq[:, :], WSs(wq)[:, k * 128:(k + 1) * 128], XN1s(k, s),
                                start=(k == 0), stop=(k == KC - 1)),
                                reads=[bWS[wq], bXN1[k]], writes=[bpq])
                        ctx.do("dve", lambda e, h=h, s=s, pq=pq: e.tensor_scalar(
                            out=qslot(h, s), in0=pq[:, :], scalar1=bqkvc[:, h:h + 1],
                            scalar2=None, op0=ALU.add),
                            reads=[bpq, bsm], writes=[bQ[h]])

                def ktile(b):
                    return WS[:, b * 512:(b + 1) * 512]
                def vtile(b):
                    return WS[:, 1024 + b * 512: 1024 + (b + 1) * 512]
                def ptile(b):
                    return WS[:, 2048 + b * 2048: 2048 + (b + 1) * 2048]
                bkt = [ctx.buf("kt0"), ctx.buf("kt1")]
                bvt = [ctx.buf("vt0b"), ctx.buf("vt1b")]
                bpt = [ctx.buf("pt0"), ctx.buf("pt1")]
                gkt = [ctx.dma_group(f"g_kt0{part}"), ctx.dma_group(f"g_kt1{part}")]
                gvt = [ctx.dma_group(f"g_vt0{part}"), ctx.dma_group(f"g_vt1{part}")]
                blrow = ctx.buf("lrow")

                ctx.fence()
                ctx.begin_switch(["pe", "dve", "act", "sp"])
                for c8 in range(8):
                    ctx.begin_body()
                    ns = [c8 + 1, 8 - c8]
                    for s in range(NSLAB):
                        for h in range(NH):
                            par = (s * NH + h) % 2
                            psE_, bpsE_ = (psE, bpsE) if par == 0 else (psA, bpsA)
                            psL_, bpsL_ = (psL, bpsL) if par == 0 else (psB2, bpsB2)
                            for kt in range(ns[s]):
                                kb = kt % 2
                                gch = kt if s == 0 else 7 - kt
                                owner = gch
                                sel = 0 if s == 0 else 1
                                krow = owner * 4096 + sel * 2048 + h * 128
                                ctx.dma("sp", lambda e, krow=krow, kb=kb: e.dma_start(
                                    out=ktile(kb), in_=k_ag[krow:krow + 128, :]),
                                    gkt[kb], writes=[bkt[kb]])
                                ctx.dma("act", lambda e, krow=krow, kb=kb: e.dma_start(
                                    out=vtile(kb), in_=v_ag[krow:krow + 128, :]),
                                    gvt[kb], writes=[bvt[kb]])
                                first = (kt == 0)
                                last = (kt == ns[s] - 1)
                                for ksub in range(4):
                                    pss, bpss = ((psC, bpsC) if ksub % 2 == 0 else (psD, bpsD))
                                    ctx.do("pe", lambda e, kb=kb, ksub=ksub, h=h, s=s, pss=pss: e.matmul(
                                        pss[:, :], ktile(kb)[:, ksub * 128:(ksub + 1) * 128],
                                        qslot(h, s), start=True, stop=True),
                                        reads=[bkt[kb], bQ[h]], writes=[bpss])
                                    ctx.do("act", lambda e, kb=kb, ksub=ksub, pss=pss: e.activation(
                                        out=ptile(kb)[:, ksub * 512:(ksub + 1) * 512], in_=pss[:, :],
                                        func=AF.Exp, scale=ATT_SCALE, bias=expb[:, :]),
                                        reads=[bpss, bexpb], writes=[bpt[kb]])
                                    if last:
                                        ctx.do("dve", lambda e, kb=kb, ksub=ksub: e.tensor_tensor(
                                            out=ptile(kb)[:, ksub * 512:(ksub + 1) * 512],
                                            in0=ptile(kb)[:, ksub * 512:(ksub + 1) * 512],
                                            in1=dmask_s[:, ksub * SL:(ksub + 1) * SL], op=ALU.mult),
                                            reads=[bsm], writes=[bpt[kb]])
                                for ksub in range(4):
                                    ctx.do("pe", lambda e, kb=kb, ksub=ksub, st_=(first and ksub == 0),
                                           sp_=(last and ksub == 3), psE_=psE_: e.matmul(
                                        psE_[:, :], vtile(kb)[:, ksub * 128:(ksub + 1) * 128],
                                        ptile(kb)[:, ksub * 512:(ksub + 1) * 512],
                                        start=st_, stop=sp_),
                                        reads=[bvt[kb], bpt[kb]], writes=[bpsE_])
                                    ctx.do("pe", lambda e, kb=kb, ksub=ksub, st_=(first and ksub == 0),
                                           sp_=(last and ksub == 3), psL_=psL_: e.matmul(
                                        psL_[:, :], ones_cb[:, :],
                                        ptile(kb)[:, ksub * 512:(ksub + 1) * 512],
                                        start=st_, stop=sp_),
                                        reads=[bsm, bpt[kb]], writes=[bpsL_])
                            lr_ = row_t
                            ctx.do("dve", lambda e, psL_=psL_, lr_=lr_: e.reciprocal(
                                lr_[:, :], psL_[:, :]),
                                reads=[bpsL_], writes=[blrow])
                            pbc, bpbc = (psC, bpsC) if par == 0 else (psD, bpsD)
                            ctx.do("pe", lambda e, pbc=pbc, lr_=lr_: e.matmul(
                                pbc[:, :], ones_rf[:, :], lr_[:, :], start=True, stop=True),
                                reads=[blrow, bsm], writes=[bpbc])
                            ctx.do("act", lambda e, psE_=psE_, par=par: e.activation(
                                out=SCt(par)[:, :], in_=psE_[:, :], func=AF.Copy),
                                reads=[bpsE_], writes=[bSC[par]])
                            ctx.do("dve", lambda e, h=h, s=s, par=par, pbc=pbc: e.tensor_tensor(
                                out=aslot(h, s), in0=SCt(par)[:, :], in1=pbc[:, :], op=ALU.mult),
                                reads=[bSC[par], bpbc], writes=[battn[h]])
                    ctx.end_body()
                ctx.end_switch(g["rank_in"][0:1, 0:1])

                for q4 in range(8):
                    wsl2 = q4 % 2
                    ctx.dma("gp", lambda e, q4=q4, wsl2=wsl2: e.dma_start(
                        out=WS[:, wsl2 * 4096:(wsl2 + 1) * 4096],
                        in_=g["wouttp"][:, q4 * 4096:(q4 + 1) * 4096],
                        max_dma_last_dim=8192),
                        gws[wsl2], writes=[bWS[2 * wsl2], bWS[2 * wsl2 + 1]])
                    for k in range(NH):
                        wsl = 2 * wsl2
                        for dcl in range(2):
                            for s in range(NSLAB):
                                ps, bps = [(psA, bpsA), (psB, bpsB),
                                           (psC, bpsC), (psD, bpsD)][dcl * 2 + s]
                                ctx.do("pe", lambda e, k=k, dcl=dcl, s=s, wsl2=wsl2, ps=ps: e.matmul(
                                    ps[:, :], WS[:, wsl2 * 4096 + k * 256 + dcl * 128:
                                                 wsl2 * 4096 + k * 256 + (dcl + 1) * 128],
                                    aslot(k, s), start=(k == 0), stop=(k == NH - 1)),
                                    reads=[bWS[2 * wsl2], battn[k]], writes=[bps])
                    for dcl in range(2):
                        for s in range(NSLAB):
                            ps, bps = [(psA, bpsA), (psB, bpsB),
                                       (psC, bpsC), (psD, bpsD)][dcl * 2 + s]
                            kd = q4 * 2 + dcl
                            ctx.do("dve", lambda e, kd=kd, s=s, ps=ps: e.scalar_tensor_tensor(
                                out=Xs(kd, s), in0=ps[:, :], scalar=bout1[:, kd:kd + 1],
                                in1=Xs(kd, s), op0=ALU.add, op1=ALU.add),
                                reads=[bps, bsm], writes=[bX[kd]])
            ctx.fence()

        phase5(0)
        ctx.fence(exclude=("cc",))

        def phase6():
            with contextlib.ExitStack() as p6s:
                psA = p6s.enter_context(nc.psum_tensor("p6A", [128, SL], F32))
                psB = p6s.enter_context(nc.psum_tensor("p6B", [128, SL], F32))
                acc = [p6s.enter_context(nc.psum_tensor(f"p6c{i}", [128, SL], F32)) for i in range(4)]
                psT = p6s.enter_context(nc.psum_tensor("p6T", [128, 1024], BF16))
                psT2 = p6s.enter_context(nc.psum_tensor("p6T2", [128, 1024], BF16))
                bpsA = ctx.buf("p6A"); bpsB = ctx.buf("p6B")
                bacc = [ctx.buf(f"p6c{i}") for i in range(4)]
                bpsT = ctx.buf("p6T"); bpsT2 = ctx.buf("p6T2")
                bwset = [ctx.buf("p6ws0"), ctx.buf("p6ws1")]
                bhtm = [ctx.buf(f"p6htm{t}") for t in range(8)]
                bhfm = [ctx.buf(f"p6hfm{i}") for i in range(16)]
                bwd = [ctx.buf("p6wd0"), ctx.buf("p6wd1")]
                bSC6 = [[ctx.buf(f"p6sc{p}_{i}") for i in range(4)] for p in range(2)]
                gw6 = [ctx.dma_group("g_w6a"), ctx.dma_group("g_w6b")]
                gw6d = [ctx.dma_group("g_w6d0"), ctx.dma_group("g_w6d1")]
                IW = 256

                def wgsl(st_, k):
                    return BB[:, st_ * 8192 + k * IW: st_ * 8192 + (k + 1) * IW]
                def wusl(st_, k):
                    return BB[:, st_ * 8192 + 4096 + k * IW: st_ * 8192 + 4096 + (k + 1) * IW]
                def htm(tsub):
                    return WS[:, tsub * IW:(tsub + 1) * IW]
                def hfm(ic):
                    return BA[:, ic * TPC:(ic + 1) * TPC]
                def wd_t(hb):
                    return WS[:, 4096 + hb * 2048: 4096 + (hb + 1) * 2048]

                for ig in range(4):
                    for isl8 in range(8):
                        ii = ig * 8 + isl8
                        st_ = ii % 2
                        ctx.dma("gp", lambda e, ii=ii, st_=st_: e.dma_start(
                            out=BB[:, st_ * 8192: st_ * 8192 + 4096],
                            in_=g["wgp"][:, ii * 4096:(ii + 1) * 4096],
                            max_dma_last_dim=8192),
                            gw6[st_], writes=[bwset[st_]])
                        ctx.dma("gp", lambda e, ii=ii, st_=st_: e.dma_start(
                            out=BB[:, st_ * 8192 + 4096: st_ * 8192 + 8192],
                            in_=g["wup"][:, ii * 4096:(ii + 1) * 4096],
                            max_dma_last_dim=8192),
                            gw6[st_], writes=[bwset[st_]])
                        for tsub in range(8):
                            s, tl = tsub // 4, tsub % 4
                            pp = tsub % 2
                            pg, bpg = (psA, bpsA) if pp == 0 else (acc[0], bacc[0])
                            pu, bpu = (psB, bpsB) if pp == 0 else (acc[1], bacc[1])
                            PO = pp * IW
                            bSCp = [bSC6[pp][i] for i in range(4)]
                            for k in range(KC):
                                ctx.do("pe", lambda e, PO=PO, k=k, s=s, tl=tl, st_=st_, pg=pg: e.matmul(
                                    pg[:, :IW], XN1s(k, s)[:, tl * 128:(tl + 1) * 128], wgsl(st_, k),
                                    start=(k == 0), stop=(k == KC - 1)),
                                    reads=[bXN1[k], bwset[st_]], writes=[bpg])
                            for k in range(KC):
                                ctx.do("pe", lambda e, PO=PO, k=k, s=s, tl=tl, st_=st_, pu=pu: e.matmul(
                                    pu[:, :IW], XN1s(k, s)[:, tl * 128:(tl + 1) * 128], wusl(st_, k),
                                    start=(k == 0), stop=(k == KC - 1)),
                                    reads=[bXN1[k], bwset[st_]], writes=[bpu])
                            ctx.do("act", lambda e, PO=PO, pg=pg: e.activation(
                                out=SCt(0)[:, PO:PO + IW], in_=pg[:, :IW], func=AF.Silu),
                                reads=[bpg], writes=[bSCp[0]])
                            ctx.do("dve", lambda e, PO=PO, pu=pu: e.tensor_tensor(
                                out=SCt(1)[:, PO:PO + IW], in0=SCt(0)[:, PO:PO + IW], in1=pu[:, :IW], op=ALU.mult),
                                reads=[bSCp[0], bpu], writes=[bSCp[1]])
                            ctx.do("act", lambda e, PO=PO: e.activation(
                                out=SCt(2)[:, PO:PO + IW], in_=SCt(1)[:, PO:PO + IW], func=AF.Abs),
                                reads=[bSCp[1]], writes=[bSCp[2]])
                            NG = IW // 4
                            def gview(tile, j, PO=0):
                                return tile[:, PO:PO + IW].rearrange("p (gg f) -> p gg f", f=4)[:, :, j]
                            for (dst, i0, i1, op) in [(0, 0, 1, ALU.max), (1, 0, 1, ALU.min),
                                                      (2, 2, 3, ALU.max), (3, 2, 3, ALU.min)]:
                                ctx.do("dve", lambda e, PO=PO, dst=dst, i0=i0, i1=i1, op=op: e.tensor_tensor(
                                    out=SCt(3)[:, PO + dst * NG:PO + (dst + 1) * NG],
                                    in0=gview(SCt(2), i0, PO), in1=gview(SCt(2), i1, PO), op=op),
                                    reads=[bSCp[2]], writes=[bSCp[3]])
                            ctx.do("dve", lambda e, PO=PO: e.tensor_tensor(
                                out=SCt(0)[:, PO + 0:PO + NG], in0=SCt(3)[:, PO + 0:PO + NG], in1=SCt(3)[:, PO + 2 * NG:PO + 3 * NG], op=ALU.min),
                                reads=[bSCp[3]], writes=[bSCp[0]])
                            ctx.do("dve", lambda e, PO=PO: e.tensor_tensor(
                                out=SCt(0)[:, PO + NG:PO + 2 * NG], in0=SCt(3)[:, PO + NG:PO + 2 * NG], in1=SCt(3)[:, PO + 3 * NG:PO + 4 * NG], op=ALU.max),
                                reads=[bSCp[3]], writes=[bSCp[0]])
                            ctx.do("dve", lambda e, PO=PO: e.tensor_tensor(
                                out=SCt(0)[:, PO + 2 * NG:PO + 3 * NG], in0=SCt(0)[:, PO + 0:PO + NG], in1=SCt(0)[:, PO + NG:PO + 2 * NG], op=ALU.max),
                                reads=[bSCp[0]], writes=[bSCp[0]])
                            for j4 in range(4):
                                ctx.do("dve", lambda e, PO=PO, j4=j4: e.tensor_tensor(
                                    out=SCt(3)[:, PO + j4 * NG:PO + (j4 + 1) * NG],
                                    in0=gview(SCt(2), j4, PO), in1=SCt(0)[:, PO + 2 * NG:PO + 3 * NG], op=ALU.is_ge),
                                    reads=[bSCp[2], bSCp[0]], writes=[bSCp[3]])
                                ctx.do("dve", lambda e, PO=PO, j4=j4, tsub=tsub: e.tensor_tensor(
                                    out=gview(htm(tsub), j4), in0=gview(SCt(1), j4, PO),
                                    in1=SCt(3)[:, PO + j4 * NG:PO + (j4 + 1) * NG], op=ALU.mult),
                                    reads=[bSCp[1], bSCp[3]], writes=[bhtm[tsub]])
                        for tsub in range(8):
                            s, tl = tsub // 4, tsub % 4
                            for ic2 in range(2):
                                tp = (2 * tsub + ic2) % 2
                                pT, bpT = (psT, bpsT) if tp == 0 else (psT2, bpsT2)
                                ctx.do("pe", lambda e, tsub=tsub, ic2=ic2, pT=pT: e.transpose(
                                    pT[:, 0:128], htm(tsub)[:, ic2 * 128:(ic2 + 1) * 128], ident_s[:, :]),
                                    reads=[bhtm[tsub], bsm], writes=[bpT])
                                ic = isl8 * 2 + ic2
                                ctx.do("act", lambda e, ic=ic, s=s, tl=tl, pT=pT: e.activation(
                                    out=hfm(ic)[:, s * SL + tl * 128: s * SL + (tl + 1) * 128],
                                    in_=pT[:, 0:128], func=AF.Copy),
                                    reads=[bpT], writes=[bhfm[ic]])
                    for q4 in range(8):
                        for hb in range(2):
                            ctx.dma("gp", lambda e, ig=ig, q4=q4, hb=hb: e.dma_start(
                                out=wd_t(hb),
                                in_=g["wdp"][:, (ig * 8 + q4) * 4096 + hb * 2048:
                                             (ig * 8 + q4) * 4096 + (hb + 1) * 2048]),
                                gw6d[hb], writes=[bwd[hb]])
                        for ic in range(KC):
                            hb = ic // 8
                            for dcl in range(2):
                                for s in range(NSLAB):
                                    ps, bps = (acc[dcl * 2 + s], bacc[dcl * 2 + s])
                                    ctx.do("pe", lambda e, ic=ic, dcl=dcl, s=s, hb=hb, ps=ps: e.matmul(
                                        ps[:, :], wd_t(hb)[:, (ic % 8) * 256 + dcl * 128:
                                                           (ic % 8) * 256 + (dcl + 1) * 128],
                                        hfm(ic)[:, s * SL:(s + 1) * SL],
                                        start=(ic == 0), stop=(ic == KC - 1)),
                                        reads=[bwd[hb], bhfm[ic]], writes=[bps])
                        for dcl in range(2):
                            for s in range(NSLAB):
                                ps, bps = (acc[dcl * 2 + s], bacc[dcl * 2 + s])
                                kd = q4 * 2 + dcl
                                ctx.do("dve", lambda e, kd=kd, s=s, ps=ps: e.tensor_tensor(
                                    out=Xs(kd, s), in0=Xs(kd, s), in1=ps[:, :], op=ALU.add),
                                    reads=[bps], writes=[bX[kd]])
            ctx.fence()

        phase6()
        phase5(1)
        dbg_snap("X5", Xs, bX)
        if "attnfm" in dbg_outs:
            battn2 = ctx.buf("attn2")
            for k in range(KC):
                ctx.dma("sp", lambda e, k=k: e.dma_start(
                    out=dbg_outs["attnfm"][k * 128:(k + 1) * 128, :], in_=BA[:, k * TPC:(k + 1) * TPC]),
                    grp_out, reads=[battn2], writes=[bdbg])

        for k in range(KC):
            ctx.dma("sp", lambda e, k=k: e.dma_start(out=y[k * 128:(k + 1) * 128, :], in_=Xs(k)),
                    grp_out, reads=[bX[k]], writes=[bdbg])
        final_out = ctx.semcount["g_out"]
        ctx.raw("sp", lambda e, v=final_out: e.wait_ge(ctx.semh["g_out"], v))

        block = es.enter_context(nc.Block())
        ctx.finalize(block)
    return nc



def kernel(**inputs):
    shared, per_core = prep_inputs(inputs)
    nc = build_program()
    in_maps = []
    for c in range(8):
        m = dict(shared)
        m.update(per_core[c])
        in_maps.append(m)
    res = run_bass_kernel_spmd(nc, in_maps, core_ids=list(range(8)))
    out = np.zeros((2, 4096, D), dtype=np.float32)
    for c in range(8):
        yc = np.asarray(res.results[c]["y_fm"])
        out[0, SL * c:SL * (c + 1), :] = yc[:, 0:SL].T
        out[1, SL * (7 - c):SL * (8 - c), :] = yc[:, SL:2 * SL].T
    return out
